# version 15
# baseline (speedup 1.0000x reference)
import contextlib

import numpy as np

import concourse.bass as bass
import concourse.mybir as mybir
from concourse.bass_utils import run_bass_kernel_spmd

F32 = mybir.dt.float32
BF16 = mybir.dt.bfloat16
FP8 = mybir.dt.float8e4
I32 = mybir.dt.int32
I8 = mybir.dt.int8
I16 = mybir.dt.int16
Alu = mybir.AluOpType
Act = mybir.ActivationFunctionType
AX = mybir.AxisListType.X

S = 2048
D = 2048
H = 16
HD = 128
NCH = 16
F = 8192
M = 4
DC = D // 128
FT = F // 128
ST = S // 128
SCALE = float(HD) ** -0.5
EPS = 1e-6
LNEPS = 1e-5
INV_D = 1.0 / D


def build_nc():
    nc = bass.Bass("TRN2", target_bir_lowering=False)

    x0 = nc.dram_tensor("x0", [S, D], F32, kind="ExternalInput")
    wq = nc.dram_tensor("wq", [128, H, DC, 128], F32, kind="ExternalInput")
    wk = nc.dram_tensor("wk", [128, H, DC, 128], F32, kind="ExternalInput")
    wv = nc.dram_tensor("wv", [128, H, DC, 128], F32, kind="ExternalInput")
    wo = nc.dram_tensor("wo", [128, DC, D], F32, kind="ExternalInput")
    W1h = nc.dram_tensor("W1h", [128, FT, DC, 128], F32, kind="ExternalInput")
    W2h = nc.dram_tensor("W2h", [128, 16, 64, 128], F32, kind="ExternalInput")
    memh = nc.dram_tensor("memh", [H, 128, M * 128], F32, kind="ExternalInput")
    mnh = nc.dram_tensor("mnh", [128, H * M], F32, kind="ExternalInput")
    gater = nc.dram_tensor("gater", [1, H], F32, kind="ExternalInput")
    b1h = nc.dram_tensor("b1h", [128, FT], F32, kind="ExternalInput")
    b2h = nc.dram_tensor("b2h", [128, 16], F32, kind="ExternalInput")
    s1t = nc.dram_tensor("s1t", [1, 1], F32, kind="ExternalInput")
    s2t = nc.dram_tensor("s2t", [1, 1], F32, kind="ExternalInput")
    maskT = nc.dram_tensor("maskT", [128, 128], F32, kind="ExternalInput")
    id16 = nc.dram_tensor("id16", [128, 128], BF16, kind="ExternalInput")
    onescol = nc.dram_tensor("onescol", [128, 1], F32, kind="ExternalInput")
    ones16 = nc.dram_tensor("ones16", [128, 1], BF16, kind="ExternalInput")
    onesrow = nc.dram_tensor("onesrow", [1, 128], F32, kind="ExternalInput")

    y = nc.dram_tensor("y", [S, D], F32, kind="ExternalOutput")
    x1scr = nc.dram_tensor("x1scr", [S, D], F32, kind="Internal")
    w1qscr = nc.dram_tensor("w1qscr", [FT, 128, DC * 128], FP8, kind="Internal")
    w2qscr = nc.dram_tensor("w2qscr", [16, 128, 64 * 128], FP8, kind="Internal")

    es_g = contextlib.ExitStack()

    def sbg(name, shape, dt):
        return es_g.enter_context(nc.sbuf_tensor(name, shape, dt))

    g_mask = sbg("g_mask", [128, 128], F32)
    g_id16 = sbg("g_id16", [128, 128], BF16)
    g_ones16 = sbg("g_ones16", [128, 1], BF16)
    g_onescol = sbg("g_onescol", [128, 1], F32)
    g_onesrow = sbg("g_onesrow", [1, 128], F32)
    g_mn = sbg("g_mn", [128, H * M], F32)
    g_mnb = sbg("g_mnb", [128, H * M], BF16)
    g_b1 = sbg("g_b1", [128, FT], F32)
    g_b2 = sbg("g_b2", [128, 16], F32)
    g_b1k = sbg("g_b1k", [128, FT], F32)
    g_b2q = sbg("g_b2q", [128, 16], F32)
    g_row = sbg("g_row", [1, 64], F32)
    g_row2 = sbg("g_row2", [1, 64], F32)
    g_grow = sbg("g_grow", [1, H], F32)
    g_gbar = sbg("g_gbar", [1, H], F32)
    g_em30 = sbg("g_em30", [1, M], F32)
    g_emc = sbg("g_emc", [1, M], F32)
    g_cols = sbg("g_cols", [128, 28], F32)
    g_x0st = sbg("g_x0st", [128, 2048], F32)
    g_idf = sbg("g_idf", [128, 128], F32)

    CN = {n: i for i, n in enumerate(
        ["amax1", "amax2", "sw1", "sw2", "inv1", "inv2", "s1", "s2",
         "invs1", "invs2", "c1", "invc1", "k", "c2", "invc2", "qsum", "zf",
         "mean", "ssq", "msq", "var", "rstd", "nmr", "scale2", "tmp",
         "zt0", "zt1", "zt2"])}

    def col(name):
        i = CN[name]
        return g_cols[:, i:i + 1]

    es_p = contextlib.ExitStack()
    P = [es_p.enter_context(nc.psum_tensor("pb%d" % i, [128, 512], F32))
         for i in range(6)]
    TA = es_p.enter_context(nc.psum_tensor("pta", [128, 1024], BF16))
    TB = es_p.enter_context(nc.psum_tensor("ptb", [128, 1024], BF16))
    TAB = [TA, TB]

    es_s = contextlib.ExitStack()
    sem = {}
    for name in ["qC", "qW0", "qW1", "qX", "qF0", "qF1", "qG0", "qG1",
                 "qSx0", "qSx1", "qSo0", "qSo1", "qSa0", "qSa1", "qSb0",
                 "qSb1", "tM", "vE", "cE", "gE", "bar"]:
        sem[name] = es_s.enter_context(nc.semaphore("sem_" + name))
    C = {k: 0 for k in sem}

    blk_es = contextlib.ExitStack()
    blk = blk_es.enter_context(nc.Block())

    DMA_SEMS = ["qC", "qW0", "qW1", "qX", "qF0", "qF1", "qG0", "qG1",
                "qSx0", "qSx1", "qSo0", "qSo1", "qSa0", "qSa1", "qSb0",
                "qSb1"]

    def barrier():
        waits = [(q, C[q]) for q in DMA_SEMS if C[q]]

        def _w(e, waits=tuple(waits)):
            for q, v in waits:
                e.wait_ge(sem[q], v)
        blk.sync(_w)
        C["bar"] += 5
        target = C["bar"]
        for en in ["sync", "scalar", "vector", "tensor", "gpsimd"]:
            def _(e, target=target):
                e.drain().then_inc(sem["bar"], 1)
                e.wait_ge(sem["bar"], target)
            getattr(blk, en)(_)

    es_am = contextlib.ExitStack()
    amix = es_am.enter_context(nc.sbuf_tensor("amix", [128, H * 2048], BF16))
    es_xfm = contextlib.ExitStack()
    xfm = es_xfm.enter_context(nc.sbuf_tensor("xfm", [128, DC * 2048], BF16))
    es_p0 = contextlib.ExitStack()
    x0b = [es_p0.enter_context(nc.sbuf_tensor("x0b%d" % i, [128, 2048], BF16))
           for i in range(2)]

    def _(e):
        for t, src in [(g_mask, maskT), (g_id16, id16), (g_ones16, ones16),
                       (g_onescol, onescol), (g_onesrow, onesrow),
                       (g_mn, mnh), (g_b1, b1h), (g_b2, b2h),
                       (g_grow, gater)]:
            e.dma_start(t[:], src[:]).then_inc(sem["qC"], 16)
        e.dma_start(g_row2[0:1, 44:45], s1t[:]).then_inc(sem["qC"], 16)
        e.dma_start(g_row2[0:1, 45:46], s2t[:]).then_inc(sem["qC"], 16)
    blk.sync(_)
    C["qC"] += 16 * 11
    consts_ld = C["qC"]

    def _(e):
        e.wait_ge(sem["qC"], consts_ld)
        e.activation(g_row2[0:1, 0:H], g_grow[0:1, :], Act.Exp, bias=0.0,
                     scale=-1.0).then_inc(sem["cE"], 1)
    blk.scalar(_)
    C["cE"] += 1
    gexp = C["cE"]

    def _(e, gexp=gexp):
        e.wait_ge(sem["cE"], gexp)
        e.tensor_single_scalar(g_row2[0:1, 16:16 + H], g_row2[0:1, 0:H], 1.0,
                               Alu.add)
        e.drain()
        e.reciprocal(g_grow[0:1, :], g_row2[0:1, 16:16 + H])
        e.drain()
        e.tensor_scalar(g_gbar[0:1, :], g_grow[0:1, :], -1.0, 1.0, Alu.mult,
                        Alu.add)
        e.tensor_copy(g_mnb[:], g_mn[:])
        e.tensor_copy(g_idf[:], g_id16[:])
        e.drain().then_inc(sem["vE"], 1)
    blk.vector(_)
    C["vE"] += 1

    def _(e):
        e.wait_ge(sem["qC"], consts_ld)
        e.matmul(P[5][0:1, 0:64], g_onescol[:], g_mn[:], start=True,
                 stop=True).then_inc(sem["tM"], 1)
    blk.tensor(_)
    C["tM"] += 1
    em_mm = C["tM"]

    def _(e, em_mm=em_mm):
        e.wait_ge(sem["tM"], em_mm)
        e.tensor_copy(g_row[0:1, 0:64], P[5][0:1, 0:64])
        e.drain()
        e.tensor_reduce(g_row2[0:1, 32:32 + M],
                        g_row[0:1, 0:64].rearrange("p (h m) -> p m h", m=M),
                        AX, Alu.add)
        e.drain()
        e.tensor_single_scalar(g_row2[0:1, 36:36 + M], g_row2[0:1, 32:32 + M],
                               1e-6, Alu.is_lt)
        e.drain()
        e.tensor_single_scalar(g_em30[0:1, :], g_row2[0:1, 36:36 + M], 1e30,
                               Alu.mult)
        e.tensor_scalar(g_emc[0:1, :], g_row2[0:1, 36:36 + M], -1.0, 1.0,
                        Alu.mult, Alu.add)
        e.drain().then_inc(sem["vE"], 1)
    blk.vector(_)
    C["vE"] += 1

    x0cast = [0] * ST
    for st in range(ST):
        def _(e, st=st):
            if st >= 1:
                e.wait_ge(sem["vE"], x0cast[st - 1])
            e.dma_start(g_x0st[:], x0[st * 128:(st + 1) * 128, :]).then_inc(
                sem["qX"], 16)
        blk.sync(_)
        C["qX"] += 16
        ld = C["qX"]

        def _(e, st=st, ld=ld):
            e.wait_ge(sem["qX"], ld)
            e.tensor_copy(x0b[st % 2][:], g_x0st[:]).then_inc(sem["vE"], 1)
        blk.vector(_)
        C["vE"] += 1
        x0cast[st] = C["vE"]

        vpre = C["vE"]

        def _(e, st=st, vpre=vpre):
            e.wait_ge(sem["vE"], x0cast[st])
            for dc in range(DC):
                if dc >= 2:
                    e.wait_ge(sem["vE"], vpre + dc - 1)
                e.transpose(TAB[dc % 2][:, 0:128],
                            x0b[st % 2][:, dc * 128:(dc + 1) * 128],
                            g_id16[:]).then_inc(sem["tM"], 1)
        blk.tensor(_)
        tb = C["tM"]
        C["tM"] += DC

        def _(e, st=st, tb=tb):
            for dc in range(DC):
                e.wait_ge(sem["tM"], tb + dc + 1)
                e.tensor_copy(
                    xfm[:, dc * 2048 + st * 128: dc * 2048 + (st + 1) * 128],
                    TAB[dc % 2][:, 0:128]).then_inc(sem["vE"], 1)
        blk.vector(_)
        C["vE"] += DC

    barrier()
    es_p0.close()

    es_A = contextlib.ExitStack()

    def sba(name, shape, dt):
        return es_A.enter_context(nc.sbuf_tensor(name, shape, dt))

    wst = [sba("wst0", [128, 2048], F32), sba("wst1", [128, 2048], F32)]
    qp = [sba("qp0", [128, 2048], BF16), sba("qp1", [128, 2048], BF16)]
    kp = [sba("kp0", [128, 2048], BF16), sba("kp1", [128, 2048], BF16)]
    vf = [sba("vf0", [128, 2048], BF16), sba("vf1", [128, 2048], BF16)]
    wb = [sba("wb0", [128, 2048], BF16), sba("wb1", [128, 2048], BF16)]
    memb = [sba("memb0", [128, 512], BF16), sba("memb1", [128, 512], BF16)]
    qevW = sba("qevW", [128, 1024], F32)
    tmpW = sba("tmpW", [128, 1024], F32)
    S16 = [sba("S16a", [128, 128], BF16), sba("S16b", [128, 128], BF16)]
    S16z = sba("S16z", [128, 128], BF16)
    dnrhs_all = [sba("dnrhs0", [128, 96], BF16), sba("dnrhs1", [128, 96], BF16)]
    zt16 = sba("zt16", [128, 16], F32)
    kvT = sba("kvT", [128, 256], BF16)
    msT = sba("msT", [128, 128], BF16)
    mixb = [sba("mixb0", [128, 128], BF16), sba("mixb1", [128, 128], BF16)]
    mixw = sba("mixw", [128, 512], F32)
    mixf128 = sba("mixf128", [128, 128], F32)
    dnc = sba("dnc", [128, 5], F32)
    dnr = sba("dnr", [128, 5], F32)
    facm = sba("facm", [128, 5], F32)
    wgb = [sba("wgbA", [128, 5], F32), sba("wgbB", [128, 5], F32)]
    ztmp = sba("ztmp", [128, 2], F32)

    wst_free = [("vE", C["vE"]), ("vE", C["vE"])]
    wst_ld = [("qW0", 0), ("qW1", 0)]

    def load_wst(slot, src_ap):
        qn = "qW%d" % slot
        def _(e, slot=slot, src_ap=src_ap, fr=wst_free[slot], qn=qn):
            if fr[1]:
                e.wait_ge(sem[fr[0]], fr[1])
            e.dma_start(wst[slot][:], src_ap).then_inc(sem[qn], 16)
        blk.sync(_)
        C[qn] += 16
        wst_ld[slot] = (qn, C[qn])

    am_chunks = [("W1", i) for i in range(FT)] + \
                [("W2", i) for i in range(64)]
    am_next = [0]

    def emit_am_chunk(slot):
        if am_next[0] >= len(am_chunks):
            return
        which, ci = am_chunks[am_next[0]]
        am_next[0] += 1
        if which == "W1":
            src = W1h[:, ci, :, :]
            acc = col("amax1")
            first = ci == 0
        else:
            dt, sub = divmod(ci, 4)
            src = W2h[:, dt, sub * 16:(sub + 1) * 16, :]
            acc = col("amax2")
            first = ci == 0
        load_wst(slot, src)
        ld = wst_ld[slot]

        def _(e, ld=ld, acc=acc, slot=slot, first=first):
            e.wait_ge(sem[ld[0]], ld[1])
            e.tensor_reduce(ztmp[:, 1:2], wst[slot][:], AX, Alu.max,
                            apply_absolute_value=True)
            e.drain()
            if first:
                e.tensor_copy(acc, ztmp[:, 1:2])
            else:
                e.tensor_tensor(acc, acc, ztmp[:, 1:2], Alu.max)
            e.drain().then_inc(sem["vE"], 1)
        blk.vector(_)
        C["vE"] += 1
        wst_free[slot] = ("vE", C["vE"])

    wb_cast = [0, 0]
    wb_free = [0, 0]
    memb_cast = [0, 0]
    memb_free = [0, 0]


    def _z(e):
        e.memzero(S16z[:])
        e.drain().then_inc(sem["vE"], 1)
    blk.vector(_z)
    C["vE"] += 1
    s16z0 = C["vE"]

    head_marks = {}
    p4x = [0]
    cur_dn_ev = [0]
    act_free = [0, 0]
    vec_free = [0, 0]

    def make_tasks(g):
        par = g % 2
        prev = head_marks.get(g - 1, {})
        marks = {"p1ev": 0, "eluadd": prev.get("qev_last", 0),
                 "exp": prev.get("exp", 0)}
        head_marks[g] = marks
        tasks = []

        def make_elu(wi, hf, par):
            def t_elu(wi=wi, hf=hf, par=par):
                vb = C["vE"]
                if wi == 0:
                    def _(e, ev=marks["p1ev"], hf=hf):
                        e.wait_ge(sem["vE"], ev)
                        e.reduce_sum(col("zt1" if hf == 0 else "zt2"),
                                     qevW[:], AX).then_inc(sem["vE"], 1)
                    blk.vector(_)
                    C["vE"] += 1
                    qs_red = C["vE"]
                else:
                    qs_red = 0
                cb = C["cE"]

                def _(e, ev=marks["p1ev"], ex=marks.get("exp", 0), cb=cb,
                      qs_red=qs_red):
                    e.wait_ge(sem["vE"], ev)
                    if ex:
                        e.wait_ge(sem["vE"], ex)
                    if qs_red:
                        e.wait_ge(sem["vE"], qs_red)
                    e.activation(tmpW[:], qevW[:], Act.Exp).then_inc(
                        sem["cE"], 1)
                    e.wait_ge(sem["cE"], cb + 1)
                    e.activation(qevW[:], qevW[:], Act.Relu).then_inc(
                        sem["cE"], 1)
                blk.scalar(_)
                C["cE"] += 2
                echk = C["cE"]
                vb2 = C["vE"]

                def _(e, echk=echk, wi=wi, hf=hf, par=par, vb2=vb2,
                      af=act_free[par]):
                    e.wait_ge(sem["cE"], echk)
                    if af and hf == 0:
                        e.wait_ge(sem["tM"], af)
                    dst = [qp, kp][wi]
                    e.scalar_tensor_tensor(
                        dst[par][:, hf * 1024:(hf + 1) * 1024], tmpW[:], 1.0,
                        qevW[:], Alu.min, Alu.add).then_inc(sem["vE"], 1)
                    if wi == 0 and hf == 1:
                        e.tensor_tensor(col("qsum"), col("zt1"), col("zt2"),
                                        Alu.add).then_inc(sem["vE"], 1)
                blk.vector(_)
                C["vE"] += 2 if (wi == 0 and hf == 1) else 1
                marks["exp"] = C["vE"] if not (wi == 0 and hf == 1) \
                    else C["vE"] - 1
                marks["eluadd"] = marks["exp"]
                if wi == 0 and hf == 1:
                    marks["qsum"] = C["vE"]
                if hf == 1:
                    marks["qp" if wi == 0 else "kp"] = marks["eluadd"]
                    if wi == 1:
                        marks["qev_last"] = marks["eluadd"]
            return t_elu

        for wi in range(3):
            wbslot = [0, 1, 0][wi]
            lslot = wi % 2

            def t_load(wi=wi, lslot=lslot, g=g):
                wten = [wq, wk, wv][wi]
                load_wst(lslot, wten[:, g, :, :])
            tasks.append(t_load)

            def t_cast(lslot=lslot, wbslot=wbslot):
                def _(e, ldw=wst_ld[lslot], fr=wb_free[wbslot]):
                    e.wait_ge(sem[ldw[0]], ldw[1])
                    if fr:
                        e.wait_ge(sem["tM"], fr)
                    e.tensor_copy(wb[wbslot][:], wst[lslot][:]).then_inc(
                        sem["vE"], 1)
                blk.vector(_)
                C["vE"] += 1
                wb_cast[wbslot] = C["vE"]
                wst_free[lslot] = ("vE", C["vE"])
            tasks.append(t_cast)

            for tb in range(4):
                def t_piece(wi=wi, wbslot=wbslot, tb=tb, par=par):
                    def _(e, wc=wb_cast[wbslot], pv=marks["p1ev"]):
                        e.wait_ge(sem["vE"], wc)
                        if pv:
                            e.wait_ge(sem["vE"], pv)
                        for dc in range(DC):
                            mm = e.matmul(
                                P[1][:], wb[wbslot][:, dc * 128:(dc + 1) * 128],
                                xfm[:, dc * 2048 + tb * 512:
                                    dc * 2048 + (tb + 1) * 512],
                                start=(dc == 0), stop=(dc == DC - 1),
                                skip_group_check=True)
                            if dc == DC - 1:
                                mm.then_inc(sem["tM"], 1)
                    blk.tensor(_)
                    C["tM"] += 1
                    pm = C["tM"]

                    def _(e, pm=pm, wi=wi, tb=tb, par=par,
                          qg=marks.get("eluadd", 0),
                          af=act_free[par]):
                        if tb % 2 == 0 and wi < 2 and qg:
                            e.wait_ge(sem["vE"], qg)
                        if tb == 0 and wi == 2 and af:
                            e.wait_ge(sem["tM"], af)
                        e.wait_ge(sem["tM"], pm)
                        dst = (vf[par][:, tb * 512:(tb + 1) * 512] if wi == 2
                               else qevW[:, (tb % 2) * 512:
                                          (tb % 2 + 1) * 512])
                        e.tensor_copy(dst, P[1][:]).then_inc(sem["vE"], 1)
                    blk.vector(_)
                    C["vE"] += 1
                    marks["p1ev"] = C["vE"]
                    if wi == 2 and tb == 3:
                        marks["vfc"] = C["vE"]
                tasks.append(t_piece)
                if wi < 2 and tb % 2 == 1:
                    tasks.append(make_elu(wi, tb // 2, par))


        def t_mem(g=g, par=par):
            def _(e, fr=wst_free[1]):
                if fr[1]:
                    e.wait_ge(sem[fr[0]], fr[1])
                e.dma_start(wst[1][:, 0:512], memh[g, :, :]).then_inc(
                    sem["qW1"], 16)
            blk.sync(_)
            C["qW1"] += 16
            wst_ld[1] = ("qW1", C["qW1"])

            def _(e, ldm=wst_ld[1], fr=memb_free[par]):
                e.wait_ge(sem[ldm[0]], ldm[1])
                if fr:
                    e.wait_ge(sem["tM"], fr)
                e.tensor_copy(memb[par][:], wst[1][:, 0:512]).then_inc(
                    sem["vE"], 1)
            blk.vector(_)
            C["vE"] += 1
            memb_cast[par] = C["vE"]
            marks["memb"] = C["vE"]
            wst_free[1] = ("vE", C["vE"])
        tasks.append(t_mem)

        def t_relmm(g=g):
            def _(e, qs=marks["qsum"], de=cur_dn_ev[0]):
                e.wait_ge(sem["vE"], qs)
                if de:
                    e.wait_ge(sem["vE"], de)
                e.matmul(P[4][0:1, 70:70 + M], col("qsum"),
                         g_mn[:, g * M:(g + 1) * M], start=True, stop=True,
                         skip_group_check=True).then_inc(sem["tM"], 1)
            blk.tensor(_)
            C["tM"] += 1
            marks["relmm"] = C["tM"]
        tasks.append(t_relmm)

        def t_relv1():
            vb = C["vE"]

            def _(e, rm=marks["relmm"], vb=vb):
                e.wait_ge(sem["tM"], rm)
                e.scalar_tensor_tensor(g_row[0:1, 0:M], P[4][0:1, 70:70 + M],
                                       SCALE / S, g_em30[0:1, :], Alu.mult,
                                       Alu.subtract).then_inc(sem["vE"], 1)
                e.wait_ge(sem["vE"], vb + 1)
                e.reduce_max(g_row[0:1, 8:9], g_row[0:1, 0:M],
                             AX).then_inc(sem["vE"], 1)
                e.wait_ge(sem["vE"], vb + 2)
                e.tensor_single_scalar(g_row[0:1, 9:10], g_row[0:1, 8:9],
                                       -1.0, Alu.mult).then_inc(sem["vE"], 1)
            blk.vector(_)
            C["vE"] += 3
            p4x[0] = max(p4x[0], C["vE"] - 2)
            marks["relv"] = C["vE"]
        tasks.append(t_relv1)

        def t_rele():
            def _(e, rv=marks["relv"]):
                e.wait_ge(sem["vE"], rv)
                e.activation(g_row[0:1, 4:4 + M], g_row[0:1, 0:M], Act.Exp,
                             bias=g_row[0:1, 9:10], scale=1.0).then_inc(
                    sem["cE"], 1)
            blk.scalar(_)
            C["cE"] += 1
            marks["rele"] = C["cE"]
        tasks.append(t_rele)

        def t_relv2(g=g):
            vb = C["vE"]

            def _(e, re_=marks["rele"], vb=vb, g=g):
                e.wait_ge(sem["cE"], re_)
                e.reduce_sum(g_row[0:1, 10:11], g_row[0:1, 4:4 + M],
                             AX).then_inc(sem["vE"], 1)
                e.wait_ge(sem["vE"], vb + 1)
                e.reciprocal(g_row[0:1, 11:12],
                             g_row[0:1, 10:11]).then_inc(sem["vE"], 1)
                e.wait_ge(sem["vE"], vb + 2)
                e.tensor_single_scalar(g_row[0:1, 12:12 + M],
                                       g_row[0:1, 4:4 + M],
                                       g_row[0:1, 11:12],
                                       Alu.mult).then_inc(sem["vE"], 1)
                e.wait_ge(sem["vE"], vb + 3)
                e.tensor_tensor(g_row2[0:1, 0:M], g_row[0:1, 12:12 + M],
                                g_emc[0:1, :], Alu.mult).then_inc(sem["vE"], 1)
                e.wait_ge(sem["vE"], vb + 4)
                e.tensor_single_scalar(g_row2[0:1, 4:4 + M], g_row2[0:1, 0:M],
                                       g_grow[0:1, g:g + 1],
                                       Alu.mult).then_inc(sem["vE"], 1)
                e.tensor_copy(g_row2[0:1, 8:9],
                              g_gbar[0:1, g:g + 1]).then_inc(sem["vE"], 1)
            blk.vector(_)
            C["vE"] += 6
            marks["wrow"] = C["vE"]
        tasks.append(t_relv2)

        def t_bcast():
            def _(e, wr=marks["wrow"], de=cur_dn_ev[0]):
                e.wait_ge(sem["vE"], wr)
                if de:
                    e.wait_ge(sem["vE"], de)
                e.matmul(P[4][:, 64:69], g_onesrow[:], g_row2[0:1, 4:9],
                         start=True, stop=True,
                         skip_group_check=True).then_inc(sem["tM"], 1)
            blk.tensor(_)
            C["tM"] += 1
            marks["bcmm"] = C["tM"]
        tasks.append(t_bcast)

        def t_wgb(par=par):
            def _(e, bm=marks["bcmm"], vfree=vec_free[par]):
                e.wait_ge(sem["tM"], bm)
                if vfree:
                    e.wait_ge(sem["vE"], vfree)
                e.tensor_copy(wgb[par][:], P[4][:, 64:69]).then_inc(
                    sem["vE"], 1)
            blk.vector(_)
            C["vE"] += 1
            p4x[0] = max(p4x[0], C["vE"])
            marks["wgb"] = C["vE"]
        tasks.append(t_wgb)

        def t_zscan(g=g, par=par):
            vb = C["vE"]

            def _(e, kpm=marks["kp"], vb=vb, par=par, g=g,
                  af=act_free[par]):
                e.wait_ge(sem["vE"], kpm)
                if af:
                    e.wait_ge(sem["tM"], af)
                e.tensor_reduce(zt16[:],
                                kp[par][:].rearrange("p (c j) -> p c j",
                                                     c=NCH),
                                AX, Alu.add).then_inc(sem["vE"], 1)
                e.tensor_copy(
                    bass.AP(dnrhs_all[par], 0, [[96, 128], [6, 16], [1, 4]]),
                    bass.AP(g_mnb, g * M, [[H * M, 128], [0, 16],
                                           [1, 4]])).then_inc(sem["vE"], 1)
                e.memzero(dnrhs_all[par][:, 4:6])
                e.drain().then_inc(sem["vE"], 1)
                e.wait_ge(sem["vE"], vb + 3)
                e.tensor_tensor_scan(
                    bass.AP(dnrhs_all[par], 10, [[96, 128], [6, 15]]),
                    zt16[:, 0:15], zt16[:, 0:15], 0.0, Alu.add,
                    Alu.bypass).then_inc(sem["vE"], 1)
            blk.vector(_)
            C["vE"] += 4
            marks["dn"] = C["vE"]
        tasks.append(t_zscan)

        return tasks

    kvT_ev = mx_ev = 0
    kv_ev = 0
    dn_ev = am_ev = sc_ev = 0
    mixb_rdy = [0, 0]

    for t in make_tasks(0):
        t()

    for h in range(H):
        par = h % 2
        m = head_marks[h]
        head_ready = max(m["qp"], m["kp"], m["memb"], m["wgb"],
                         m["dn"], m["vfc"], s16z0)
        head_ready_c = 0
        tasks = make_tasks(h + 1) if h + 1 < H else []
        ti = [0]

        def run_tasks(c):
            remaining = len(tasks) - ti[0]
            gaps = NCH - c
            need = -(-remaining // gaps) if gaps > 0 else remaining
            for _ in range(need):
                if ti[0] < len(tasks):
                    tasks[ti[0]]()
                    ti[0] += 1

        supd = ("vE", head_ready)

        for c in range(NCH):
            if c > 0:
                run_tasks(c)
            cs = slice(c * 128, (c + 1) * 128)
            vpre = C["vE"]
            msT_target = vpre + 1
            this_kvT = C["gE"] + 1

            def _(e, c=c, cs=cs, h=h, par=par, supd=supd, kvT_ev=kvT_ev,
                  sc_ev=sc_ev, dn_ev=dn_ev, am_ev=am_ev, kv_ev=kv_ev,
                  mx_ev=mx_ev, head_ready=head_ready, this_kvT=this_kvT,
                  head_ready_c=head_ready_c,
                  msT_target=msT_target, mbr=tuple(mixb_rdy), px=p4x[0]):
                e.wait_ge(sem["vE"], head_ready)
                if kvT_ev:
                    e.wait_ge(sem["gE"], kvT_ev)
                e.transpose(TA[:, 0:128], kp[par][:, cs], g_id16[:])
                e.transpose(TA[:, 128:256], vf[par][:, cs],
                            g_id16[:]).then_inc(sem["tM"], 1)
                if sc_ev:
                    e.wait_ge(sem["vE"], sc_ev)
                e.matmul(P[0][:, 0:128], kp[par][:, cs], qp[par][:, cs],
                         start=True, stop=True).then_inc(sem["tM"], 1)
                e.wait_ge(sem[supd[0]], supd[1])
                if am_ev:
                    e.wait_ge(sem["vE"], am_ev)
                e.matmul(P[3][:, 0:128], qp[par][:, cs],
                         (S16z if c == 0 else S16[c % 2])[:],
                         start=True, stop=False, skip_group_check=True)
                if dn_ev:
                    e.wait_ge(sem["vE"], dn_ev)
                if px:
                    e.wait_ge(sem["vE"], px)
                e.matmul(P[4][:, 0:5], qp[par][:, cs],
                         dnrhs_all[par][:, 6 * c:6 * c + 5], start=True,
                         stop=False, skip_group_check=True)
                if am_ev:
                    e.wait_ge(sem["vE"], am_ev)
                e.matmul(P[5][:], qp[par][:, cs], memb[par][:], start=True,
                         stop=True).then_inc(sem["tM"], 1)
                e.wait_ge(sem["vE"], msT_target)
                e.wait_ge(sem["gE"], this_kvT)
                e.matmul(P[3][:, 0:128], msT[:], kvT[:, 128:256],
                         start=False, stop=True, skip_group_check=True)
                e.matmul(P[4][:, 4:5], msT[:], g_ones16[:], start=False,
                         stop=True, skip_group_check=True).then_inc(
                    sem["tM"], 1)
                if kv_ev:
                    e.wait_ge(sem["gE"], kv_ev)
                e.matmul(P[2][:, 0:128], kvT[:, 0:128], kvT[:, 128:256],
                         start=(c == 0), stop=(c == NCH - 1),
                         skip_group_check=True).then_inc(sem["tM"], 1)
                if c > 0:
                    e.wait_ge(sem["vE"], mbr[(c - 1) % 2])
                    if mx_ev:
                        e.wait_ge(sem["gE"], mx_ev)
                    e.transpose(TB[:, 0:128], mixb[(c - 1) % 2][:],
                                g_id16[:]).then_inc(sem["tM"], 1)
            blk.tensor(_)
            tb = C["tM"]
            n_t = 5 + (1 if c > 0 else 0)
            C["tM"] += n_t
            t_kvT, t_sc, t_qgrp, t_intra, t_kv = (tb + 1, tb + 2, tb + 3,
                                                  tb + 4, tb + 5)
            t_mx = tb + 6 if c > 0 else 0

            spre = C["gE"]

            def _(e, c=c, h=h, t_kvT=t_kvT, t_kv=t_kv, t_mx=t_mx):
                e.wait_ge(sem["tM"], t_kvT)
                e.activation(kvT[:], TA[:, 0:256],
                             Act.Identity).then_inc(sem["gE"], 1)
                e.wait_ge(sem["tM"], t_kv)
                e.activation(S16[(c + 1) % 2][:], P[2][:, 0:128],
                             Act.Identity).then_inc(sem["gE"], 1)
                if t_mx:
                    e.wait_ge(sem["tM"], t_mx)
                    e.activation(
                        amix[:, h * 2048 + (c - 1) * 128:h * 2048 + c * 128],
                        TB[:, 0:128], Act.Identity).then_inc(sem["gE"], 1)
            blk.scalar(_)
            C["gE"] += 3 if c > 0 else 2
            kvT_ev = spre + 1
            kv_ev = spre + 2
            supd = ("gE", spre + 2)
            mx_ev = spre + 3 if c > 0 else 0

            def _(e, c=c, par=par, t_sc=t_sc, t_intra=t_intra, vpre=vpre):
                e.wait_ge(sem["tM"], t_sc)
                e.tensor_tensor(msT[:], P[0][:, 0:128], g_mask[:],
                                Alu.mult).then_inc(sem["vE"], 1)
                e.wait_ge(sem["tM"], t_intra)
                e.tensor_single_scalar(dnc[:], P[4][:, 0:5], EPS,
                                       Alu.max).then_inc(sem["vE"], 1)
                e.wait_ge(sem["vE"], vpre + 2)
                e.reciprocal(dnr[:], dnc[:]).then_inc(sem["vE"], 1)
                e.wait_ge(sem["vE"], vpre + 3)
                e.tensor_tensor(facm[:, 0:5], dnr[:, 0:5], wgb[par][:, 0:5],
                                Alu.mult).then_inc(sem["vE"], 1)
            blk.vector(_)
            C["vE"] += 4
            sc_ev = vpre + 1
            dn_ev = vpre + 2
            cur_dn_ev[0] = dn_ev
            fac_done = vpre + 4

            def _(e, c=c, fd=fac_done, vpre=vpre):
                e.wait_ge(sem["vE"], fd)
                e.tensor_tensor(
                    mixw[:], P[5][:, 0:512],
                    facm[:, 0:4].unsqueeze(-1).broadcast_to([128, 4, 128]),
                    Alu.mult).then_inc(sem["vE"], 1)
                e.wait_ge(sem["vE"], vpre + 5)
                e.tensor_reduce(mixf128[:],
                                mixw[:].rearrange("p (m j) -> p j m", m=4),
                                AX, Alu.add).then_inc(sem["vE"], 1)
                e.wait_ge(sem["vE"], vpre + 6)
                e.scalar_tensor_tensor(mixb[c % 2][:], P[3][:, 0:128],
                                       facm[:, 4:5], mixf128[:], Alu.mult,
                                       Alu.add).then_inc(sem["vE"], 1)
            blk.vector(_)
            C["vE"] += 3
            am_ev = C["vE"]
            mixb_rdy[c % 2] = C["vE"]

        def _(e, h=h, mr=mixb_rdy[(NCH - 1) % 2], mx_ev=mx_ev):
            e.wait_ge(sem["vE"], mr)
            if mx_ev:
                e.wait_ge(sem["gE"], mx_ev)
            e.transpose(TB[:, 0:128], mixb[(NCH - 1) % 2][:],
                        g_id16[:]).then_inc(sem["tM"], 1)
        blk.tensor(_)
        C["tM"] += 1
        tfin = C["tM"]
        memb_free[par] = C["tM"]
        act_free[par] = C["tM"]

        def _(e, h=h, tfin=tfin):
            e.wait_ge(sem["tM"], tfin)
            e.activation(amix[:, h * 2048 + (NCH - 1) * 128:(h + 1) * 2048],
                         TB[:, 0:128], Act.Identity).then_inc(sem["gE"], 1)
        blk.scalar(_)
        C["gE"] += 1
        mx_ev = C["gE"]
        vec_free[par] = C["vE"]
        run_tasks(NCH)


    def _(e):
        e.matmul(P[4][:, 32:34], g_onesrow[:], g_row2[0:1, 44:46], start=True,
                 stop=True, skip_group_check=True).then_inc(sem["tM"], 1)
    blk.tensor(_)
    C["tM"] += 1
    sbc0 = C["tM"]
    vbs = C["vE"]

    def _(e, sbc0=sbc0, vbs=vbs):
        e.wait_ge(sem["tM"], sbc0)
        e.tensor_copy(col("s1"), P[4][:, 32:33]).then_inc(sem["vE"], 1)
        e.tensor_copy(col("s2"), P[4][:, 33:34]).then_inc(sem["vE"], 1)
        e.wait_ge(sem["vE"], vbs + 2)
        e.reciprocal(col("invs1"), col("s1")).then_inc(sem["vE"], 1)
        e.reciprocal(col("invs2"), col("s2")).then_inc(sem["vE"], 1)
    blk.vector(_)
    C["vE"] += 4
    cols_done = C["vE"]

    barrier()
    es_A.close()
    es_xfm.close()

    es_B = contextlib.ExitStack()

    def sbb(name, shape, dt):
        return es_B.enter_context(nc.sbuf_tensor(name, shape, dt))

    wo_sb = sbb("wo_sb", [128, DC * 2048], BF16)
    wstB = [sbb("wstB0", [128, 2048], F32), sbb("wstB1", [128, 2048], F32)]
    ztmpB = sbb("ztmpB", [128, 2], F32)
    x1pre = sbb("x1pre", [128, 2048], F32)
    x1s = [sbb("x1s0", [128, 1024], F32)]
    x1s = [x1s[0], x1s[0]]
    xc2 = sbb("xc2", [128, 512], F32)
    rint_b = sbb("rint_b", [128, 512], I32)
    dbf_b = sbb("dbf_b", [128, 512], BF16)
    abf_b = sbb("abf_b", [128, 512], BF16)
    rb16_b = sbb("rb16_b", [128, 512], BF16)
    xqt = sbb("xqt", [128, 512], BF16)
    es_xq = contextlib.ExitStack()
    xq = es_xq.enter_context(nc.sbuf_tensor("xq", [128, DC * 2048], FP8, side="right"))

    wo_cast = 0
    for ec in range(DC):
        def _(e, ec=ec, wc=wo_cast):
            e.wait_ge(sem["vE"], wc)
            e.dma_start(g_x0st[:], wo[:, ec, :]).then_inc(sem["qX"], 16)
        blk.sync(_)
        C["qX"] += 16
        ld = C["qX"]

        def _(e, ec=ec, ld=ld):
            e.wait_ge(sem["qX"], ld)
            e.tensor_copy(wo_sb[:, ec * 2048:(ec + 1) * 2048],
                          g_x0st[:]).then_inc(sem["vE"], 1)
        blk.vector(_)
        C["vE"] += 1
        wo_cast = C["vE"]

    wB_free = [("vE", 0), ("vE", 0)]
    am_prev_v = [0]

    def emit_am_chunk_B(slot):
        if am_next[0] >= len(am_chunks):
            return
        which, ci = am_chunks[am_next[0]]
        am_next[0] += 1
        if which == "W1":
            src = W1h[:, ci, :, :]
            acc = col("amax1")
            first = ci == 0
        else:
            dt, sub = divmod(ci, 4)
            src = W2h[:, dt, sub * 16:(sub + 1) * 16, :]
            acc = col("amax2")
            first = ci == 0
        qn = "qW%d" % slot

        def _(e, src=src, slot=slot, fr=wB_free[slot], qn=qn):
            if fr[1]:
                e.wait_ge(sem[fr[0]], fr[1])
            e.dma_start(wstB[slot][:], src).then_inc(sem[qn], 16)
        blk.sync(_)
        C[qn] += 16
        ld = (qn, C[qn])
        vb = C["vE"]

        def _(e, ld=ld, acc=acc, slot=slot, first=first, vb=vb,
              pz=am_prev_v[0]):
            e.wait_ge(sem[ld[0]], ld[1])
            if pz:
                e.wait_ge(sem["vE"], pz)
            e.tensor_reduce(ztmpB[:, 1:2], wstB[slot][:], AX, Alu.max,
                            apply_absolute_value=True).then_inc(sem["vE"], 1)
            e.wait_ge(sem["vE"], vb + 1)
            if first:
                e.tensor_copy(acc, ztmpB[:, 1:2]).then_inc(sem["vE"], 1)
            else:
                e.tensor_tensor(acc, acc, ztmpB[:, 1:2],
                                Alu.max).then_inc(sem["vE"], 1)
        blk.vector(_)
        C["vE"] += 2
        wB_free[slot] = ("vE", C["vE"] - 1)
        am_prev_v[0] = C["vE"]

    pp_evac = {0: 0, 1: 0}
    x0_read = wo_cast
    bank_free = 0
    x1_store = [0, 0]
    last_xchain = 0
    last_sq = 0
    for st in range(ST):
        grp = 0
        for _am in range(8):
            emit_am_chunk_B(_am % 2)

        def _(e, st=st, xr=x0_read, sq=last_sq):
            e.wait_ge(sem["vE"], xr)
            if sq:
                e.wait_ge(sem["cE"], sq)
            e.dma_start(g_x0st[:], x0[st * 128:(st + 1) * 128, :]).then_inc(
                sem["qX"], 16)
        blk.sync(_)
        C["qX"] += 16
        ldx = C["qX"]

        def _(e, st=st, grp=grp, wc=wo_cast, bf=bank_free):
            e.wait_ge(sem["vE"], wc)
            if bf:
                e.wait_ge(sem["vE"], bf)
            for ec in range(DC):
                for db in range(4):
                    mm = e.matmul(
                        P[grp + db][:],
                        amix[:, ec * 2048 + st * 128:ec * 2048 + (st + 1) * 128],
                        wo_sb[:, ec * 2048 + db * 512:ec * 2048 + (db + 1) * 512],
                        start=(ec == 0), stop=(ec == DC - 1))
                    if ec == DC - 1:
                        mm.then_inc(sem["tM"], 1)
        blk.tensor(_)
        C["tM"] += 4
        omm = C["tM"]

        def _(e, grp=grp, omm=omm, ldx=ldx, ls=last_sq):
            e.wait_ge(sem["qX"], ldx)
            if ls:
                e.wait_ge(sem["cE"], ls)
            for db in range(4):
                e.wait_ge(sem["tM"], omm - 3 + db)
                e.tensor_tensor(x1pre[:, db * 512:(db + 1) * 512],
                                P[grp + db][:],
                                g_x0st[:, db * 512:(db + 1) * 512],
                                Alu.add)
            e.drain().then_inc(sem["vE"], 1)
            e.reduce_sum(col("zt0"), x1pre[:], AX)
            e.drain().then_inc(sem["vE"], 1)
        blk.vector(_)
        C["vE"] += 2
        x0_read = C["vE"] - 1
        resid = C["vE"]
        bank_free = C["vE"] - 1

        def _(e, resid=resid, lx=last_xchain):
            e.wait_ge(sem["vE"], resid)
            if lx:
                e.wait_ge(sem["vE"], lx)
            e.activation(g_x0st[:, 0:512], x1pre[:, 0:512], Act.Square,
                         accum_out=col("zt1"))
            e.activation(g_x0st[:, 512:1024], x1pre[:, 512:1024], Act.Square,
                         accum_out=col("zt2"))
            e.activation(g_x0st[:, 1024:1536], x1pre[:, 1024:1536], Act.Square,
                         accum_out=col("msq"))
            e.activation(g_x0st[:, 1536:2048], x1pre[:, 1536:2048], Act.Square,
                         accum_out=col("ssq"))
            e.drain().then_inc(sem["cE"], 1)
        blk.scalar(_)
        C["cE"] += 1
        sq2_done = C["cE"]

        vb = C["vE"]

        def _(e, sq2_done=sq2_done, vb=vb):
            e.wait_ge(sem["cE"], sq2_done)
            e.tensor_tensor(col("ssq"), col("ssq"), col("msq"),
                            Alu.add).then_inc(sem["vE"], 1)
            e.wait_ge(sem["vE"], vb + 1)
            e.tensor_tensor(col("ssq"), col("ssq"), col("zt1"),
                            Alu.add).then_inc(sem["vE"], 1)
            e.tensor_single_scalar(col("mean"), col("zt0"), INV_D,
                                   Alu.mult).then_inc(sem["vE"], 1)
            e.wait_ge(sem["vE"], vb + 3)
            e.tensor_tensor(col("ssq"), col("ssq"), col("zt2"),
                            Alu.add).then_inc(sem["vE"], 1)
            e.tensor_tensor(col("msq"), col("mean"), col("mean"),
                            Alu.mult).then_inc(sem["vE"], 1)
            e.wait_ge(sem["vE"], vb + 5)
            e.scalar_tensor_tensor(col("var"), col("ssq"), INV_D, col("msq"),
                                   Alu.mult, Alu.subtract).then_inc(
                sem["vE"], 1)
            e.wait_ge(sem["vE"], vb + 6)
            e.tensor_single_scalar(col("var"), col("var"), LNEPS,
                                   Alu.add).then_inc(sem["vE"], 1)
        blk.vector(_)
        C["vE"] += 7
        stat1 = C["vE"]

        cb = C["cE"]

        def _(e, stat1=stat1, cb=cb):
            e.wait_ge(sem["vE"], stat1)
            e.activation(col("tmp"), col("var"), Act.Ln).then_inc(
                sem["cE"], 1)
            e.wait_ge(sem["cE"], cb + 1)
            e.activation(col("rstd"), col("tmp"), Act.Exp, bias=0.0,
                         scale=-0.5).then_inc(sem["cE"], 1)
        blk.scalar(_)
        C["cE"] += 2
        rstd_done = C["cE"]

        vb = C["vE"]

        def _(e, rstd_done=rstd_done, vb=vb):
            e.wait_ge(sem["cE"], rstd_done)
            e.tensor_single_scalar(col("nmr"), col("mean"), -1.0,
                                   Alu.mult).then_inc(sem["vE"], 1)
            e.wait_ge(sem["vE"], vb + 1)
            e.tensor_tensor(col("nmr"), col("nmr"), col("rstd"),
                            Alu.mult).then_inc(sem["vE"], 1)
            e.tensor_tensor(col("scale2"), col("rstd"), col("invs1"),
                            Alu.mult).then_inc(sem["vE"], 1)
        blk.vector(_)
        C["vE"] += 3
        lncols = C["vE"]

        for half in range(2):
            def _(e, half=half, lncols=lncols, xs0=x1_store[0],
                  xs1=x1_store[1]):
                e.wait_ge(sem["vE"], lncols)
                if xs0:
                    e.wait_ge(sem["qSx0"], xs0)
                if xs1:
                    e.wait_ge(sem["qSx1"], xs1)
                e.activation(x1s[half][:],
                             x1pre[:, half * 1024:(half + 1) * 1024],
                             Act.Identity, bias=col("nmr"),
                             scale=col("rstd")).then_inc(sem["cE"], 1)
            blk.scalar(_)
            C["cE"] += 1
            x1a = C["cE"]

            def _(e, half=half, st=st, x1a=x1a):
                e.wait_ge(sem["cE"], x1a)
                e.dma_start(x1scr[st * 128:(st + 1) * 128,
                                  half * 1024:(half + 1) * 1024],
                            x1s[half][:]).then_inc(sem["qSx%d" % half], 16)
            blk.scalar(_)
            C["qSx%d" % half] += 16
            x1_store[half] = C["qSx%d" % half]
        last_sq = C["cE"]

        for q in range(4):
            qs = slice(q * 512, (q + 1) * 512)

            def _(e, qs=qs, lncols=lncols, lx=last_xchain):
                e.wait_ge(sem["vE"], lncols)
                e.tensor_single_scalar(xc2[:], x1pre[:, qs], col("mean"),
                                       Alu.subtract).then_inc(sem["vE"], 1)
            blk.vector(_)
            C["vE"] += 1
            xcd = C["vE"]

            def _(e, xcd=xcd):
                e.wait_ge(sem["vE"], xcd)
                e.activation(rint_b[:], xc2[:], Act.Identity, bias=0.0,
                             scale=col("scale2")).then_inc(sem["cE"], 1)
            blk.scalar(_)
            C["cE"] += 1
            rqd = C["cE"]

            def _(e, rqd=rqd):
                e.wait_ge(sem["cE"], rqd)
                e.scalar_tensor_tensor(dbf_b[:], xc2[:], col("scale2"),
                                       rint_b[:], Alu.mult, Alu.subtract)
                e.drain()
                e.scalar_tensor_tensor(abf_b[:], dbf_b[:], -0.5, rint_b[:],
                                       Alu.is_lt, Alu.subtract)
                e.drain()
                e.scalar_tensor_tensor(rb16_b[:], dbf_b[:], 0.5, abf_b[:],
                                       Alu.is_gt, Alu.subtract)
                e.drain()
                e.tensor_scalar(xqt[:], rb16_b[:], 7.0, -8.0, Alu.min, Alu.max)
                e.drain().then_inc(sem["vE"], 1)
            blk.vector(_)
            C["vE"] += 1
            last_xchain = C["vE"]
            xqtd = C["vE"]

            vbase = C["vE"]

            def _(e, xqtd=xqtd, q=q, st=st, vbase=vbase):
                e.wait_ge(sem["vE"], xqtd)
                for k in range(4):
                    bank = k % 2
                    if k >= 2:
                        e.wait_ge(sem["vE"], vbase + k - 1)
                    elif pp_evac[bank]:
                        e.wait_ge(sem["vE"], pp_evac[bank])
                    e.transpose(TAB[bank][:, 0:128],
                                xqt[:, k * 128:(k + 1) * 128],
                                g_id16[:]).then_inc(sem["tM"], 1)
            blk.tensor(_)
            tb = C["tM"]
            C["tM"] += 4

            def _(e, tb=tb, q=q, st=st):
                for k in range(4):
                    bank = k % 2
                    dc = q * 4 + k
                    e.wait_ge(sem["tM"], tb + k + 1)
                    e.tensor_copy(
                        xq[:, dc * 2048 + st * 128:dc * 2048 + (st + 1) * 128],
                        TAB[bank][:, 0:128]).then_inc(sem["vE"], 1)
            blk.vector(_)
            for k in range(4):
                C["vE"] += 1
                pp_evac[k % 2] = C["vE"]

    amfinB = C["vE"]

    def _(e, amfinB=amfinB):
        e.wait_ge(sem["vE"], amfinB)
        e.transpose(P[4][0:1, 0:128], col("amax1"), g_idf[:])
        e.transpose(P[4][0:1, 128:256], col("amax2"),
                    g_idf[:]).then_inc(sem["tM"], 1)
    blk.tensor(_)
    C["tM"] += 1
    trm = C["tM"]

    def _(e, trm=trm):
        e.wait_ge(sem["tM"], trm)
        e.reduce_max(g_row2[0:1, 46:47], P[4][0:1, 0:128], AX)
        e.reduce_max(g_row2[0:1, 47:48], P[4][0:1, 128:256], AX)
        e.drain().then_inc(sem["vE"], 1)
    blk.vector(_)
    C["vE"] += 1
    amred = C["vE"]

    def _(e, amred=amred):
        e.wait_ge(sem["vE"], amred)
        e.matmul(P[4][:, 34:36], g_onesrow[:], g_row2[0:1, 46:48], start=True,
                 stop=True, skip_group_check=True).then_inc(sem["tM"], 1)
    blk.tensor(_)
    C["tM"] += 1
    sbc1 = C["tM"]

    def _(e, sbc1=sbc1):
        e.wait_ge(sem["tM"], sbc1)
        e.tensor_single_scalar(col("sw1"), P[4][:, 34:35], 1.0 / 7.0, Alu.mult)
        e.tensor_single_scalar(col("sw2"), P[4][:, 35:36], 1.0 / 7.0, Alu.mult)
        e.drain()
        e.reciprocal(col("inv1"), col("sw1"))
        e.reciprocal(col("inv2"), col("sw2"))
        e.tensor_tensor(col("c1"), col("s1"), col("sw1"), Alu.mult)
        e.tensor_tensor(col("c2"), col("s2"), col("sw2"), Alu.mult)
        e.drain()
        e.reciprocal(col("invc1"), col("c1"))
        e.tensor_tensor(col("k"), col("c1"), col("invs2"), Alu.mult)
        e.reciprocal(col("invc2"), col("c2"))
        e.drain().then_inc(sem["vE"], 1)
    blk.vector(_)
    C["vE"] += 1
    cols_done = C["vE"]

    def _(e, cols_done=cols_done):
        e.wait_ge(sem["vE"], cols_done)
        e.activation(rint_b[:, 0:FT], g_b1[:], Act.Identity, bias=0.0,
                     scale=col("invc1")).then_inc(sem["cE"], 1)
    blk.scalar(_)
    C["cE"] += 1
    m1 = C["cE"]

    def _(e, m1=m1):
        e.wait_ge(sem["cE"], m1)
        e.scalar_tensor_tensor(dbf_b[:, 0:FT], g_b1[:], col("invc1"),
                               rint_b[:, 0:FT], Alu.mult, Alu.subtract)
        e.drain()
        e.scalar_tensor_tensor(abf_b[:, 0:FT], dbf_b[:, 0:FT], -0.5,
                               rint_b[:, 0:FT], Alu.is_lt, Alu.subtract)
        e.drain()
        e.scalar_tensor_tensor(rb16_b[:, 0:FT], dbf_b[:, 0:FT], 0.5,
                               abf_b[:, 0:FT], Alu.is_gt, Alu.subtract)
        e.drain()
        e.tensor_single_scalar(g_b1k[:], rb16_b[:, 0:FT], col("k"), Alu.mult)
        e.drain().then_inc(sem["vE"], 1)
    blk.vector(_)
    C["vE"] += 1
    b1k_done = C["vE"]

    def _(e, b1k_done=b1k_done):
        e.wait_ge(sem["vE"], b1k_done)
        e.activation(rint_b[:, 0:16], g_b2[:], Act.Identity, bias=0.0,
                     scale=col("invc2")).then_inc(sem["cE"], 1)
    blk.scalar(_)
    C["cE"] += 1
    m2 = C["cE"]

    def _(e, m2=m2):
        e.wait_ge(sem["cE"], m2)
        e.scalar_tensor_tensor(dbf_b[:, 0:16], g_b2[:], col("invc2"),
                               rint_b[:, 0:16], Alu.mult, Alu.subtract)
        e.drain()
        e.scalar_tensor_tensor(abf_b[:, 0:16], dbf_b[:, 0:16], -0.5,
                               rint_b[:, 0:16], Alu.is_lt, Alu.subtract)
        e.drain()
        e.scalar_tensor_tensor(rb16_b[:, 0:16], dbf_b[:, 0:16], 0.5,
                               abf_b[:, 0:16], Alu.is_gt, Alu.subtract)
        e.drain()
        e.tensor_single_scalar(g_b2q[:], rb16_b[:, 0:16], col("c2"), Alu.mult)
        e.drain().then_inc(sem["vE"], 1)
    blk.vector(_)
    C["vE"] += 1
    bias_prep = C["vE"]


    b_x1_stores = (C["qSx0"], C["qSx1"])
    barrier()
    es_B.close()
    es_am.close()

    es_C = contextlib.ExitStack()

    def sbc(name, shape, dt):
        return es_C.enter_context(nc.sbuf_tensor(name, shape, dt))

    wstc = [sbc("wstc0", [128, 2048], F32), sbc("wstc1", [128, 2048], F32)]
    hq = sbc("hq", [128, FT * 1024], FP8)
    ffn_tm = sbc("ffn_tm", [128, 8 * 2048], BF16)
    w1q = [sbc("w1q0", [128, 2048], FP8), sbc("w1q1", [128, 2048], FP8)]
    w2q = [sbc("w2q0", [128, 8192], FP8), sbc("w2q1", [128, 8192], FP8)]
    ubuf = sbc("ubuf", [128, 1024], F32)
    rint_h = sbc("rint_h", [128, 1024], I16)
    rintW = sbc("rintW", [128, 2048], I8)
    dbf_h = sbc("dbf_h", [128, 1024], BF16)
    abf_h = sbc("abf_h", [128, 1024], BF16)
    pbf_h = sbc("pbf_h", [128, 1024], BF16)
    o2b = sbc("o2b", [128, 1024], BF16)
    x2pre = sbc("x2pre", [128, 2048], F32)
    outh = [sbc("outh0", [128, 1024], F32), sbc("outh1", [128, 1024], F32)]

    wst_free = [("vE", C["vE"]), ("vE", C["vE"])]
    wst_ld = [("qW0", C["qW0"]), ("qW1", C["qW1"])]

    def load_wstc(slot, src_ap):
        qn = "qW%d" % slot
        def _(e, slot=slot, src_ap=src_ap, fr=wst_free[slot], qn=qn):
            if fr[1]:
                e.wait_ge(sem[fr[0]], fr[1])
            e.dma_start(wstc[slot][:], src_ap).then_inc(sem[qn], 16)
        blk.sync(_)
        C[qn] += 16
        wst_ld[slot] = (qn, C[qn])

    rintW_state = [0]

    def nonlocal_set_rintW(v):
        rintW_state[0] = v
    hq_end = 0
    hq_rint_rd = 0
    wq_wr = {}

    def w_cast_block(slot, src_w, dma_wait):
        def _(e, slot=slot, src_w=src_w, dma_wait=dma_wait, rr=rintW_state[0]):
            e.wait_ge(sem[dma_wait[0]], dma_wait[1])
            if rr:
                e.wait_ge(sem["vE"], rr)
            e.activation(rintW[:], wstc[slot][:], Act.Identity, bias=0.0,
                         scale=src_w).then_inc(sem["cE"], 1)
        blk.scalar(_)
        C["cE"] += 1
        return C["cE"]

    def w_chain_block(slot, dstbuf, dst_slot, invcol, cast_mark, extra=(),
                      dst_off=0):
        vb = C["vE"]

        def _(e, slot=slot, dstbuf=dstbuf, dst_slot=dst_slot, invcol=invcol,
              cast_mark=cast_mark, vb=vb, he=hq_end, extra=tuple(extra),
              dst_off=dst_off):
            e.wait_ge(sem["cE"], cast_mark)
            if he:
                e.wait_ge(sem["vE"], he)
            for sn, sv in extra:
                e.wait_ge(sem[sn], sv)
            for part in range(2):
                ps = slice(part * 1024, (part + 1) * 1024)
                ds = slice(dst_off + part * 1024, dst_off + (part + 1) * 1024)
                if part:
                    e.wait_ge(sem["vE"], vb + 3)
                e.scalar_tensor_tensor(dbf_h[:], wstc[slot][:, ps], invcol,
                                       rintW[:, ps], Alu.mult,
                                       Alu.subtract).then_inc(sem["vE"], 1)
                e.wait_ge(sem["vE"], vb + part * 3 + 1)
                e.scalar_tensor_tensor(abf_h[:], dbf_h[:], -0.5, rintW[:, ps],
                                       Alu.is_lt,
                                       Alu.subtract).then_inc(sem["vE"], 1)
                e.wait_ge(sem["vE"], vb + part * 3 + 2)
                e.scalar_tensor_tensor(dstbuf[dst_slot][:, ds], dbf_h[:], 0.5,
                                       abf_h[:], Alu.is_gt,
                                       Alu.subtract).then_inc(sem["vE"], 1)
        blk.vector(_)
        C["vE"] += 6
        return C["vE"]

    w1q_free = [0, 0]
    w1q_store = [0, 0]
    w2q_free = [0, 0]
    w2q_store = [0, 0]
    hq_psum_free = 0
    qS_half0_end = []
    h2_end_tm = [0]
    x0_read_c = C["vE"]
    out_store = [0, 0]

    LN2 = {"x0r": x0_read_c, "lastsq": 0, "ffn": 0}

    def emit_ln2_st(st):
        def _(e, st=st, xr=LN2["x0r"], bx=b_x1_stores):
            e.wait_ge(sem["qSx0"], bx[0])
            e.wait_ge(sem["qSx1"], bx[1])
            e.wait_ge(sem["vE"], xr)
            e.dma_start(g_x0st[:],
                        x1scr[st * 128:(st + 1) * 128, :]).then_inc(
                sem["qX"], 16)
        blk.sync(_)
        C["qX"] += 16
        ldx1 = C["qX"]
        stl8 = st % 8

        def _(e, stl8=stl8, ldx1=ldx1, ls=LN2["lastsq"]):
            e.wait_ge(sem["qX"], ldx1)
            if ls:
                e.wait_ge(sem["cE"], ls)
            e.tensor_tensor(x2pre[:], g_x0st[:],
                            ffn_tm[:, stl8 * 2048:(stl8 + 1) * 2048],
                            Alu.add)
            e.drain().then_inc(sem["vE"], 1)
            e.reduce_sum(col("zt0"), x2pre[:], AX)
            e.drain().then_inc(sem["vE"], 1)
        blk.vector(_)
        C["vE"] += 2
        LN2["x0r"] = C["vE"] - 1
        LN2["ffn"] = C["vE"] - 1
        x2d = C["vE"]

        def _(e, x2d=x2d, os0=out_store[0], os1=out_store[1]):
            e.wait_ge(sem["vE"], x2d)
            if os0:
                e.wait_ge(sem["qSo0"], os0)
            if os1:
                e.wait_ge(sem["qSo1"], os1)
            e.activation(outh[0][:], x2pre[:, 0:1024], Act.Square,
                         accum_out=col("zt1"))
            e.activation(outh[1][:], x2pre[:, 1024:2048], Act.Square,
                         accum_out=col("zt2"))
            e.drain().then_inc(sem["cE"], 1)
        blk.scalar(_)
        C["cE"] += 1
        sq2 = C["cE"]
        outh_sq = C["cE"]

        vb2 = C["vE"]

        def _(e, sq2=sq2, vb2=vb2):
            e.wait_ge(sem["cE"], sq2)
            e.tensor_tensor(col("ssq"), col("zt1"), col("zt2"),
                            Alu.add).then_inc(sem["vE"], 1)
            e.tensor_single_scalar(col("mean"), col("zt0"), INV_D,
                                   Alu.mult).then_inc(sem["vE"], 1)
            e.wait_ge(sem["vE"], vb2 + 2)
            e.tensor_tensor(col("msq"), col("mean"), col("mean"),
                            Alu.mult).then_inc(sem["vE"], 1)
            e.wait_ge(sem["vE"], vb2 + 3)
            e.scalar_tensor_tensor(col("var"), col("ssq"), INV_D,
                                   col("msq"), Alu.mult,
                                   Alu.subtract).then_inc(sem["vE"], 1)
            e.wait_ge(sem["vE"], vb2 + 4)
            e.tensor_single_scalar(col("var"), col("var"), LNEPS,
                                   Alu.add).then_inc(sem["vE"], 1)
        blk.vector(_)
        C["vE"] += 5
        stat2 = C["vE"]

        cb2 = C["cE"]

        def _(e, stat2=stat2, cb2=cb2):
            e.wait_ge(sem["vE"], stat2)
            e.activation(col("tmp"), col("var"), Act.Ln).then_inc(
                sem["cE"], 1)
            e.wait_ge(sem["cE"], cb2 + 1)
            e.activation(col("rstd"), col("tmp"), Act.Exp, bias=0.0,
                         scale=-0.5).then_inc(sem["cE"], 1)
        blk.scalar(_)
        C["cE"] += 2
        r2d = C["cE"]

        vb3 = C["vE"]

        def _(e, r2d=r2d, vb3=vb3):
            e.wait_ge(sem["cE"], r2d)
            e.tensor_single_scalar(col("nmr"), col("mean"), -1.0,
                                   Alu.mult).then_inc(sem["vE"], 1)
            e.wait_ge(sem["vE"], vb3 + 1)
            e.tensor_tensor(col("nmr"), col("nmr"), col("rstd"),
                            Alu.mult).then_inc(sem["vE"], 1)
        blk.vector(_)
        C["vE"] += 2
        nm2 = C["vE"]

        for half in range(2):
            def _(e, nm2=nm2, half=half, os=out_store[half], oq=outh_sq):
                e.wait_ge(sem["vE"], nm2)
                e.wait_ge(sem["cE"], oq)
                e.activation(outh[half][:],
                             x2pre[:, half * 1024:(half + 1) * 1024],
                             Act.Identity, bias=col("nmr"),
                             scale=col("rstd")).then_inc(sem["cE"], 1)
            blk.scalar(_)
            C["cE"] += 1
            oa = C["cE"]

            def _(e, st=st, half=half, oa=oa):
                e.wait_ge(sem["cE"], oa)
                e.dma_start(y[st * 128:(st + 1) * 128,
                              half * 1024:(half + 1) * 1024],
                            outh[half][:]).then_inc(sem["qSo%d" % half], 16)
            blk.scalar(_)
            C["qSo%d" % half] += 16
            out_store[half] = C["qSo%d" % half]
        LN2["lastsq"] = C["cE"]

    for shalf in range(2):
        tok0 = shalf * 1024
        if shalf == 0:
            load_wstc(0, W1h[:, 0, :, :])
            cm = w_cast_block(0, col("inv1"), wst_ld[0])
            wr = w_chain_block(0, w1q, 0, col("inv1"), cm,
                               extra=([("tM", w1q_free[0])]
                                      if w1q_free[0] else []))
            wq_wr[("w1", 0)] = wr
            nonlocal_set_rintW(wr - 1)
            wst_free[0] = ("vE", wr)

            def _(e, wr=wr):
                e.wait_ge(sem["vE"], wr)
                e.dma_start(w1qscr[0, :, :], w1q[0][:]).then_inc(
                    sem["qSa0"], 16)
            blk.scalar(_)
            C["qSa0"] += 16
            w1q_store[0] = C["qSa0"]

        for ft in range(FT):
            slot = ft % 2
            nslot = (ft + 1) % 2
            if shalf == 0:
                if ft + 1 < FT:
                    load_wstc(nslot, W1h[:, ft + 1, :, :])
                    cm = w_cast_block(nslot, col("inv1"), wst_ld[nslot])
                    wr = w_chain_block(
                        nslot, w1q, nslot, col("inv1"), cm,
                        extra=([("tM", w1q_free[nslot]),
                                ("qSa%d" % nslot, w1q_store[nslot])]
                               if w1q_free[nslot] else []))
                    wq_wr[("w1", nslot)] = wr
                    nonlocal_set_rintW(wr - 1)
                    wst_free[nslot] = ("vE", wr)

                    def _(e, wr=wr, nslot=nslot, ft=ft):
                        e.wait_ge(sem["vE"], wr)
                        e.dma_start(w1qscr[ft + 1, :, :],
                                    w1q[nslot][:]).then_inc(
                            sem["qSa%d" % nslot], 16)
                    blk.scalar(_)
                    C["qSa%d" % nslot] += 16
                    w1q_store[nslot] = C["qSa%d" % nslot]
                w1q_rdy = ("vE", wq_wr[("w1", slot)])
            else:
                def _(e, slot=slot, ft=ft, fr=w1q_free[slot],
                      q0=qS_half0_end):
                    if ft < 2:
                        for qn, qv in q0:
                            e.wait_ge(sem[qn], qv)
                    if fr:
                        e.wait_ge(sem["tM"], fr)
                    e.dma_start(w1q[slot][:], w1qscr[ft, :, :]).then_inc(
                        sem["qF%d" % slot], 16)
                blk.sync(_)
                C["qF%d" % slot] += 16
                w1q_rdy = ("qF%d" % slot, C["qF%d" % slot])

            grp = (ft % 2) * 2

            def _(e, ft=ft, slot=slot, grp=grp, tok0=tok0, rdy=w1q_rdy,
                  pf=hq_psum_free):
                e.wait_ge(sem[rdy[0]], rdy[1])
                if pf:
                    e.wait_ge(sem["cE"], pf)
                for dc in range(DC):
                    for sb2 in range(2):
                        mm = e.matmul(
                            P[grp + sb2][:],
                            w1q[slot][:, dc * 128:(dc + 1) * 128],
                            xq[:, dc * 2048 + tok0 + sb2 * 512:
                               dc * 2048 + tok0 + (sb2 + 1) * 512],
                            start=(dc == 0), stop=(dc == DC - 1))
                        if dc == DC - 1:
                            mm.then_inc(sem["tM"], 1)
            blk.tensor(_)
            C["tM"] += 2
            h1mm = C["tM"]
            w1q_free[slot] = C["tM"]

            def _(e, ft=ft, grp=grp, h1mm=h1mm, he=hq_end, hr=hq_rint_rd):
                if he:
                    e.wait_ge(sem["vE"], he)
                e.wait_ge(sem["tM"], h1mm - 1)
                e.activation(ubuf[:, 0:512], P[grp][:], Act.Relu,
                             bias=g_b1k[:, ft:ft + 1], scale=col("k"))
                e.wait_ge(sem["tM"], h1mm)
                e.activation(ubuf[:, 512:1024], P[grp + 1][:], Act.Relu,
                             bias=g_b1k[:, ft:ft + 1], scale=col("k"))
                e.drain()
                e.activation(rint_h[:], ubuf[:], Act.Identity)
                e.drain().then_inc(sem["cE"], 1)
            blk.scalar(_)
            C["cE"] += 1
            hqu = C["cE"]
            hq_psum_free = C["cE"]

            vb = C["vE"]

            def _(e, hqu=hqu, ft=ft, vb=vb, he=hq_end,
                  h2e=(h2_end_tm[0] if (shalf == 1 and ft == 0) else 0)):
                e.wait_ge(sem["cE"], hqu)
                if h2e:
                    e.wait_ge(sem["tM"], h2e)
                if he:
                    e.wait_ge(sem["vE"], he)
                e.tensor_tensor(dbf_h[:], ubuf[:], rint_h[:],
                                Alu.subtract).then_inc(sem["vE"], 1)
                e.wait_ge(sem["vE"], vb + 1)
                e.scalar_tensor_tensor(o2b[:], dbf_h[:], -0.5, rint_h[:],
                                       Alu.is_lt,
                                       Alu.subtract).then_inc(sem["vE"], 1)
                e.tensor_single_scalar(pbf_h[:], dbf_h[:], 0.5,
                                       Alu.is_gt).then_inc(sem["vE"], 1)
                e.wait_ge(sem["vE"], vb + 3)
                e.scalar_tensor_tensor(abf_h[:], dbf_h[:], -1.5, o2b[:],
                                       Alu.is_lt, Alu.add).then_inc(
                    sem["vE"], 1)
                e.scalar_tensor_tensor(ubuf[:], dbf_h[:], 1.5,
                                       pbf_h[:], Alu.is_gt, Alu.add).then_inc(
                    sem["vE"], 1)
                e.wait_ge(sem["vE"], vb + 5)
                e.scalar_tensor_tensor(o2b[:], dbf_h[:], -2.5, abf_h[:],
                                       Alu.is_lt, Alu.add).then_inc(
                    sem["vE"], 1)
                e.scalar_tensor_tensor(pbf_h[:], dbf_h[:], 2.5, ubuf[:],
                                       Alu.is_gt, Alu.add).then_inc(
                    sem["vE"], 1)
                e.wait_ge(sem["vE"], vb + 7)
                e.tensor_tensor(abf_h[:], pbf_h[:], o2b[:],
                                Alu.subtract).then_inc(sem["vE"], 1)
                e.wait_ge(sem["vE"], vb + 8)
                e.tensor_scalar(hq[:, ft * 1024:(ft + 1) * 1024], abf_h[:],
                                15.0, 0.0, Alu.min, Alu.max).then_inc(
                    sem["vE"], 1)
            blk.vector(_)
            C["vE"] += 9
            hq_end = C["vE"]
            hq_rint_rd = vb + 2
            if shalf == 1 and ft % 8 == 7:
                emit_ln2_st(ft // 8)
        h1_end_v = C["vE"]

        def quant_w2_tile(nslot, dt2):
            global_rd = 0
            for sub in range(4):
                lslot = nslot
                load_wstc(lslot, W2h[:, dt2, sub * 16:(sub + 1) * 16, :])
                cm = w_cast_block(lslot, col("inv2"), wst_ld[lslot])
                wr = w_chain_block(
                    lslot, w2q, nslot, col("inv2"), cm,
                    extra=([("tM", w2q_free[nslot]),
                            ("qSb%d" % nslot, w2q_store[nslot])]
                           if (sub == 0 and w2q_free[nslot]) else []),
                    dst_off=sub * 2048)
                wst_free[lslot] = ("vE", wr)
                global_rd = wr
                nonlocal_set_rintW(wr - 1)
            wq_wr[("w2", nslot)] = global_rd

            def _(e, wr=global_rd, nslot=nslot, dt2=dt2):
                e.wait_ge(sem["vE"], wr)
                e.dma_start(w2qscr[dt2, :, :], w2q[nslot][:]).then_inc(
                    sem["qSb%d" % nslot], 16)
            blk.scalar(_)
            C["qSb%d" % nslot] += 16
            w2q_store[nslot] = C["qSb%d" % nslot]

        if shalf == 0:
            quant_w2_tile(0, 0)
        for dt in range(16):
            slot = dt % 2
            if shalf == 0:
                if dt + 1 < 16:
                    quant_w2_tile((dt + 1) % 2, dt + 1)
                w2q_rdy = ("vE", wq_wr[("w2", slot)])
            else:
                def _(e, slot=slot, dt=dt, fr=w2q_free[slot],
                      q0=qS_half0_end):
                    if dt < 2:
                        for qn, qv in q0:
                            e.wait_ge(sem[qn], qv)
                    if fr:
                        e.wait_ge(sem["tM"], fr)
                    e.dma_start(w2q[slot][:], w2qscr[dt, :, :]).then_inc(
                        sem["qG%d" % slot], 16)
                blk.sync(_)
                C["qG%d" % slot] += 16
                w2q_rdy = ("qG%d" % slot, C["qG%d" % slot])

            grp = (dt % 2) * 2

            def _(e, dt=dt, slot=slot, grp=grp, rdy=w2q_rdy, hv=h1_end_v):
                e.wait_ge(sem[rdy[0]], rdy[1])
                e.wait_ge(sem["vE"], hv)
                for fc in range(FT):
                    for sb2 in range(2):
                        mm = e.matmul(
                            P[grp + sb2][:],
                            w2q[slot][:, fc * 128:(fc + 1) * 128],
                            hq[:, fc * 1024 + sb2 * 512:
                               fc * 1024 + (sb2 + 1) * 512],
                            start=(fc == 0), stop=(fc == FT - 1))
                        if fc == FT - 1:
                            mm.then_inc(sem["tM"], 1)
            blk.tensor(_)
            C["tM"] += 2
            h2mm = C["tM"]
            w2q_free[slot] = C["tM"]

            def _(e, dt=dt, grp=grp, h2mm=h2mm):
                e.wait_ge(sem["tM"], h2mm - 1)
                e.activation(o2b[:, 0:512], P[grp][:], Act.Identity,
                             bias=g_b2q[:, dt:dt + 1], scale=col("c2"))
                e.wait_ge(sem["tM"], h2mm)
                e.activation(o2b[:, 512:1024], P[grp + 1][:], Act.Identity,
                             bias=g_b2q[:, dt:dt + 1], scale=col("c2"))
                e.drain().then_inc(sem["cE"], 1)
            blk.scalar(_)
            C["cE"] += 1
            o2d = C["cE"]

            vbase2 = C["vE"]

            def _(e, o2d=o2d, vbase2=vbase2):
                e.wait_ge(sem["cE"], o2d)
                for k in range(8):
                    bank = k % 2
                    if k >= 2:
                        e.wait_ge(sem["vE"], vbase2 + k - 1)
                    elif pp_evac[bank]:
                        e.wait_ge(sem["vE"], pp_evac[bank])
                    e.transpose(TAB[bank][:, 0:128],
                                o2b[:, k * 128:(k + 1) * 128],
                                g_id16[:]).then_inc(sem["tM"], 1)
            blk.tensor(_)
            tb = C["tM"]
            C["tM"] += 8

            def _(e, tb=tb, dt=dt, lf=(LN2["ffn"] if dt == 0 else 0)):
                if lf:
                    e.wait_ge(sem["vE"], lf)
                for k in range(8):
                    bank = k % 2
                    e.wait_ge(sem["tM"], tb + k + 1)
                    e.tensor_copy(
                        ffn_tm[:, k * 2048 + dt * 128:k * 2048 + (dt + 1) * 128],
                        TAB[bank][:, 0:128]).then_inc(sem["vE"], 1)
            blk.vector(_)
            for k in range(8):
                C["vE"] += 1
                pp_evac[k % 2] = C["vE"]

        if shalf == 1:
            for stl in range(8):
                emit_ln2_st(8 + stl)

        if shalf == 0:
            qS_half0_end = [("qSa0", C["qSa0"]), ("qSa1", C["qSa1"]),
                            ("qSb0", C["qSb0"]), ("qSb1", C["qSb1"])]
            h2_end_tm[0] = C["tM"]

    def _(e, t0=C["qSo0"], t1=C["qSo1"]):
        e.wait_ge(sem["qSo0"], t0)
        e.wait_ge(sem["qSo1"], t1)
    blk.scalar(_)

    blk_es.close()
    es_C.close()
    es_xq.close()
    es_p.close()
    es_g.close()
    es_s.close()
    return nc


def _host_layout(inputs):
    import ml_dtypes
    f32 = np.float32

    def c(a):
        return np.ascontiguousarray(a, dtype=f32)

    d = dict(
        wq=c(inputs["L0_attn_w_q"].T.reshape(DC, 128, H, 128)
             .transpose(1, 2, 0, 3)),
        wk=c(inputs["L0_attn_w_k"].T.reshape(DC, 128, H, 128)
             .transpose(1, 2, 0, 3)),
        wv=c(inputs["L0_attn_w_v"].T.reshape(DC, 128, H, 128)
             .transpose(1, 2, 0, 3)),
        wo=c(inputs["L0_attn_w_o"].T.reshape(DC, 128, D).transpose(1, 0, 2)),
        W1h=c(inputs["L0_ffn_W1"].T.reshape(DC, 128, FT, 128)
              .transpose(1, 2, 0, 3)),
        W2h=c(inputs["L0_ffn_W2"].T.reshape(64, 128, 16, 128)
              .transpose(1, 2, 0, 3)),
        memh=c(inputs["L0_attn_memories"].transpose(1, 2, 0, 3)
               .reshape(H, 128, M * 128)),
        mnh=c(inputs["L0_attn_memory_norms"].transpose(2, 1, 0)
              .reshape(128, H * M)),
        gater=c(inputs["L0_attn_gate"].reshape(1, H)),
        b1h=c(np.asarray(inputs["L0_ffn_b1"]).reshape(FT, 128).T),
        b2h=c(np.asarray(inputs["L0_ffn_b2"]).reshape(16, 128).T),
        s1t=c(np.asarray(inputs["L0_ffn_s_act1"]).reshape(1, 1)),
        s2t=c(np.asarray(inputs["L0_ffn_s_act2"]).reshape(1, 1)),
        maskT=np.triu(np.ones((128, 128), dtype=f32)),
        id16=np.eye(128, dtype=f32).astype(ml_dtypes.bfloat16),
        onescol=np.ones((128, 1), dtype=f32),
        ones16=np.ones((128, 1), dtype=f32).astype(ml_dtypes.bfloat16),
        onesrow=np.ones((1, 128), dtype=f32),
    )
    return d


def kernel(**inputs):
    shared = _host_layout(inputs)
    x = np.ascontiguousarray(inputs["L0_attn_hidden_states"], dtype=np.float32)
    nc = build_nc()
    in_maps = []
    for b in range(8):
        m = dict(shared)
        m["x0"] = x[b]
        in_maps.append(m)
    res = run_bass_kernel_spmd(nc, in_maps, core_ids=list(range(8)))
    out = np.stack([res.results[b]["y"] for b in range(8)], axis=0)
    return out.astype(np.float32, copy=False)


# version 17
# speedup vs baseline: 1.0019x; 1.0019x over previous
import contextlib

import numpy as np

import concourse.bass as bass
import concourse.mybir as mybir
from concourse.bass_utils import run_bass_kernel_spmd

F32 = mybir.dt.float32
BF16 = mybir.dt.bfloat16
FP8 = mybir.dt.float8e4
I32 = mybir.dt.int32
I8 = mybir.dt.int8
I16 = mybir.dt.int16
Alu = mybir.AluOpType
Act = mybir.ActivationFunctionType
AX = mybir.AxisListType.X

S = 2048
D = 2048
H = 16
HD = 128
NCH = 16
F = 8192
M = 4
DC = D // 128
FT = F // 128
ST = S // 128
SCALE = float(HD) ** -0.5
EPS = 1e-6
LNEPS = 1e-5
INV_D = 1.0 / D


def build_nc():
    nc = bass.Bass("TRN2", target_bir_lowering=False)

    x0 = nc.dram_tensor("x0", [S, D], F32, kind="ExternalInput")
    wq = nc.dram_tensor("wq", [128, H, DC, 128], F32, kind="ExternalInput")
    wk = nc.dram_tensor("wk", [128, H, DC, 128], F32, kind="ExternalInput")
    wv = nc.dram_tensor("wv", [128, H, DC, 128], F32, kind="ExternalInput")
    wo = nc.dram_tensor("wo", [128, DC, D], F32, kind="ExternalInput")
    W1h = nc.dram_tensor("W1h", [128, FT, DC, 128], F32, kind="ExternalInput")
    W2h = nc.dram_tensor("W2h", [128, 16, 64, 128], F32, kind="ExternalInput")
    memh = nc.dram_tensor("memh", [H, 128, M * 128], F32, kind="ExternalInput")
    mnh = nc.dram_tensor("mnh", [128, H * M], F32, kind="ExternalInput")
    gater = nc.dram_tensor("gater", [1, H], F32, kind="ExternalInput")
    b1h = nc.dram_tensor("b1h", [128, FT], F32, kind="ExternalInput")
    b2h = nc.dram_tensor("b2h", [128, 16], F32, kind="ExternalInput")
    s1t = nc.dram_tensor("s1t", [1, 1], F32, kind="ExternalInput")
    s2t = nc.dram_tensor("s2t", [1, 1], F32, kind="ExternalInput")
    maskT = nc.dram_tensor("maskT", [128, 128], F32, kind="ExternalInput")
    id16 = nc.dram_tensor("id16", [128, 128], BF16, kind="ExternalInput")
    onescol = nc.dram_tensor("onescol", [128, 1], F32, kind="ExternalInput")
    ones16 = nc.dram_tensor("ones16", [128, 1], BF16, kind="ExternalInput")
    onesrow = nc.dram_tensor("onesrow", [1, 128], F32, kind="ExternalInput")

    y = nc.dram_tensor("y", [S, D], F32, kind="ExternalOutput")
    x1scr = nc.dram_tensor("x1scr", [S, D], F32, kind="Internal")
    w1qscr = nc.dram_tensor("w1qscr", [FT, 128, DC * 128], FP8, kind="Internal")
    w2qscr = nc.dram_tensor("w2qscr", [16, 128, 64 * 128], FP8, kind="Internal")

    es_g = contextlib.ExitStack()

    def sbg(name, shape, dt):
        return es_g.enter_context(nc.sbuf_tensor(name, shape, dt))

    g_mask = sbg("g_mask", [128, 128], F32)
    g_id16 = sbg("g_id16", [128, 128], BF16)
    g_ones16 = sbg("g_ones16", [128, 1], BF16)
    g_onescol = sbg("g_onescol", [128, 1], F32)
    g_onesrow = sbg("g_onesrow", [1, 128], F32)
    g_mn = sbg("g_mn", [128, H * M], F32)
    g_mnb = sbg("g_mnb", [128, H * M], BF16)
    g_b1 = sbg("g_b1", [128, FT], F32)
    g_b2 = sbg("g_b2", [128, 16], F32)
    g_b1k = sbg("g_b1k", [128, FT], F32)
    g_b2q = sbg("g_b2q", [128, 16], F32)
    g_row = sbg("g_row", [1, 64], F32)
    g_row2 = sbg("g_row2", [1, 64], F32)
    g_grow = sbg("g_grow", [1, H], F32)
    g_gbar = sbg("g_gbar", [1, H], F32)
    g_em30 = sbg("g_em30", [1, M], F32)
    g_emc = sbg("g_emc", [1, M], F32)
    g_cols = sbg("g_cols", [128, 38], F32)
    g_x0st = sbg("g_x0st", [128, 2048], F32)
    g_idf = sbg("g_idf", [128, 128], F32)

    CN = {n: i for i, n in enumerate(
        ["amax1", "amax2", "sw1", "sw2", "inv1", "inv2", "s1", "s2",
         "invs1", "invs2", "c1", "invc1", "k", "c2", "invc2", "qsum", "zf",
         "mean", "ssq", "msq", "var", "rstd", "nmr", "scale2", "tmp",
         "zt0", "zt1", "zt2",
         "meanB", "ssqB", "msqB", "varB", "rstdB", "nmrB", "tmpB",
         "zt0B", "zt1B", "zt2B"])}

    def col(name):
        i = CN[name]
        return g_cols[:, i:i + 1]

    es_p = contextlib.ExitStack()
    P = [es_p.enter_context(nc.psum_tensor("pb%d" % i, [128, 512], F32))
         for i in range(6)]
    TA = es_p.enter_context(nc.psum_tensor("pta", [128, 1024], BF16))
    TB = es_p.enter_context(nc.psum_tensor("ptb", [128, 1024], BF16))
    TAB = [TA, TB]

    es_s = contextlib.ExitStack()
    sem = {}
    for name in ["qC", "qW0", "qW1", "qX", "qF0", "qF1", "qG0", "qG1",
                 "qSx0", "qSx1", "qSo0", "qSo1", "qSa0", "qSa1", "qSb0",
                 "qSb1", "tM", "vE", "cE", "gE", "bar"]:
        sem[name] = es_s.enter_context(nc.semaphore("sem_" + name))
    C = {k: 0 for k in sem}

    blk_es = contextlib.ExitStack()
    blk = blk_es.enter_context(nc.Block())

    DMA_SEMS = ["qC", "qW0", "qW1", "qX", "qF0", "qF1", "qG0", "qG1",
                "qSx0", "qSx1", "qSo0", "qSo1", "qSa0", "qSa1", "qSb0",
                "qSb1"]

    def barrier():
        waits = [(q, C[q]) for q in DMA_SEMS if C[q]]

        def _w(e, waits=tuple(waits)):
            for q, v in waits:
                e.wait_ge(sem[q], v)
        blk.sync(_w)
        C["bar"] += 5
        target = C["bar"]
        for en in ["sync", "scalar", "vector", "tensor", "gpsimd"]:
            def _(e, target=target):
                e.drain().then_inc(sem["bar"], 1)
                e.wait_ge(sem["bar"], target)
            getattr(blk, en)(_)

    es_am = contextlib.ExitStack()
    amix = es_am.enter_context(nc.sbuf_tensor("amix", [128, H * 2048], BF16))
    es_xfm = contextlib.ExitStack()
    xfm = es_xfm.enter_context(nc.sbuf_tensor("xfm", [128, DC * 2048], BF16))
    es_p0 = contextlib.ExitStack()
    x0b = [es_p0.enter_context(nc.sbuf_tensor("x0b%d" % i, [128, 2048], BF16))
           for i in range(2)]

    def _(e):
        for t, src in [(g_mask, maskT), (g_id16, id16), (g_ones16, ones16),
                       (g_onescol, onescol), (g_onesrow, onesrow),
                       (g_mn, mnh), (g_b1, b1h), (g_b2, b2h),
                       (g_grow, gater)]:
            e.dma_start(t[:], src[:]).then_inc(sem["qC"], 16)
        e.dma_start(g_row2[0:1, 44:45], s1t[:]).then_inc(sem["qC"], 16)
        e.dma_start(g_row2[0:1, 45:46], s2t[:]).then_inc(sem["qC"], 16)
    blk.sync(_)
    C["qC"] += 16 * 11
    consts_ld = C["qC"]

    def _(e):
        e.wait_ge(sem["qC"], consts_ld)
        e.activation(g_row2[0:1, 0:H], g_grow[0:1, :], Act.Exp, bias=0.0,
                     scale=-1.0).then_inc(sem["cE"], 1)
    blk.scalar(_)
    C["cE"] += 1
    gexp = C["cE"]

    def _(e, gexp=gexp):
        e.wait_ge(sem["cE"], gexp)
        e.tensor_single_scalar(g_row2[0:1, 16:16 + H], g_row2[0:1, 0:H], 1.0,
                               Alu.add)
        e.drain()
        e.reciprocal(g_grow[0:1, :], g_row2[0:1, 16:16 + H])
        e.drain()
        e.tensor_scalar(g_gbar[0:1, :], g_grow[0:1, :], -1.0, 1.0, Alu.mult,
                        Alu.add)
        e.tensor_copy(g_mnb[:], g_mn[:])
        e.tensor_copy(g_idf[:], g_id16[:])
        e.drain().then_inc(sem["vE"], 1)
    blk.vector(_)
    C["vE"] += 1

    def _(e):
        e.wait_ge(sem["qC"], consts_ld)
        e.matmul(P[5][0:1, 0:64], g_onescol[:], g_mn[:], start=True,
                 stop=True).then_inc(sem["tM"], 1)
    blk.tensor(_)
    C["tM"] += 1
    em_mm = C["tM"]

    def _(e, em_mm=em_mm):
        e.wait_ge(sem["tM"], em_mm)
        e.tensor_copy(g_row[0:1, 0:64], P[5][0:1, 0:64])
        e.drain()
        e.tensor_reduce(g_row2[0:1, 32:32 + M],
                        g_row[0:1, 0:64].rearrange("p (h m) -> p m h", m=M),
                        AX, Alu.add)
        e.drain()
        e.tensor_single_scalar(g_row2[0:1, 36:36 + M], g_row2[0:1, 32:32 + M],
                               1e-6, Alu.is_lt)
        e.drain()
        e.tensor_single_scalar(g_em30[0:1, :], g_row2[0:1, 36:36 + M], 1e30,
                               Alu.mult)
        e.tensor_scalar(g_emc[0:1, :], g_row2[0:1, 36:36 + M], -1.0, 1.0,
                        Alu.mult, Alu.add)
        e.drain().then_inc(sem["vE"], 1)
    blk.vector(_)
    C["vE"] += 1

    x0cast = [0] * ST
    for st in range(ST):
        def _(e, st=st):
            if st >= 1:
                e.wait_ge(sem["vE"], x0cast[st - 1])
            e.dma_start(g_x0st[:], x0[st * 128:(st + 1) * 128, :]).then_inc(
                sem["qX"], 16)
        blk.sync(_)
        C["qX"] += 16
        ld = C["qX"]

        def _(e, st=st, ld=ld):
            e.wait_ge(sem["qX"], ld)
            e.tensor_copy(x0b[st % 2][:], g_x0st[:]).then_inc(sem["vE"], 1)
        blk.vector(_)
        C["vE"] += 1
        x0cast[st] = C["vE"]

        vpre = C["vE"]

        def _(e, st=st, vpre=vpre):
            e.wait_ge(sem["vE"], x0cast[st])
            for dc in range(DC):
                if dc >= 2:
                    e.wait_ge(sem["vE"], vpre + dc - 1)
                e.transpose(TAB[dc % 2][:, 0:128],
                            x0b[st % 2][:, dc * 128:(dc + 1) * 128],
                            g_id16[:]).then_inc(sem["tM"], 1)
        blk.tensor(_)
        tb = C["tM"]
        C["tM"] += DC

        def _(e, st=st, tb=tb):
            for dc in range(DC):
                e.wait_ge(sem["tM"], tb + dc + 1)
                e.tensor_copy(
                    xfm[:, dc * 2048 + st * 128: dc * 2048 + (st + 1) * 128],
                    TAB[dc % 2][:, 0:128]).then_inc(sem["vE"], 1)
        blk.vector(_)
        C["vE"] += DC

    barrier()
    es_p0.close()

    es_A = contextlib.ExitStack()

    def sba(name, shape, dt):
        return es_A.enter_context(nc.sbuf_tensor(name, shape, dt))

    wst = [sba("wst0", [128, 2048], F32), sba("wst1", [128, 2048], F32)]
    qp = [sba("qp0", [128, 2048], BF16), sba("qp1", [128, 2048], BF16)]
    kp = [sba("kp0", [128, 2048], BF16), sba("kp1", [128, 2048], BF16)]
    vf = [sba("vf0", [128, 2048], BF16), sba("vf1", [128, 2048], BF16)]
    wb = [sba("wb0", [128, 2048], BF16), sba("wb1", [128, 2048], BF16)]
    memb = [sba("memb0", [128, 512], BF16), sba("memb1", [128, 512], BF16)]
    qevW = sba("qevW", [128, 1024], F32)
    tmpW = sba("tmpW", [128, 1024], F32)
    S16 = [sba("S16a", [128, 128], BF16), sba("S16b", [128, 128], BF16)]
    S16z = sba("S16z", [128, 128], BF16)
    dnrhs_all = [sba("dnrhs0", [128, 96], BF16), sba("dnrhs1", [128, 96], BF16)]
    zt16 = sba("zt16", [128, 16], F32)
    kvT = sba("kvT", [128, 256], BF16)
    msT = sba("msT", [128, 128], BF16)
    mixb = [sba("mixb0", [128, 128], BF16), sba("mixb1", [128, 128], BF16)]
    mixw = sba("mixw", [128, 512], F32)
    mixf128 = sba("mixf128", [128, 128], F32)
    dnc = sba("dnc", [128, 5], F32)
    dnr = sba("dnr", [128, 5], F32)
    facm = sba("facm", [128, 5], F32)
    wgb = [sba("wgbA", [128, 5], F32), sba("wgbB", [128, 5], F32)]
    ztmp = sba("ztmp", [128, 2], F32)

    wst_free = [("vE", C["vE"]), ("vE", C["vE"])]
    wst_ld = [("qW0", 0), ("qW1", 0)]

    def load_wst(slot, src_ap):
        qn = "qW%d" % slot
        def _(e, slot=slot, src_ap=src_ap, fr=wst_free[slot], qn=qn):
            if fr[1]:
                e.wait_ge(sem[fr[0]], fr[1])
            e.dma_start(wst[slot][:], src_ap).then_inc(sem[qn], 16)
        blk.sync(_)
        C[qn] += 16
        wst_ld[slot] = (qn, C[qn])

    am_chunks = [("W1", i) for i in range(FT)] + \
                [("W2", i) for i in range(64)]
    am_next = [0]

    def emit_am_chunk(slot):
        if am_next[0] >= len(am_chunks):
            return
        which, ci = am_chunks[am_next[0]]
        am_next[0] += 1
        if which == "W1":
            src = W1h[:, ci, :, :]
            acc = col("amax1")
            first = ci == 0
        else:
            dt, sub = divmod(ci, 4)
            src = W2h[:, dt, sub * 16:(sub + 1) * 16, :]
            acc = col("amax2")
            first = ci == 0
        load_wst(slot, src)
        ld = wst_ld[slot]

        def _(e, ld=ld, acc=acc, slot=slot, first=first):
            e.wait_ge(sem[ld[0]], ld[1])
            e.tensor_reduce(ztmp[:, 1:2], wst[slot][:], AX, Alu.max,
                            apply_absolute_value=True)
            e.drain()
            if first:
                e.tensor_copy(acc, ztmp[:, 1:2])
            else:
                e.tensor_tensor(acc, acc, ztmp[:, 1:2], Alu.max)
            e.drain().then_inc(sem["vE"], 1)
        blk.vector(_)
        C["vE"] += 1
        wst_free[slot] = ("vE", C["vE"])

    wb_cast = [0, 0]
    wb_free = [0, 0]
    memb_cast = [0, 0]
    memb_free = [0, 0]


    def _z(e):
        e.memzero(S16z[:])
        e.drain().then_inc(sem["vE"], 1)
    blk.vector(_z)
    C["vE"] += 1
    s16z0 = C["vE"]

    head_marks = {}
    p4x = [0]
    cur_dn_ev = [0]
    act_free = [0, 0]
    vec_free = [0, 0]

    def make_tasks(g):
        par = g % 2
        prev = head_marks.get(g - 1, {})
        marks = {"p1ev": 0, "eluadd": prev.get("qev_last", 0),
                 "exp": prev.get("exp", 0)}
        head_marks[g] = marks
        tasks = []

        def make_elu(wi, hf, par):
            def t_elu(wi=wi, hf=hf, par=par):
                vb = C["vE"]
                if wi == 0:
                    def _(e, ev=marks["p1ev"], hf=hf):
                        e.wait_ge(sem["vE"], ev)
                        e.reduce_sum(col("zt1" if hf == 0 else "zt2"),
                                     qevW[:], AX).then_inc(sem["vE"], 1)
                    blk.vector(_)
                    C["vE"] += 1
                    qs_red = C["vE"]
                else:
                    qs_red = 0
                cb = C["cE"]

                def _(e, ev=marks["p1ev"], ex=marks.get("exp", 0), cb=cb,
                      qs_red=qs_red):
                    e.wait_ge(sem["vE"], ev)
                    if ex:
                        e.wait_ge(sem["vE"], ex)
                    if qs_red:
                        e.wait_ge(sem["vE"], qs_red)
                    e.activation(tmpW[:], qevW[:], Act.Exp).then_inc(
                        sem["cE"], 1)
                    e.wait_ge(sem["cE"], cb + 1)
                    e.activation(qevW[:], qevW[:], Act.Relu).then_inc(
                        sem["cE"], 1)
                blk.scalar(_)
                C["cE"] += 2
                echk = C["cE"]
                vb2 = C["vE"]

                def _(e, echk=echk, wi=wi, hf=hf, par=par, vb2=vb2,
                      af=act_free[par]):
                    e.wait_ge(sem["cE"], echk)
                    if af and hf == 0:
                        e.wait_ge(sem["tM"], af)
                    dst = [qp, kp][wi]
                    e.scalar_tensor_tensor(
                        dst[par][:, hf * 1024:(hf + 1) * 1024], tmpW[:], 1.0,
                        qevW[:], Alu.min, Alu.add).then_inc(sem["vE"], 1)
                    if wi == 0 and hf == 1:
                        e.tensor_tensor(col("qsum"), col("zt1"), col("zt2"),
                                        Alu.add).then_inc(sem["vE"], 1)
                blk.vector(_)
                C["vE"] += 2 if (wi == 0 and hf == 1) else 1
                marks["exp"] = C["vE"] if not (wi == 0 and hf == 1) \
                    else C["vE"] - 1
                marks["eluadd"] = marks["exp"]
                if wi == 0 and hf == 1:
                    marks["qsum"] = C["vE"]
                if hf == 1:
                    marks["qp" if wi == 0 else "kp"] = marks["eluadd"]
                    if wi == 1:
                        marks["qev_last"] = marks["eluadd"]
            return t_elu

        for wi in range(3):
            wbslot = [0, 1, 0][wi]
            lslot = wi % 2

            def t_load(wi=wi, lslot=lslot, g=g):
                wten = [wq, wk, wv][wi]
                load_wst(lslot, wten[:, g, :, :])
            tasks.append(t_load)

            def t_cast(lslot=lslot, wbslot=wbslot):
                def _(e, ldw=wst_ld[lslot], fr=wb_free[wbslot]):
                    e.wait_ge(sem[ldw[0]], ldw[1])
                    if fr:
                        e.wait_ge(sem["tM"], fr)
                    e.tensor_copy(wb[wbslot][:], wst[lslot][:]).then_inc(
                        sem["vE"], 1)
                blk.vector(_)
                C["vE"] += 1
                wb_cast[wbslot] = C["vE"]
                wst_free[lslot] = ("vE", C["vE"])
            tasks.append(t_cast)

            for tb in range(4):
                def t_piece(wi=wi, wbslot=wbslot, tb=tb, par=par):
                    def _(e, wc=wb_cast[wbslot], pv=marks["p1ev"]):
                        e.wait_ge(sem["vE"], wc)
                        if pv:
                            e.wait_ge(sem["vE"], pv)
                        for dc in range(DC):
                            mm = e.matmul(
                                P[1][:], wb[wbslot][:, dc * 128:(dc + 1) * 128],
                                xfm[:, dc * 2048 + tb * 512:
                                    dc * 2048 + (tb + 1) * 512],
                                start=(dc == 0), stop=(dc == DC - 1),
                                skip_group_check=True)
                            if dc == DC - 1:
                                mm.then_inc(sem["tM"], 1)
                    blk.tensor(_)
                    C["tM"] += 1
                    pm = C["tM"]

                    def _(e, pm=pm, wi=wi, tb=tb, par=par,
                          qg=marks.get("eluadd", 0),
                          af=act_free[par]):
                        if tb % 2 == 0 and wi < 2 and qg:
                            e.wait_ge(sem["vE"], qg)
                        if tb == 0 and wi == 2 and af:
                            e.wait_ge(sem["tM"], af)
                        e.wait_ge(sem["tM"], pm)
                        dst = (vf[par][:, tb * 512:(tb + 1) * 512] if wi == 2
                               else qevW[:, (tb % 2) * 512:
                                          (tb % 2 + 1) * 512])
                        e.tensor_copy(dst, P[1][:]).then_inc(sem["vE"], 1)
                    blk.vector(_)
                    C["vE"] += 1
                    marks["p1ev"] = C["vE"]
                    if wi == 2 and tb == 3:
                        marks["vfc"] = C["vE"]
                tasks.append(t_piece)
                if wi < 2 and tb % 2 == 1:
                    tasks.append(make_elu(wi, tb // 2, par))


        def t_mem(g=g, par=par):
            def _(e, fr=wst_free[1]):
                if fr[1]:
                    e.wait_ge(sem[fr[0]], fr[1])
                e.dma_start(wst[1][:, 0:512], memh[g, :, :]).then_inc(
                    sem["qW1"], 16)
            blk.sync(_)
            C["qW1"] += 16
            wst_ld[1] = ("qW1", C["qW1"])

            def _(e, ldm=wst_ld[1], fr=memb_free[par]):
                e.wait_ge(sem[ldm[0]], ldm[1])
                if fr:
                    e.wait_ge(sem["tM"], fr)
                e.tensor_copy(memb[par][:], wst[1][:, 0:512]).then_inc(
                    sem["vE"], 1)
            blk.vector(_)
            C["vE"] += 1
            memb_cast[par] = C["vE"]
            marks["memb"] = C["vE"]
            wst_free[1] = ("vE", C["vE"])
        tasks.append(t_mem)

        def t_relmm(g=g):
            def _(e, qs=marks["qsum"], de=cur_dn_ev[0]):
                e.wait_ge(sem["vE"], qs)
                if de:
                    e.wait_ge(sem["vE"], de)
                e.matmul(P[4][0:1, 70:70 + M], col("qsum"),
                         g_mn[:, g * M:(g + 1) * M], start=True, stop=True,
                         skip_group_check=True).then_inc(sem["tM"], 1)
            blk.tensor(_)
            C["tM"] += 1
            marks["relmm"] = C["tM"]
        tasks.append(t_relmm)

        def t_relv1():
            vb = C["vE"]

            def _(e, rm=marks["relmm"], vb=vb):
                e.wait_ge(sem["tM"], rm)
                e.scalar_tensor_tensor(g_row[0:1, 0:M], P[4][0:1, 70:70 + M],
                                       SCALE / S, g_em30[0:1, :], Alu.mult,
                                       Alu.subtract).then_inc(sem["vE"], 1)
                e.wait_ge(sem["vE"], vb + 1)
                e.reduce_max(g_row[0:1, 8:9], g_row[0:1, 0:M],
                             AX).then_inc(sem["vE"], 1)
                e.wait_ge(sem["vE"], vb + 2)
                e.tensor_single_scalar(g_row[0:1, 9:10], g_row[0:1, 8:9],
                                       -1.0, Alu.mult).then_inc(sem["vE"], 1)
            blk.vector(_)
            C["vE"] += 3
            p4x[0] = max(p4x[0], C["vE"] - 2)
            marks["relv"] = C["vE"]
        tasks.append(t_relv1)

        def t_rele():
            def _(e, rv=marks["relv"]):
                e.wait_ge(sem["vE"], rv)
                e.activation(g_row[0:1, 4:4 + M], g_row[0:1, 0:M], Act.Exp,
                             bias=g_row[0:1, 9:10], scale=1.0).then_inc(
                    sem["cE"], 1)
            blk.scalar(_)
            C["cE"] += 1
            marks["rele"] = C["cE"]
        tasks.append(t_rele)

        def t_relv2(g=g):
            vb = C["vE"]

            def _(e, re_=marks["rele"], vb=vb, g=g):
                e.wait_ge(sem["cE"], re_)
                e.reduce_sum(g_row[0:1, 10:11], g_row[0:1, 4:4 + M],
                             AX).then_inc(sem["vE"], 1)
                e.wait_ge(sem["vE"], vb + 1)
                e.reciprocal(g_row[0:1, 11:12],
                             g_row[0:1, 10:11]).then_inc(sem["vE"], 1)
                e.wait_ge(sem["vE"], vb + 2)
                e.tensor_single_scalar(g_row[0:1, 12:12 + M],
                                       g_row[0:1, 4:4 + M],
                                       g_row[0:1, 11:12],
                                       Alu.mult).then_inc(sem["vE"], 1)
                e.wait_ge(sem["vE"], vb + 3)
                e.tensor_tensor(g_row2[0:1, 0:M], g_row[0:1, 12:12 + M],
                                g_emc[0:1, :], Alu.mult).then_inc(sem["vE"], 1)
                e.wait_ge(sem["vE"], vb + 4)
                e.tensor_single_scalar(g_row2[0:1, 4:4 + M], g_row2[0:1, 0:M],
                                       g_grow[0:1, g:g + 1],
                                       Alu.mult).then_inc(sem["vE"], 1)
                e.tensor_copy(g_row2[0:1, 8:9],
                              g_gbar[0:1, g:g + 1]).then_inc(sem["vE"], 1)
            blk.vector(_)
            C["vE"] += 6
            marks["wrow"] = C["vE"]
        tasks.append(t_relv2)

        def t_bcast():
            def _(e, wr=marks["wrow"], de=cur_dn_ev[0]):
                e.wait_ge(sem["vE"], wr)
                if de:
                    e.wait_ge(sem["vE"], de)
                e.matmul(P[4][:, 64:69], g_onesrow[:], g_row2[0:1, 4:9],
                         start=True, stop=True,
                         skip_group_check=True).then_inc(sem["tM"], 1)
            blk.tensor(_)
            C["tM"] += 1
            marks["bcmm"] = C["tM"]
        tasks.append(t_bcast)

        def t_wgb(par=par):
            def _(e, bm=marks["bcmm"], vfree=vec_free[par]):
                e.wait_ge(sem["tM"], bm)
                if vfree:
                    e.wait_ge(sem["vE"], vfree)
                e.tensor_copy(wgb[par][:], P[4][:, 64:69]).then_inc(
                    sem["vE"], 1)
            blk.vector(_)
            C["vE"] += 1
            p4x[0] = max(p4x[0], C["vE"])
            marks["wgb"] = C["vE"]
        tasks.append(t_wgb)

        def t_zscan(g=g, par=par):
            vb = C["vE"]

            def _(e, kpm=marks["kp"], vb=vb, par=par, g=g,
                  af=act_free[par]):
                e.wait_ge(sem["vE"], kpm)
                if af:
                    e.wait_ge(sem["tM"], af)
                e.tensor_reduce(zt16[:],
                                kp[par][:].rearrange("p (c j) -> p c j",
                                                     c=NCH),
                                AX, Alu.add).then_inc(sem["vE"], 1)
                e.tensor_copy(
                    bass.AP(dnrhs_all[par], 0, [[96, 128], [6, 16], [1, 4]]),
                    bass.AP(g_mnb, g * M, [[H * M, 128], [0, 16],
                                           [1, 4]])).then_inc(sem["vE"], 1)
                e.memzero(dnrhs_all[par][:, 4:6])
                e.drain().then_inc(sem["vE"], 1)
                e.wait_ge(sem["vE"], vb + 3)
                e.tensor_tensor_scan(
                    bass.AP(dnrhs_all[par], 10, [[96, 128], [6, 15]]),
                    zt16[:, 0:15], zt16[:, 0:15], 0.0, Alu.add,
                    Alu.bypass).then_inc(sem["vE"], 1)
            blk.vector(_)
            C["vE"] += 4
            marks["dn"] = C["vE"]
        tasks.append(t_zscan)

        return tasks

    kvT_ev = mx_ev = 0
    kv_ev = 0
    dn_ev = am_ev = sc_ev = 0
    mixb_rdy = [0, 0]

    for t in make_tasks(0):
        t()

    for h in range(H):
        par = h % 2
        m = head_marks[h]
        head_ready = max(m["qp"], m["kp"], m["memb"], m["wgb"],
                         m["dn"], m["vfc"], s16z0)
        head_ready_c = 0
        tasks = make_tasks(h + 1) if h + 1 < H else []
        ti = [0]

        def run_tasks(c):
            remaining = len(tasks) - ti[0]
            gaps = NCH - c
            need = -(-remaining // gaps) if gaps > 0 else remaining
            for _ in range(need):
                if ti[0] < len(tasks):
                    tasks[ti[0]]()
                    ti[0] += 1

        supd = ("vE", head_ready)

        for c in range(NCH):
            if c > 0:
                run_tasks(c)
            cs = slice(c * 128, (c + 1) * 128)
            vpre = C["vE"]
            msT_target = vpre + 1
            this_kvT = C["gE"] + 1

            def _(e, c=c, cs=cs, h=h, par=par, supd=supd, kvT_ev=kvT_ev,
                  sc_ev=sc_ev, dn_ev=dn_ev, am_ev=am_ev, kv_ev=kv_ev,
                  mx_ev=mx_ev, head_ready=head_ready, this_kvT=this_kvT,
                  head_ready_c=head_ready_c,
                  msT_target=msT_target, mbr=tuple(mixb_rdy), px=p4x[0]):
                e.wait_ge(sem["vE"], head_ready)
                if kvT_ev:
                    e.wait_ge(sem["gE"], kvT_ev)
                e.transpose(TA[:, 0:128], kp[par][:, cs], g_id16[:])
                e.transpose(TA[:, 128:256], vf[par][:, cs],
                            g_id16[:]).then_inc(sem["tM"], 1)
                if sc_ev:
                    e.wait_ge(sem["vE"], sc_ev)
                e.matmul(P[0][:, 0:128], kp[par][:, cs], qp[par][:, cs],
                         start=True, stop=True).then_inc(sem["tM"], 1)
                e.wait_ge(sem[supd[0]], supd[1])
                if am_ev:
                    e.wait_ge(sem["vE"], am_ev)
                e.matmul(P[3][:, 0:128], qp[par][:, cs],
                         (S16z if c == 0 else S16[c % 2])[:],
                         start=True, stop=False, skip_group_check=True)
                if dn_ev:
                    e.wait_ge(sem["vE"], dn_ev)
                if px:
                    e.wait_ge(sem["vE"], px)
                e.matmul(P[4][:, 0:5], qp[par][:, cs],
                         dnrhs_all[par][:, 6 * c:6 * c + 5], start=True,
                         stop=False, skip_group_check=True)
                if am_ev:
                    e.wait_ge(sem["vE"], am_ev)
                e.matmul(P[5][:], qp[par][:, cs], memb[par][:], start=True,
                         stop=True).then_inc(sem["tM"], 1)
                e.wait_ge(sem["vE"], msT_target)
                e.wait_ge(sem["gE"], this_kvT)
                e.matmul(P[3][:, 0:128], msT[:], kvT[:, 128:256],
                         start=False, stop=True, skip_group_check=True)
                e.matmul(P[4][:, 4:5], msT[:], g_ones16[:], start=False,
                         stop=True, skip_group_check=True).then_inc(
                    sem["tM"], 1)
                if kv_ev:
                    e.wait_ge(sem["gE"], kv_ev)
                e.matmul(P[2][:, 0:128], kvT[:, 0:128], kvT[:, 128:256],
                         start=(c == 0), stop=(c == NCH - 1),
                         skip_group_check=True).then_inc(sem["tM"], 1)
                if c > 0:
                    e.wait_ge(sem["vE"], mbr[(c - 1) % 2])
                    if mx_ev:
                        e.wait_ge(sem["gE"], mx_ev)
                    e.transpose(TB[:, 0:128], mixb[(c - 1) % 2][:],
                                g_id16[:]).then_inc(sem["tM"], 1)
            blk.tensor(_)
            tb = C["tM"]
            n_t = 5 + (1 if c > 0 else 0)
            C["tM"] += n_t
            t_kvT, t_sc, t_qgrp, t_intra, t_kv = (tb + 1, tb + 2, tb + 3,
                                                  tb + 4, tb + 5)
            t_mx = tb + 6 if c > 0 else 0

            spre = C["gE"]

            def _(e, c=c, h=h, t_kvT=t_kvT, t_kv=t_kv, t_mx=t_mx):
                e.wait_ge(sem["tM"], t_kvT)
                e.activation(kvT[:], TA[:, 0:256],
                             Act.Identity).then_inc(sem["gE"], 1)
                e.wait_ge(sem["tM"], t_kv)
                e.activation(S16[(c + 1) % 2][:], P[2][:, 0:128],
                             Act.Identity).then_inc(sem["gE"], 1)
                if t_mx:
                    e.wait_ge(sem["tM"], t_mx)
                    e.activation(
                        amix[:, h * 2048 + (c - 1) * 128:h * 2048 + c * 128],
                        TB[:, 0:128], Act.Identity).then_inc(sem["gE"], 1)
            blk.scalar(_)
            C["gE"] += 3 if c > 0 else 2
            kvT_ev = spre + 1
            kv_ev = spre + 2
            supd = ("gE", spre + 2)
            mx_ev = spre + 3 if c > 0 else 0

            def _(e, c=c, par=par, t_sc=t_sc, t_intra=t_intra, vpre=vpre):
                e.wait_ge(sem["tM"], t_sc)
                e.tensor_tensor(msT[:], P[0][:, 0:128], g_mask[:],
                                Alu.mult).then_inc(sem["vE"], 1)
                e.wait_ge(sem["tM"], t_intra)
                e.tensor_single_scalar(dnc[:], P[4][:, 0:5], EPS,
                                       Alu.max).then_inc(sem["vE"], 1)
                e.wait_ge(sem["vE"], vpre + 2)
                e.reciprocal(dnr[:], dnc[:]).then_inc(sem["vE"], 1)
                e.wait_ge(sem["vE"], vpre + 3)
                e.tensor_tensor(facm[:, 0:5], dnr[:, 0:5], wgb[par][:, 0:5],
                                Alu.mult).then_inc(sem["vE"], 1)
            blk.vector(_)
            C["vE"] += 4
            sc_ev = vpre + 1
            dn_ev = vpre + 2
            cur_dn_ev[0] = dn_ev
            fac_done = vpre + 4

            def _(e, c=c, fd=fac_done, vpre=vpre):
                e.wait_ge(sem["vE"], fd)
                e.tensor_tensor(
                    mixw[:], P[5][:, 0:512],
                    facm[:, 0:4].unsqueeze(-1).broadcast_to([128, 4, 128]),
                    Alu.mult).then_inc(sem["vE"], 1)
                e.wait_ge(sem["vE"], vpre + 5)
                e.tensor_reduce(mixf128[:],
                                mixw[:].rearrange("p (m j) -> p j m", m=4),
                                AX, Alu.add).then_inc(sem["vE"], 1)
                e.wait_ge(sem["vE"], vpre + 6)
                e.scalar_tensor_tensor(mixb[c % 2][:], P[3][:, 0:128],
                                       facm[:, 4:5], mixf128[:], Alu.mult,
                                       Alu.add).then_inc(sem["vE"], 1)
            blk.vector(_)
            C["vE"] += 3
            am_ev = C["vE"]
            mixb_rdy[c % 2] = C["vE"]

        def _(e, h=h, mr=mixb_rdy[(NCH - 1) % 2], mx_ev=mx_ev):
            e.wait_ge(sem["vE"], mr)
            if mx_ev:
                e.wait_ge(sem["gE"], mx_ev)
            e.transpose(TB[:, 0:128], mixb[(NCH - 1) % 2][:],
                        g_id16[:]).then_inc(sem["tM"], 1)
        blk.tensor(_)
        C["tM"] += 1
        tfin = C["tM"]
        memb_free[par] = C["tM"]
        act_free[par] = C["tM"]

        def _(e, h=h, tfin=tfin):
            e.wait_ge(sem["tM"], tfin)
            e.activation(amix[:, h * 2048 + (NCH - 1) * 128:(h + 1) * 2048],
                         TB[:, 0:128], Act.Identity).then_inc(sem["gE"], 1)
        blk.scalar(_)
        C["gE"] += 1
        mx_ev = C["gE"]
        vec_free[par] = C["vE"]
        run_tasks(NCH)


    def _(e):
        e.matmul(P[4][:, 32:34], g_onesrow[:], g_row2[0:1, 44:46], start=True,
                 stop=True, skip_group_check=True).then_inc(sem["tM"], 1)
    blk.tensor(_)
    C["tM"] += 1
    sbc0 = C["tM"]
    vbs = C["vE"]

    def _(e, sbc0=sbc0, vbs=vbs):
        e.wait_ge(sem["tM"], sbc0)
        e.tensor_copy(col("s1"), P[4][:, 32:33]).then_inc(sem["vE"], 1)
        e.tensor_copy(col("s2"), P[4][:, 33:34]).then_inc(sem["vE"], 1)
        e.wait_ge(sem["vE"], vbs + 2)
        e.reciprocal(col("invs1"), col("s1")).then_inc(sem["vE"], 1)
        e.reciprocal(col("invs2"), col("s2")).then_inc(sem["vE"], 1)
    blk.vector(_)
    C["vE"] += 4
    cols_done = C["vE"]

    barrier()
    es_A.close()
    es_xfm.close()

    es_B = contextlib.ExitStack()

    def sbb(name, shape, dt):
        return es_B.enter_context(nc.sbuf_tensor(name, shape, dt))

    wo_sb = sbb("wo_sb", [128, DC * 2048], BF16)
    wstB = [sbb("wstB0", [128, 2048], F32), sbb("wstB1", [128, 2048], F32)]
    ztmpB = sbb("ztmpB", [128, 2], F32)
    x1pre = sbb("x1pre", [128, 2048], F32)
    x1s = [sbb("x1s0", [128, 1024], F32)]
    x1s = [x1s[0], x1s[0]]
    xc2 = sbb("xc2", [128, 512], F32)
    rint_b = sbb("rint_b", [128, 512], I32)
    dbf_b = sbb("dbf_b", [128, 512], BF16)
    abf_b = sbb("abf_b", [128, 512], BF16)
    rb16_b = sbb("rb16_b", [128, 512], BF16)
    xqt = sbb("xqt", [128, 512], BF16)
    es_xq = contextlib.ExitStack()
    xq = es_xq.enter_context(nc.sbuf_tensor("xq", [128, DC * 2048], FP8, side="right"))

    wo_cast = 0
    for ec in range(DC):
        def _(e, ec=ec, wc=wo_cast):
            e.wait_ge(sem["vE"], wc)
            e.dma_start(g_x0st[:], wo[:, ec, :]).then_inc(sem["qX"], 16)
        blk.sync(_)
        C["qX"] += 16
        ld = C["qX"]

        def _(e, ec=ec, ld=ld):
            e.wait_ge(sem["qX"], ld)
            e.tensor_copy(wo_sb[:, ec * 2048:(ec + 1) * 2048],
                          g_x0st[:]).then_inc(sem["vE"], 1)
        blk.vector(_)
        C["vE"] += 1
        wo_cast = C["vE"]

    wB_free = [("vE", 0), ("vE", 0)]
    am_prev_v = [0]

    def emit_am_chunk_B(slot):
        if am_next[0] >= len(am_chunks):
            return
        which, ci = am_chunks[am_next[0]]
        am_next[0] += 1
        if which == "W1":
            src = W1h[:, ci, :, :]
            acc = col("amax1")
            first = ci == 0
        else:
            dt, sub = divmod(ci, 4)
            src = W2h[:, dt, sub * 16:(sub + 1) * 16, :]
            acc = col("amax2")
            first = ci == 0
        qn = "qW%d" % slot

        def _(e, src=src, slot=slot, fr=wB_free[slot], qn=qn):
            if fr[1]:
                e.wait_ge(sem[fr[0]], fr[1])
            e.dma_start(wstB[slot][:], src).then_inc(sem[qn], 16)
        blk.sync(_)
        C[qn] += 16
        ld = (qn, C[qn])
        vb = C["vE"]

        def _(e, ld=ld, acc=acc, slot=slot, first=first, vb=vb,
              pz=am_prev_v[0]):
            e.wait_ge(sem[ld[0]], ld[1])
            if pz:
                e.wait_ge(sem["vE"], pz)
            e.tensor_reduce(ztmpB[:, 1:2], wstB[slot][:], AX, Alu.max,
                            apply_absolute_value=True).then_inc(sem["vE"], 1)
            e.wait_ge(sem["vE"], vb + 1)
            if first:
                e.tensor_copy(acc, ztmpB[:, 1:2]).then_inc(sem["vE"], 1)
            else:
                e.tensor_tensor(acc, acc, ztmpB[:, 1:2],
                                Alu.max).then_inc(sem["vE"], 1)
        blk.vector(_)
        C["vE"] += 2
        wB_free[slot] = ("vE", C["vE"] - 1)
        am_prev_v[0] = C["vE"]

    pp_evac = {0: 0, 1: 0}
    x0_read = wo_cast
    bank_free = 0
    x1_store = [0, 0]
    last_xchain = 0
    last_sq = 0
    for st in range(ST):
        grp = 0
        for _am in range(8):
            emit_am_chunk_B(_am % 2)

        def _(e, st=st, xr=x0_read, sq=last_sq):
            e.wait_ge(sem["vE"], xr)
            if sq:
                e.wait_ge(sem["cE"], sq)
            e.dma_start(g_x0st[:], x0[st * 128:(st + 1) * 128, :]).then_inc(
                sem["qX"], 16)
        blk.sync(_)
        C["qX"] += 16
        ldx = C["qX"]

        def _(e, st=st, grp=grp, wc=wo_cast, bf=bank_free):
            e.wait_ge(sem["vE"], wc)
            if bf:
                e.wait_ge(sem["vE"], bf)
            for ec in range(DC):
                for db in range(4):
                    mm = e.matmul(
                        P[grp + db][:],
                        amix[:, ec * 2048 + st * 128:ec * 2048 + (st + 1) * 128],
                        wo_sb[:, ec * 2048 + db * 512:ec * 2048 + (db + 1) * 512],
                        start=(ec == 0), stop=(ec == DC - 1))
                    if ec == DC - 1:
                        mm.then_inc(sem["tM"], 1)
        blk.tensor(_)
        C["tM"] += 4
        omm = C["tM"]

        def _(e, grp=grp, omm=omm, ldx=ldx, ls=last_sq):
            e.wait_ge(sem["qX"], ldx)
            if ls:
                e.wait_ge(sem["cE"], ls)
            for db in range(4):
                e.wait_ge(sem["tM"], omm - 3 + db)
                e.tensor_tensor(x1pre[:, db * 512:(db + 1) * 512],
                                P[grp + db][:],
                                g_x0st[:, db * 512:(db + 1) * 512],
                                Alu.add)
            e.drain().then_inc(sem["vE"], 1)
            e.reduce_sum(col("zt0"), x1pre[:], AX)
            e.drain().then_inc(sem["vE"], 1)
        blk.vector(_)
        C["vE"] += 2
        x0_read = C["vE"] - 1
        resid = C["vE"]
        bank_free = C["vE"] - 1

        def _(e, resid=resid, lx=last_xchain):
            e.wait_ge(sem["vE"], resid)
            if lx:
                e.wait_ge(sem["vE"], lx)
            e.activation(g_x0st[:, 0:512], x1pre[:, 0:512], Act.Square,
                         accum_out=col("zt1"))
            e.activation(g_x0st[:, 512:1024], x1pre[:, 512:1024], Act.Square,
                         accum_out=col("zt2"))
            e.activation(g_x0st[:, 1024:1536], x1pre[:, 1024:1536], Act.Square,
                         accum_out=col("msq"))
            e.activation(g_x0st[:, 1536:2048], x1pre[:, 1536:2048], Act.Square,
                         accum_out=col("ssq"))
            e.drain().then_inc(sem["cE"], 1)
        blk.scalar(_)
        C["cE"] += 1
        sq2_done = C["cE"]

        vb = C["vE"]

        def _(e, sq2_done=sq2_done, vb=vb):
            e.wait_ge(sem["cE"], sq2_done)
            e.tensor_tensor(col("ssq"), col("ssq"), col("msq"),
                            Alu.add).then_inc(sem["vE"], 1)
            e.wait_ge(sem["vE"], vb + 1)
            e.tensor_tensor(col("ssq"), col("ssq"), col("zt1"),
                            Alu.add).then_inc(sem["vE"], 1)
            e.tensor_single_scalar(col("mean"), col("zt0"), INV_D,
                                   Alu.mult).then_inc(sem["vE"], 1)
            e.wait_ge(sem["vE"], vb + 3)
            e.tensor_tensor(col("ssq"), col("ssq"), col("zt2"),
                            Alu.add).then_inc(sem["vE"], 1)
            e.tensor_tensor(col("msq"), col("mean"), col("mean"),
                            Alu.mult).then_inc(sem["vE"], 1)
            e.wait_ge(sem["vE"], vb + 5)
            e.scalar_tensor_tensor(col("var"), col("ssq"), INV_D, col("msq"),
                                   Alu.mult, Alu.subtract).then_inc(
                sem["vE"], 1)
            e.wait_ge(sem["vE"], vb + 6)
            e.tensor_single_scalar(col("var"), col("var"), LNEPS,
                                   Alu.add).then_inc(sem["vE"], 1)
        blk.vector(_)
        C["vE"] += 7
        stat1 = C["vE"]

        cb = C["cE"]

        def _(e, stat1=stat1, cb=cb):
            e.wait_ge(sem["vE"], stat1)
            e.activation(col("tmp"), col("var"), Act.Ln).then_inc(
                sem["cE"], 1)
            e.wait_ge(sem["cE"], cb + 1)
            e.activation(col("rstd"), col("tmp"), Act.Exp, bias=0.0,
                         scale=-0.5).then_inc(sem["cE"], 1)
        blk.scalar(_)
        C["cE"] += 2
        rstd_done = C["cE"]

        vb = C["vE"]

        def _(e, rstd_done=rstd_done, vb=vb):
            e.wait_ge(sem["cE"], rstd_done)
            e.tensor_single_scalar(col("nmr"), col("mean"), -1.0,
                                   Alu.mult).then_inc(sem["vE"], 1)
            e.wait_ge(sem["vE"], vb + 1)
            e.tensor_tensor(col("nmr"), col("nmr"), col("rstd"),
                            Alu.mult).then_inc(sem["vE"], 1)
            e.tensor_tensor(col("scale2"), col("rstd"), col("invs1"),
                            Alu.mult).then_inc(sem["vE"], 1)
        blk.vector(_)
        C["vE"] += 3
        lncols = C["vE"]

        for half in range(2):
            def _(e, half=half, lncols=lncols, xs0=x1_store[0],
                  xs1=x1_store[1]):
                e.wait_ge(sem["vE"], lncols)
                if xs0:
                    e.wait_ge(sem["qSx0"], xs0)
                if xs1:
                    e.wait_ge(sem["qSx1"], xs1)
                e.activation(x1s[half][:],
                             x1pre[:, half * 1024:(half + 1) * 1024],
                             Act.Identity, bias=col("nmr"),
                             scale=col("rstd")).then_inc(sem["cE"], 1)
            blk.scalar(_)
            C["cE"] += 1
            x1a = C["cE"]

            def _(e, half=half, st=st, x1a=x1a):
                e.wait_ge(sem["cE"], x1a)
                e.dma_start(x1scr[st * 128:(st + 1) * 128,
                                  half * 1024:(half + 1) * 1024],
                            x1s[half][:]).then_inc(sem["qSx%d" % half], 16)
            blk.scalar(_)
            C["qSx%d" % half] += 16
            x1_store[half] = C["qSx%d" % half]
        last_sq = C["cE"]

        for q in range(4):
            qs = slice(q * 512, (q + 1) * 512)

            def _(e, qs=qs, lncols=lncols, lx=last_xchain):
                e.wait_ge(sem["vE"], lncols)
                e.tensor_single_scalar(xc2[:], x1pre[:, qs], col("mean"),
                                       Alu.subtract).then_inc(sem["vE"], 1)
            blk.vector(_)
            C["vE"] += 1
            xcd = C["vE"]

            def _(e, xcd=xcd):
                e.wait_ge(sem["vE"], xcd)
                e.activation(rint_b[:], xc2[:], Act.Identity, bias=0.0,
                             scale=col("scale2")).then_inc(sem["cE"], 1)
            blk.scalar(_)
            C["cE"] += 1
            rqd = C["cE"]

            def _(e, rqd=rqd):
                e.wait_ge(sem["cE"], rqd)
                e.scalar_tensor_tensor(dbf_b[:], xc2[:], col("scale2"),
                                       rint_b[:], Alu.mult, Alu.subtract)
                e.drain()
                e.scalar_tensor_tensor(abf_b[:], dbf_b[:], -0.5, rint_b[:],
                                       Alu.is_lt, Alu.subtract)
                e.drain()
                e.scalar_tensor_tensor(rb16_b[:], dbf_b[:], 0.5, abf_b[:],
                                       Alu.is_gt, Alu.subtract)
                e.drain()
                e.tensor_scalar(xqt[:], rb16_b[:], 7.0, -8.0, Alu.min, Alu.max)
                e.drain().then_inc(sem["vE"], 1)
            blk.vector(_)
            C["vE"] += 1
            last_xchain = C["vE"]
            xqtd = C["vE"]

            vbase = C["vE"]

            def _(e, xqtd=xqtd, q=q, st=st, vbase=vbase):
                e.wait_ge(sem["vE"], xqtd)
                for k in range(4):
                    bank = k % 2
                    if k >= 2:
                        e.wait_ge(sem["vE"], vbase + k - 1)
                    elif pp_evac[bank]:
                        e.wait_ge(sem["vE"], pp_evac[bank])
                    e.transpose(TAB[bank][:, 0:128],
                                xqt[:, k * 128:(k + 1) * 128],
                                g_id16[:]).then_inc(sem["tM"], 1)
            blk.tensor(_)
            tb = C["tM"]
            C["tM"] += 4

            def _(e, tb=tb, q=q, st=st):
                for k in range(4):
                    bank = k % 2
                    dc = q * 4 + k
                    e.wait_ge(sem["tM"], tb + k + 1)
                    e.tensor_copy(
                        xq[:, dc * 2048 + st * 128:dc * 2048 + (st + 1) * 128],
                        TAB[bank][:, 0:128]).then_inc(sem["vE"], 1)
            blk.vector(_)
            for k in range(4):
                C["vE"] += 1
                pp_evac[k % 2] = C["vE"]

    amfinB = C["vE"]

    def _(e, amfinB=amfinB):
        e.wait_ge(sem["vE"], amfinB)
        e.transpose(P[4][0:1, 0:128], col("amax1"), g_idf[:])
        e.transpose(P[4][0:1, 128:256], col("amax2"),
                    g_idf[:]).then_inc(sem["tM"], 1)
    blk.tensor(_)
    C["tM"] += 1
    trm = C["tM"]

    def _(e, trm=trm):
        e.wait_ge(sem["tM"], trm)
        e.reduce_max(g_row2[0:1, 46:47], P[4][0:1, 0:128], AX)
        e.reduce_max(g_row2[0:1, 47:48], P[4][0:1, 128:256], AX)
        e.drain().then_inc(sem["vE"], 1)
    blk.vector(_)
    C["vE"] += 1
    amred = C["vE"]

    def _(e, amred=amred):
        e.wait_ge(sem["vE"], amred)
        e.matmul(P[4][:, 34:36], g_onesrow[:], g_row2[0:1, 46:48], start=True,
                 stop=True, skip_group_check=True).then_inc(sem["tM"], 1)
    blk.tensor(_)
    C["tM"] += 1
    sbc1 = C["tM"]

    def _(e, sbc1=sbc1):
        e.wait_ge(sem["tM"], sbc1)
        e.tensor_single_scalar(col("sw1"), P[4][:, 34:35], 1.0 / 7.0, Alu.mult)
        e.tensor_single_scalar(col("sw2"), P[4][:, 35:36], 1.0 / 7.0, Alu.mult)
        e.drain()
        e.reciprocal(col("inv1"), col("sw1"))
        e.reciprocal(col("inv2"), col("sw2"))
        e.tensor_tensor(col("c1"), col("s1"), col("sw1"), Alu.mult)
        e.tensor_tensor(col("c2"), col("s2"), col("sw2"), Alu.mult)
        e.drain()
        e.reciprocal(col("invc1"), col("c1"))
        e.tensor_tensor(col("k"), col("c1"), col("invs2"), Alu.mult)
        e.reciprocal(col("invc2"), col("c2"))
        e.drain().then_inc(sem["vE"], 1)
    blk.vector(_)
    C["vE"] += 1
    cols_done = C["vE"]

    def _(e, cols_done=cols_done):
        e.wait_ge(sem["vE"], cols_done)
        e.activation(rint_b[:, 0:FT], g_b1[:], Act.Identity, bias=0.0,
                     scale=col("invc1")).then_inc(sem["cE"], 1)
    blk.scalar(_)
    C["cE"] += 1
    m1 = C["cE"]

    def _(e, m1=m1):
        e.wait_ge(sem["cE"], m1)
        e.scalar_tensor_tensor(dbf_b[:, 0:FT], g_b1[:], col("invc1"),
                               rint_b[:, 0:FT], Alu.mult, Alu.subtract)
        e.drain()
        e.scalar_tensor_tensor(abf_b[:, 0:FT], dbf_b[:, 0:FT], -0.5,
                               rint_b[:, 0:FT], Alu.is_lt, Alu.subtract)
        e.drain()
        e.scalar_tensor_tensor(rb16_b[:, 0:FT], dbf_b[:, 0:FT], 0.5,
                               abf_b[:, 0:FT], Alu.is_gt, Alu.subtract)
        e.drain()
        e.tensor_single_scalar(g_b1k[:], rb16_b[:, 0:FT], col("k"), Alu.mult)
        e.drain().then_inc(sem["vE"], 1)
    blk.vector(_)
    C["vE"] += 1
    b1k_done = C["vE"]

    def _(e, b1k_done=b1k_done):
        e.wait_ge(sem["vE"], b1k_done)
        e.activation(rint_b[:, 0:16], g_b2[:], Act.Identity, bias=0.0,
                     scale=col("invc2")).then_inc(sem["cE"], 1)
    blk.scalar(_)
    C["cE"] += 1
    m2 = C["cE"]

    def _(e, m2=m2):
        e.wait_ge(sem["cE"], m2)
        e.scalar_tensor_tensor(dbf_b[:, 0:16], g_b2[:], col("invc2"),
                               rint_b[:, 0:16], Alu.mult, Alu.subtract)
        e.drain()
        e.scalar_tensor_tensor(abf_b[:, 0:16], dbf_b[:, 0:16], -0.5,
                               rint_b[:, 0:16], Alu.is_lt, Alu.subtract)
        e.drain()
        e.scalar_tensor_tensor(rb16_b[:, 0:16], dbf_b[:, 0:16], 0.5,
                               abf_b[:, 0:16], Alu.is_gt, Alu.subtract)
        e.drain()
        e.tensor_single_scalar(g_b2q[:], rb16_b[:, 0:16], col("c2"), Alu.mult)
        e.drain().then_inc(sem["vE"], 1)
    blk.vector(_)
    C["vE"] += 1
    bias_prep = C["vE"]


    b_x1_stores = (C["qSx0"], C["qSx1"])
    barrier()
    es_B.close()
    es_am.close()

    es_C = contextlib.ExitStack()

    def sbc(name, shape, dt):
        return es_C.enter_context(nc.sbuf_tensor(name, shape, dt))

    wstc = [sbc("wstc0", [128, 2048], F32), sbc("wstc1", [128, 2048], F32)]
    hq = sbc("hq", [128, FT * 1024], FP8)
    ffn_tm = sbc("ffn_tm", [128, 8 * 2048], BF16)
    w1q = [sbc("w1q0", [128, 2048], FP8), sbc("w1q1", [128, 2048], FP8)]
    w2q = [sbc("w2q0", [128, 8192], FP8), sbc("w2q1", [128, 8192], FP8)]
    ubuf = sbc("ubuf", [128, 1024], F32)
    rint_h = sbc("rint_h", [128, 1024], I16)
    rintW = sbc("rintW", [128, 2048], I8)
    dbf_h = sbc("dbf_h", [128, 1024], BF16)
    abf_h = sbc("abf_h", [128, 1024], BF16)
    pbf_h = sbc("pbf_h", [128, 1024], BF16)
    o2b = sbc("o2b", [128, 1024], BF16)
    x2p = [wstc[0], wstc[1]]
    outh = [sbc("outh0", [128, 1024], F32), sbc("outh1", [128, 1024], F32)]

    wst_free = [("vE", C["vE"]), ("vE", C["vE"])]
    wst_ld = [("qW0", C["qW0"]), ("qW1", C["qW1"])]

    def load_wstc(slot, src_ap):
        qn = "qW%d" % slot
        def _(e, slot=slot, src_ap=src_ap, fr=wst_free[slot], qn=qn):
            if fr[1]:
                e.wait_ge(sem[fr[0]], fr[1])
            e.dma_start(wstc[slot][:], src_ap).then_inc(sem[qn], 16)
        blk.sync(_)
        C[qn] += 16
        wst_ld[slot] = (qn, C[qn])

    rintW_state = [0]

    def nonlocal_set_rintW(v):
        rintW_state[0] = v
    hq_end = 0
    hq_rint_rd = 0
    wq_wr = {}

    def w_cast_block(slot, src_w, dma_wait):
        def _(e, slot=slot, src_w=src_w, dma_wait=dma_wait, rr=rintW_state[0]):
            e.wait_ge(sem[dma_wait[0]], dma_wait[1])
            if rr:
                e.wait_ge(sem["vE"], rr)
            e.activation(rintW[:], wstc[slot][:], Act.Identity, bias=0.0,
                         scale=src_w).then_inc(sem["cE"], 1)
        blk.scalar(_)
        C["cE"] += 1
        return C["cE"]

    def w_chain_block(slot, dstbuf, dst_slot, invcol, cast_mark, extra=(),
                      dst_off=0):
        vb = C["vE"]

        def _(e, slot=slot, dstbuf=dstbuf, dst_slot=dst_slot, invcol=invcol,
              cast_mark=cast_mark, vb=vb, he=hq_end, extra=tuple(extra),
              dst_off=dst_off):
            e.wait_ge(sem["cE"], cast_mark)
            if he:
                e.wait_ge(sem["vE"], he)
            for sn, sv in extra:
                e.wait_ge(sem[sn], sv)
            for part in range(2):
                ps = slice(part * 1024, (part + 1) * 1024)
                ds = slice(dst_off + part * 1024, dst_off + (part + 1) * 1024)
                if part:
                    e.wait_ge(sem["vE"], vb + 3)
                e.scalar_tensor_tensor(dbf_h[:], wstc[slot][:, ps], invcol,
                                       rintW[:, ps], Alu.mult,
                                       Alu.subtract).then_inc(sem["vE"], 1)
                e.wait_ge(sem["vE"], vb + part * 3 + 1)
                e.scalar_tensor_tensor(abf_h[:], dbf_h[:], -0.5, rintW[:, ps],
                                       Alu.is_lt,
                                       Alu.subtract).then_inc(sem["vE"], 1)
                e.wait_ge(sem["vE"], vb + part * 3 + 2)
                e.scalar_tensor_tensor(dstbuf[dst_slot][:, ds], dbf_h[:], 0.5,
                                       abf_h[:], Alu.is_gt,
                                       Alu.subtract).then_inc(sem["vE"], 1)
        blk.vector(_)
        C["vE"] += 6
        return C["vE"]

    w1q_free = [0, 0]
    w1q_store = [0, 0]
    w2q_free = [0, 0]
    w2q_store = [0, 0]
    hq_psum_free = 0
    qS_half0_end = []
    h2_end_tm = [0]
    x0_read_c = C["vE"]
    out_store = [0, 0]

    LN2 = {"x0r": x0_read_c, "lastsq": [0, 0], "ffn": 0}
    wstc_dead = [0]

    def emit_ln2_st(st):
        par2 = st % 2
        sfx = "" if par2 == 0 else "B"
        xp = x2p[par2]

        def cl(n):
            return col(n + sfx)

        def _(e, st=st, xr=LN2["x0r"], bx=b_x1_stores):
            e.wait_ge(sem["qSx0"], bx[0])
            e.wait_ge(sem["qSx1"], bx[1])
            e.wait_ge(sem["vE"], xr)
            e.dma_start(g_x0st[:],
                        x1scr[st * 128:(st + 1) * 128, :]).then_inc(
                sem["qX"], 16)
        blk.sync(_)
        C["qX"] += 16
        ldx1 = C["qX"]
        stl8 = st % 8

        def _(e, stl8=stl8, ldx1=ldx1, ls=LN2["lastsq"][par2],
              wd=(wstc_dead[0] if st < 2 else 0), xp=xp):
            e.wait_ge(sem["qX"], ldx1)
            if ls:
                e.wait_ge(sem["cE"], ls)
            if wd:
                e.wait_ge(sem["vE"], wd)
            e.tensor_tensor(xp[:], g_x0st[:],
                            ffn_tm[:, stl8 * 2048:(stl8 + 1) * 2048],
                            Alu.add)
            e.drain().then_inc(sem["vE"], 1)
            e.reduce_sum(cl("zt0"), xp[:], AX)
            e.drain().then_inc(sem["vE"], 1)
        blk.vector(_)
        C["vE"] += 2
        LN2["x0r"] = C["vE"] - 1
        LN2["ffn"] = C["vE"] - 1
        x2d = C["vE"]

        def _(e, x2d=x2d, os0=out_store[0], os1=out_store[1], xp=xp):
            e.wait_ge(sem["vE"], x2d)
            if os0:
                e.wait_ge(sem["qSo0"], os0)
            if os1:
                e.wait_ge(sem["qSo1"], os1)
            e.activation(outh[0][:], xp[:, 0:1024], Act.Square,
                         accum_out=cl("zt1"))
            e.activation(outh[1][:], xp[:, 1024:2048], Act.Square,
                         accum_out=cl("zt2"))
            e.drain().then_inc(sem["cE"], 1)
        blk.scalar(_)
        C["cE"] += 1
        sq2 = C["cE"]
        outh_sq = C["cE"]

        vb2 = C["vE"]

        def _(e, sq2=sq2, vb2=vb2):
            e.wait_ge(sem["cE"], sq2)
            e.tensor_tensor(cl("ssq"), cl("zt1"), cl("zt2"),
                            Alu.add).then_inc(sem["vE"], 1)
            e.tensor_single_scalar(cl("mean"), cl("zt0"), INV_D,
                                   Alu.mult).then_inc(sem["vE"], 1)
            e.wait_ge(sem["vE"], vb2 + 2)
            e.tensor_tensor(cl("msq"), cl("mean"), cl("mean"),
                            Alu.mult).then_inc(sem["vE"], 1)
            e.wait_ge(sem["vE"], vb2 + 3)
            e.scalar_tensor_tensor(cl("var"), cl("ssq"), INV_D,
                                   cl("msq"), Alu.mult,
                                   Alu.subtract).then_inc(sem["vE"], 1)
            e.wait_ge(sem["vE"], vb2 + 4)
            e.tensor_single_scalar(cl("var"), cl("var"), LNEPS,
                                   Alu.add).then_inc(sem["vE"], 1)
        blk.vector(_)
        C["vE"] += 5
        stat2 = C["vE"]

        cb2 = C["cE"]

        def _(e, stat2=stat2, cb2=cb2):
            e.wait_ge(sem["vE"], stat2)
            e.activation(cl("tmp"), cl("var"), Act.Ln).then_inc(
                sem["cE"], 1)
            e.wait_ge(sem["cE"], cb2 + 1)
            e.activation(cl("rstd"), cl("tmp"), Act.Exp, bias=0.0,
                         scale=-0.5).then_inc(sem["cE"], 1)
        blk.scalar(_)
        C["cE"] += 2
        r2d = C["cE"]

        vb3 = C["vE"]

        def _(e, r2d=r2d, vb3=vb3):
            e.wait_ge(sem["cE"], r2d)
            e.tensor_single_scalar(cl("nmr"), cl("mean"), -1.0,
                                   Alu.mult).then_inc(sem["vE"], 1)
            e.wait_ge(sem["vE"], vb3 + 1)
            e.tensor_tensor(cl("nmr"), cl("nmr"), cl("rstd"),
                            Alu.mult).then_inc(sem["vE"], 1)
        blk.vector(_)
        C["vE"] += 2
        nm2 = C["vE"]

        for half in range(2):
            def _(e, nm2=nm2, half=half, os=out_store[half], oq=outh_sq):
                e.wait_ge(sem["vE"], nm2)
                e.wait_ge(sem["cE"], oq)
                e.activation(outh[half][:],
                             xp[:, half * 1024:(half + 1) * 1024],
                             Act.Identity, bias=cl("nmr"),
                             scale=cl("rstd")).then_inc(sem["cE"], 1)
            blk.scalar(_)
            C["cE"] += 1
            oa = C["cE"]

            def _(e, st=st, half=half, oa=oa):
                e.wait_ge(sem["cE"], oa)
                e.dma_start(y[st * 128:(st + 1) * 128,
                              half * 1024:(half + 1) * 1024],
                            outh[half][:]).then_inc(sem["qSo%d" % half], 16)
            blk.scalar(_)
            C["qSo%d" % half] += 16
            out_store[half] = C["qSo%d" % half]
        LN2["lastsq"][par2] = C["cE"]

    for shalf in range(2):
        tok0 = shalf * 1024
        if shalf == 0:
            load_wstc(0, W1h[:, 0, :, :])
            cm = w_cast_block(0, col("inv1"), wst_ld[0])
            wr = w_chain_block(0, w1q, 0, col("inv1"), cm,
                               extra=([("tM", w1q_free[0])]
                                      if w1q_free[0] else []))
            wq_wr[("w1", 0)] = wr
            nonlocal_set_rintW(wr - 1)
            wst_free[0] = ("vE", wr)

            def _(e, wr=wr):
                e.wait_ge(sem["vE"], wr)
                e.dma_start(w1qscr[0, :, :], w1q[0][:]).then_inc(
                    sem["qSa0"], 16)
            blk.scalar(_)
            C["qSa0"] += 16
            w1q_store[0] = C["qSa0"]

        for ft in range(FT):
            slot = ft % 2
            nslot = (ft + 1) % 2
            if shalf == 0:
                if ft + 1 < FT:
                    load_wstc(nslot, W1h[:, ft + 1, :, :])
                    cm = w_cast_block(nslot, col("inv1"), wst_ld[nslot])
                    wr = w_chain_block(
                        nslot, w1q, nslot, col("inv1"), cm,
                        extra=([("tM", w1q_free[nslot]),
                                ("qSa%d" % nslot, w1q_store[nslot])]
                               if w1q_free[nslot] else []))
                    wq_wr[("w1", nslot)] = wr
                    nonlocal_set_rintW(wr - 1)
                    wst_free[nslot] = ("vE", wr)

                    def _(e, wr=wr, nslot=nslot, ft=ft):
                        e.wait_ge(sem["vE"], wr)
                        e.dma_start(w1qscr[ft + 1, :, :],
                                    w1q[nslot][:]).then_inc(
                            sem["qSa%d" % nslot], 16)
                    blk.scalar(_)
                    C["qSa%d" % nslot] += 16
                    w1q_store[nslot] = C["qSa%d" % nslot]
                w1q_rdy = ("vE", wq_wr[("w1", slot)])
            else:
                def _(e, slot=slot, ft=ft, fr=w1q_free[slot],
                      q0=qS_half0_end):
                    if ft < 2:
                        for qn, qv in q0:
                            e.wait_ge(sem[qn], qv)
                    if fr:
                        e.wait_ge(sem["tM"], fr)
                    e.dma_start(w1q[slot][:], w1qscr[ft, :, :]).then_inc(
                        sem["qF%d" % slot], 16)
                blk.sync(_)
                C["qF%d" % slot] += 16
                w1q_rdy = ("qF%d" % slot, C["qF%d" % slot])

            grp = (ft % 2) * 2

            def _(e, ft=ft, slot=slot, grp=grp, tok0=tok0, rdy=w1q_rdy,
                  pf=hq_psum_free):
                e.wait_ge(sem[rdy[0]], rdy[1])
                if pf:
                    e.wait_ge(sem["cE"], pf)
                for dc in range(DC):
                    for sb2 in range(2):
                        mm = e.matmul(
                            P[grp + sb2][:],
                            w1q[slot][:, dc * 128:(dc + 1) * 128],
                            xq[:, dc * 2048 + tok0 + sb2 * 512:
                               dc * 2048 + tok0 + (sb2 + 1) * 512],
                            start=(dc == 0), stop=(dc == DC - 1))
                        if dc == DC - 1:
                            mm.then_inc(sem["tM"], 1)
            blk.tensor(_)
            C["tM"] += 2
            h1mm = C["tM"]
            w1q_free[slot] = C["tM"]

            def _(e, ft=ft, grp=grp, h1mm=h1mm, he=hq_end, hr=hq_rint_rd):
                if he:
                    e.wait_ge(sem["vE"], he)
                e.wait_ge(sem["tM"], h1mm - 1)
                e.activation(ubuf[:, 0:512], P[grp][:], Act.Relu,
                             bias=g_b1k[:, ft:ft + 1], scale=col("k"))
                e.wait_ge(sem["tM"], h1mm)
                e.activation(ubuf[:, 512:1024], P[grp + 1][:], Act.Relu,
                             bias=g_b1k[:, ft:ft + 1], scale=col("k"))
                e.drain()
                e.activation(rint_h[:], ubuf[:], Act.Identity)
                e.drain().then_inc(sem["cE"], 1)
            blk.scalar(_)
            C["cE"] += 1
            hqu = C["cE"]
            hq_psum_free = C["cE"]

            vb = C["vE"]

            def _(e, hqu=hqu, ft=ft, vb=vb, he=hq_end,
                  h2e=(h2_end_tm[0] if (shalf == 1 and ft == 0) else 0)):
                e.wait_ge(sem["cE"], hqu)
                if h2e:
                    e.wait_ge(sem["tM"], h2e)
                if he:
                    e.wait_ge(sem["vE"], he)
                e.tensor_tensor(dbf_h[:], ubuf[:], rint_h[:],
                                Alu.subtract).then_inc(sem["vE"], 1)
                e.wait_ge(sem["vE"], vb + 1)
                e.scalar_tensor_tensor(o2b[:], dbf_h[:], -0.5, rint_h[:],
                                       Alu.is_lt,
                                       Alu.subtract).then_inc(sem["vE"], 1)
                e.tensor_single_scalar(pbf_h[:], dbf_h[:], 0.5,
                                       Alu.is_gt).then_inc(sem["vE"], 1)
                e.wait_ge(sem["vE"], vb + 3)
                e.scalar_tensor_tensor(abf_h[:], dbf_h[:], -1.5, o2b[:],
                                       Alu.is_lt, Alu.add).then_inc(
                    sem["vE"], 1)
                e.scalar_tensor_tensor(ubuf[:], dbf_h[:], 1.5,
                                       pbf_h[:], Alu.is_gt, Alu.add).then_inc(
                    sem["vE"], 1)
                e.wait_ge(sem["vE"], vb + 5)
                e.scalar_tensor_tensor(o2b[:], dbf_h[:], -2.5, abf_h[:],
                                       Alu.is_lt, Alu.add).then_inc(
                    sem["vE"], 1)
                e.scalar_tensor_tensor(pbf_h[:], dbf_h[:], 2.5, ubuf[:],
                                       Alu.is_gt, Alu.add).then_inc(
                    sem["vE"], 1)
                e.wait_ge(sem["vE"], vb + 7)
                e.tensor_tensor(abf_h[:], pbf_h[:], o2b[:],
                                Alu.subtract).then_inc(sem["vE"], 1)
                e.wait_ge(sem["vE"], vb + 8)
                e.tensor_scalar(hq[:, ft * 1024:(ft + 1) * 1024], abf_h[:],
                                15.0, 0.0, Alu.min, Alu.max).then_inc(
                    sem["vE"], 1)
            blk.vector(_)
            C["vE"] += 9
            hq_end = C["vE"]
            hq_rint_rd = vb + 2
            if shalf == 1 and ft % 8 == 7:
                emit_ln2_st(ft // 8)
        h1_end_v = C["vE"]

        def quant_w2_tile(nslot, dt2):
            global_rd = 0
            for sub in range(4):
                lslot = nslot
                load_wstc(lslot, W2h[:, dt2, sub * 16:(sub + 1) * 16, :])
                cm = w_cast_block(lslot, col("inv2"), wst_ld[lslot])
                wr = w_chain_block(
                    lslot, w2q, nslot, col("inv2"), cm,
                    extra=([("tM", w2q_free[nslot]),
                            ("qSb%d" % nslot, w2q_store[nslot])]
                           if (sub == 0 and w2q_free[nslot]) else []),
                    dst_off=sub * 2048)
                wst_free[lslot] = ("vE", wr)
                global_rd = wr
                nonlocal_set_rintW(wr - 1)
            wq_wr[("w2", nslot)] = global_rd

            def _(e, wr=global_rd, nslot=nslot, dt2=dt2):
                e.wait_ge(sem["vE"], wr)
                e.dma_start(w2qscr[dt2, :, :], w2q[nslot][:]).then_inc(
                    sem["qSb%d" % nslot], 16)
            blk.scalar(_)
            C["qSb%d" % nslot] += 16
            w2q_store[nslot] = C["qSb%d" % nslot]

        if shalf == 0:
            quant_w2_tile(0, 0)
        for dt in range(16):
            slot = dt % 2
            if shalf == 0:
                if dt + 1 < 16:
                    quant_w2_tile((dt + 1) % 2, dt + 1)
                w2q_rdy = ("vE", wq_wr[("w2", slot)])
            else:
                def _(e, slot=slot, dt=dt, fr=w2q_free[slot],
                      q0=qS_half0_end):
                    if dt < 2:
                        for qn, qv in q0:
                            e.wait_ge(sem[qn], qv)
                    if fr:
                        e.wait_ge(sem["tM"], fr)
                    e.dma_start(w2q[slot][:], w2qscr[dt, :, :]).then_inc(
                        sem["qG%d" % slot], 16)
                blk.sync(_)
                C["qG%d" % slot] += 16
                w2q_rdy = ("qG%d" % slot, C["qG%d" % slot])

            grp = (dt % 2) * 2

            def _(e, dt=dt, slot=slot, grp=grp, rdy=w2q_rdy, hv=h1_end_v):
                e.wait_ge(sem[rdy[0]], rdy[1])
                e.wait_ge(sem["vE"], hv)
                for fc in range(FT):
                    for sb2 in range(2):
                        mm = e.matmul(
                            P[grp + sb2][:],
                            w2q[slot][:, fc * 128:(fc + 1) * 128],
                            hq[:, fc * 1024 + sb2 * 512:
                               fc * 1024 + (sb2 + 1) * 512],
                            start=(fc == 0), stop=(fc == FT - 1))
                        if fc == FT - 1:
                            mm.then_inc(sem["tM"], 1)
            blk.tensor(_)
            C["tM"] += 2
            h2mm = C["tM"]
            w2q_free[slot] = C["tM"]

            def _(e, dt=dt, grp=grp, h2mm=h2mm):
                e.wait_ge(sem["tM"], h2mm - 1)
                e.activation(o2b[:, 0:512], P[grp][:], Act.Identity,
                             bias=g_b2q[:, dt:dt + 1], scale=col("c2"))
                e.wait_ge(sem["tM"], h2mm)
                e.activation(o2b[:, 512:1024], P[grp + 1][:], Act.Identity,
                             bias=g_b2q[:, dt:dt + 1], scale=col("c2"))
                e.drain().then_inc(sem["cE"], 1)
            blk.scalar(_)
            C["cE"] += 1
            o2d = C["cE"]

            vbase2 = C["vE"]

            def _(e, o2d=o2d, vbase2=vbase2):
                e.wait_ge(sem["cE"], o2d)
                for k in range(8):
                    bank = k % 2
                    if k >= 2:
                        e.wait_ge(sem["vE"], vbase2 + k - 1)
                    elif pp_evac[bank]:
                        e.wait_ge(sem["vE"], pp_evac[bank])
                    e.transpose(TAB[bank][:, 0:128],
                                o2b[:, k * 128:(k + 1) * 128],
                                g_id16[:]).then_inc(sem["tM"], 1)
            blk.tensor(_)
            tb = C["tM"]
            C["tM"] += 8

            def _(e, tb=tb, dt=dt, lf=(LN2["ffn"] if dt == 0 else 0)):
                if lf:
                    e.wait_ge(sem["vE"], lf)
                for k in range(8):
                    bank = k % 2
                    e.wait_ge(sem["tM"], tb + k + 1)
                    e.tensor_copy(
                        ffn_tm[:, k * 2048 + dt * 128:k * 2048 + (dt + 1) * 128],
                        TAB[bank][:, 0:128]).then_inc(sem["vE"], 1)
            blk.vector(_)
            for k in range(8):
                C["vE"] += 1
                pp_evac[k % 2] = C["vE"]

        if shalf == 1:
            for stl in range(8):
                emit_ln2_st(8 + stl)

        if shalf == 0:
            qS_half0_end = [("qSa0", C["qSa0"]), ("qSa1", C["qSa1"]),
                            ("qSb0", C["qSb0"]), ("qSb1", C["qSb1"])]
            h2_end_tm[0] = C["tM"]
            wstc_dead[0] = C["vE"]

    def _(e, t0=C["qSo0"], t1=C["qSo1"]):
        e.wait_ge(sem["qSo0"], t0)
        e.wait_ge(sem["qSo1"], t1)
    blk.scalar(_)

    blk_es.close()
    es_C.close()
    es_xq.close()
    es_p.close()
    es_g.close()
    es_s.close()
    return nc


def _host_layout(inputs):
    import ml_dtypes
    f32 = np.float32

    def c(a):
        return np.ascontiguousarray(a, dtype=f32)

    d = dict(
        wq=c(inputs["L0_attn_w_q"].T.reshape(DC, 128, H, 128)
             .transpose(1, 2, 0, 3)),
        wk=c(inputs["L0_attn_w_k"].T.reshape(DC, 128, H, 128)
             .transpose(1, 2, 0, 3)),
        wv=c(inputs["L0_attn_w_v"].T.reshape(DC, 128, H, 128)
             .transpose(1, 2, 0, 3)),
        wo=c(inputs["L0_attn_w_o"].T.reshape(DC, 128, D).transpose(1, 0, 2)),
        W1h=c(inputs["L0_ffn_W1"].T.reshape(DC, 128, FT, 128)
              .transpose(1, 2, 0, 3)),
        W2h=c(inputs["L0_ffn_W2"].T.reshape(64, 128, 16, 128)
              .transpose(1, 2, 0, 3)),
        memh=c(inputs["L0_attn_memories"].transpose(1, 2, 0, 3)
               .reshape(H, 128, M * 128)),
        mnh=c(inputs["L0_attn_memory_norms"].transpose(2, 1, 0)
              .reshape(128, H * M)),
        gater=c(inputs["L0_attn_gate"].reshape(1, H)),
        b1h=c(np.asarray(inputs["L0_ffn_b1"]).reshape(FT, 128).T),
        b2h=c(np.asarray(inputs["L0_ffn_b2"]).reshape(16, 128).T),
        s1t=c(np.asarray(inputs["L0_ffn_s_act1"]).reshape(1, 1)),
        s2t=c(np.asarray(inputs["L0_ffn_s_act2"]).reshape(1, 1)),
        maskT=np.triu(np.ones((128, 128), dtype=f32)),
        id16=np.eye(128, dtype=f32).astype(ml_dtypes.bfloat16),
        onescol=np.ones((128, 1), dtype=f32),
        ones16=np.ones((128, 1), dtype=f32).astype(ml_dtypes.bfloat16),
        onesrow=np.ones((1, 128), dtype=f32),
    )
    return d


def kernel(**inputs):
    shared = _host_layout(inputs)
    x = np.ascontiguousarray(inputs["L0_attn_hidden_states"], dtype=np.float32)
    nc = build_nc()
    in_maps = []
    for b in range(8):
        m = dict(shared)
        m["x0"] = x[b]
        in_maps.append(m)
    res = run_bass_kernel_spmd(nc, in_maps, core_ids=list(range(8)))
    out = np.stack([res.results[b]["y"] for b in range(8)], axis=0)
    return out.astype(np.float32, copy=False)


# version 18
# speedup vs baseline: 1.0161x; 1.0141x over previous
import contextlib

import numpy as np

import concourse.bass as bass
import concourse.mybir as mybir
from concourse.bass_utils import run_bass_kernel_spmd

F32 = mybir.dt.float32
BF16 = mybir.dt.bfloat16
FP8 = mybir.dt.float8e4
I32 = mybir.dt.int32
I8 = mybir.dt.int8
I16 = mybir.dt.int16
Alu = mybir.AluOpType
Act = mybir.ActivationFunctionType
AX = mybir.AxisListType.X

S = 2048
D = 2048
H = 16
HD = 128
NCH = 16
F = 8192
M = 4
DC = D // 128
FT = F // 128
ST = S // 128
SCALE = float(HD) ** -0.5
EPS = 1e-6
LNEPS = 1e-5
INV_D = 1.0 / D


def build_nc():
    nc = bass.Bass("TRN2", target_bir_lowering=False)

    x0 = nc.dram_tensor("x0", [S, D], F32, kind="ExternalInput")
    wq = nc.dram_tensor("wq", [128, H, DC, 128], F32, kind="ExternalInput")
    wk = nc.dram_tensor("wk", [128, H, DC, 128], F32, kind="ExternalInput")
    wv = nc.dram_tensor("wv", [128, H, DC, 128], F32, kind="ExternalInput")
    wo = nc.dram_tensor("wo", [128, DC, D], F32, kind="ExternalInput")
    W1h = nc.dram_tensor("W1h", [128, FT, DC, 128], F32, kind="ExternalInput")
    W2h = nc.dram_tensor("W2h", [128, 16, 64, 128], F32, kind="ExternalInput")
    memh = nc.dram_tensor("memh", [H, 128, M * 128], F32, kind="ExternalInput")
    mnh = nc.dram_tensor("mnh", [128, H * M], F32, kind="ExternalInput")
    gater = nc.dram_tensor("gater", [1, H], F32, kind="ExternalInput")
    b1h = nc.dram_tensor("b1h", [128, FT], F32, kind="ExternalInput")
    b2h = nc.dram_tensor("b2h", [128, 16], F32, kind="ExternalInput")
    s1t = nc.dram_tensor("s1t", [1, 1], F32, kind="ExternalInput")
    s2t = nc.dram_tensor("s2t", [1, 1], F32, kind="ExternalInput")
    maskT = nc.dram_tensor("maskT", [128, 128], F32, kind="ExternalInput")
    id16 = nc.dram_tensor("id16", [128, 128], BF16, kind="ExternalInput")
    onescol = nc.dram_tensor("onescol", [128, 1], F32, kind="ExternalInput")
    ones16 = nc.dram_tensor("ones16", [128, 1], BF16, kind="ExternalInput")
    onesrow = nc.dram_tensor("onesrow", [1, 128], F32, kind="ExternalInput")

    y = nc.dram_tensor("y", [S, D], F32, kind="ExternalOutput")
    x1scr = nc.dram_tensor("x1scr", [S, D], F32, kind="Internal")
    w1qscr = nc.dram_tensor("w1qscr", [FT, 128, DC * 128], FP8, kind="Internal")
    w2qscr = nc.dram_tensor("w2qscr", [16, 128, 64 * 128], FP8, kind="Internal")

    es_g = contextlib.ExitStack()

    def sbg(name, shape, dt):
        return es_g.enter_context(nc.sbuf_tensor(name, shape, dt))

    g_mask = sbg("g_mask", [128, 128], F32)
    g_id16 = sbg("g_id16", [128, 128], BF16)
    g_ones16 = sbg("g_ones16", [128, 1], BF16)
    g_onescol = sbg("g_onescol", [128, 1], F32)
    g_onesrow = sbg("g_onesrow", [1, 128], F32)
    g_mn = sbg("g_mn", [128, H * M], F32)
    g_mnb = sbg("g_mnb", [128, H * M], BF16)
    g_b1 = sbg("g_b1", [128, FT], F32)
    g_b2 = sbg("g_b2", [128, 16], F32)
    g_b1k = sbg("g_b1k", [128, FT], F32)
    g_b2q = sbg("g_b2q", [128, 16], F32)
    g_row = sbg("g_row", [1, 64], F32)
    g_row2 = sbg("g_row2", [1, 64], F32)
    g_grow = sbg("g_grow", [1, H], F32)
    g_gbar = sbg("g_gbar", [1, H], F32)
    g_em30 = sbg("g_em30", [1, M], F32)
    g_emc = sbg("g_emc", [1, M], F32)
    g_cols = sbg("g_cols", [128, 38], F32)
    g_x0st = sbg("g_x0st", [128, 2048], F32)
    g_idf = sbg("g_idf", [128, 128], F32)

    CN = {n: i for i, n in enumerate(
        ["amax1", "amax2", "sw1", "sw2", "inv1", "inv2", "s1", "s2",
         "invs1", "invs2", "c1", "invc1", "k", "c2", "invc2", "qsum", "zf",
         "mean", "ssq", "msq", "var", "rstd", "nmr", "scale2", "tmp",
         "zt0", "zt1", "zt2",
         "meanB", "ssqB", "msqB", "varB", "rstdB", "nmrB", "tmpB",
         "zt0B", "zt1B", "zt2B"])}

    def col(name):
        i = CN[name]
        return g_cols[:, i:i + 1]

    es_p = contextlib.ExitStack()
    P = [es_p.enter_context(nc.psum_tensor("pb%d" % i, [128, 512], F32))
         for i in range(6)]
    TA = es_p.enter_context(nc.psum_tensor("pta", [128, 1024], BF16))
    TB = es_p.enter_context(nc.psum_tensor("ptb", [128, 1024], BF16))
    TAB = [TA, TB]

    es_s = contextlib.ExitStack()
    sem = {}
    for name in ["qC", "qW0", "qW1", "qX", "qF0", "qF1", "qG0", "qG1",
                 "qSx0", "qSx1", "qSo0", "qSo1", "qSa0", "qSa1", "qSb0",
                 "qSb1", "tM", "vE", "cE", "gE", "bar"]:
        sem[name] = es_s.enter_context(nc.semaphore("sem_" + name))
    C = {k: 0 for k in sem}

    blk_es = contextlib.ExitStack()
    blk = blk_es.enter_context(nc.Block())

    DMA_SEMS = ["qC", "qW0", "qW1", "qX", "qF0", "qF1", "qG0", "qG1",
                "qSx0", "qSx1", "qSo0", "qSo1", "qSa0", "qSa1", "qSb0",
                "qSb1"]

    def barrier():
        waits = [(q, C[q]) for q in DMA_SEMS if C[q]]

        def _w(e, waits=tuple(waits)):
            for q, v in waits:
                e.wait_ge(sem[q], v)
        blk.sync(_w)
        C["bar"] += 5
        target = C["bar"]
        for en in ["sync", "scalar", "vector", "tensor", "gpsimd"]:
            def _(e, target=target):
                e.drain().then_inc(sem["bar"], 1)
                e.wait_ge(sem["bar"], target)
            getattr(blk, en)(_)

    es_am = contextlib.ExitStack()
    amix = es_am.enter_context(nc.sbuf_tensor("amix", [128, H * 2048], BF16))
    es_xfm = contextlib.ExitStack()
    xfm = es_xfm.enter_context(nc.sbuf_tensor("xfm", [128, DC * 2048], BF16))
    es_p0 = contextlib.ExitStack()
    x0b = [es_p0.enter_context(nc.sbuf_tensor("x0b%d" % i, [128, 2048], BF16))
           for i in range(2)]

    def _(e):
        for t, src in [(g_mask, maskT), (g_id16, id16), (g_ones16, ones16),
                       (g_onescol, onescol), (g_onesrow, onesrow),
                       (g_mn, mnh), (g_b1, b1h), (g_b2, b2h),
                       (g_grow, gater)]:
            e.dma_start(t[:], src[:]).then_inc(sem["qC"], 16)
        e.dma_start(g_row2[0:1, 44:45], s1t[:]).then_inc(sem["qC"], 16)
        e.dma_start(g_row2[0:1, 45:46], s2t[:]).then_inc(sem["qC"], 16)
    blk.sync(_)
    C["qC"] += 16 * 11
    consts_ld = C["qC"]

    def _(e):
        e.wait_ge(sem["qC"], consts_ld)
        e.activation(g_row2[0:1, 0:H], g_grow[0:1, :], Act.Exp, bias=0.0,
                     scale=-1.0).then_inc(sem["cE"], 1)
    blk.scalar(_)
    C["cE"] += 1
    gexp = C["cE"]

    def _(e, gexp=gexp):
        e.wait_ge(sem["cE"], gexp)
        e.tensor_single_scalar(g_row2[0:1, 16:16 + H], g_row2[0:1, 0:H], 1.0,
                               Alu.add)
        e.drain()
        e.reciprocal(g_grow[0:1, :], g_row2[0:1, 16:16 + H])
        e.drain()
        e.tensor_scalar(g_gbar[0:1, :], g_grow[0:1, :], -1.0, 1.0, Alu.mult,
                        Alu.add)
        e.tensor_copy(g_mnb[:], g_mn[:])
        e.tensor_copy(g_idf[:], g_id16[:])
        e.drain().then_inc(sem["vE"], 1)
    blk.vector(_)
    C["vE"] += 1

    def _(e):
        e.wait_ge(sem["qC"], consts_ld)
        e.matmul(P[5][0:1, 0:64], g_onescol[:], g_mn[:], start=True,
                 stop=True).then_inc(sem["tM"], 1)
    blk.tensor(_)
    C["tM"] += 1
    em_mm = C["tM"]

    def _(e, em_mm=em_mm):
        e.wait_ge(sem["tM"], em_mm)
        e.tensor_copy(g_row[0:1, 0:64], P[5][0:1, 0:64])
        e.drain()
        e.tensor_reduce(g_row2[0:1, 32:32 + M],
                        g_row[0:1, 0:64].rearrange("p (h m) -> p m h", m=M),
                        AX, Alu.add)
        e.drain()
        e.tensor_single_scalar(g_row2[0:1, 36:36 + M], g_row2[0:1, 32:32 + M],
                               1e-6, Alu.is_lt)
        e.drain()
        e.tensor_single_scalar(g_em30[0:1, :], g_row2[0:1, 36:36 + M], 1e30,
                               Alu.mult)
        e.tensor_scalar(g_emc[0:1, :], g_row2[0:1, 36:36 + M], -1.0, 1.0,
                        Alu.mult, Alu.add)
        e.drain().then_inc(sem["vE"], 1)
    blk.vector(_)
    C["vE"] += 1

    x0cast = [0] * ST
    for st in range(ST):
        def _(e, st=st):
            if st >= 1:
                e.wait_ge(sem["vE"], x0cast[st - 1])
            e.dma_start(g_x0st[:], x0[st * 128:(st + 1) * 128, :]).then_inc(
                sem["qX"], 16)
        blk.sync(_)
        C["qX"] += 16
        ld = C["qX"]

        def _(e, st=st, ld=ld):
            e.wait_ge(sem["qX"], ld)
            e.tensor_copy(x0b[st % 2][:], g_x0st[:]).then_inc(sem["vE"], 1)
        blk.vector(_)
        C["vE"] += 1
        x0cast[st] = C["vE"]

        vpre = C["vE"]

        def _(e, st=st, vpre=vpre):
            e.wait_ge(sem["vE"], x0cast[st])
            for dc in range(DC):
                if dc >= 2:
                    e.wait_ge(sem["vE"], vpre + dc - 1)
                e.transpose(TAB[dc % 2][:, 0:128],
                            x0b[st % 2][:, dc * 128:(dc + 1) * 128],
                            g_id16[:]).then_inc(sem["tM"], 1)
        blk.tensor(_)
        tb = C["tM"]
        C["tM"] += DC

        def _(e, st=st, tb=tb):
            for dc in range(DC):
                e.wait_ge(sem["tM"], tb + dc + 1)
                e.tensor_copy(
                    xfm[:, dc * 2048 + st * 128: dc * 2048 + (st + 1) * 128],
                    TAB[dc % 2][:, 0:128]).then_inc(sem["vE"], 1)
        blk.vector(_)
        C["vE"] += DC

    barrier()
    es_p0.close()

    es_A = contextlib.ExitStack()

    def sba(name, shape, dt):
        return es_A.enter_context(nc.sbuf_tensor(name, shape, dt))

    wst = [sba("wst0", [128, 2048], F32), sba("wst1", [128, 2048], F32)]
    qp = [sba("qp0", [128, 2048], BF16), sba("qp1", [128, 2048], BF16)]
    kp = [sba("kp0", [128, 2048], BF16), sba("kp1", [128, 2048], BF16)]
    vf = [sba("vf0", [128, 2048], BF16), sba("vf1", [128, 2048], BF16)]
    wb = [sba("wb0", [128, 2048], BF16), sba("wb1", [128, 2048], BF16)]
    memb = [sba("memb0", [128, 512], BF16), sba("memb1", [128, 512], BF16)]
    qevW = sba("qevW", [128, 1024], F32)
    tmpW = sba("tmpW", [128, 1024], F32)
    S16 = [sba("S16a", [128, 128], BF16), sba("S16b", [128, 128], BF16)]
    S16z = sba("S16z", [128, 128], BF16)
    dnrhs_all = [sba("dnrhs0", [128, 96], BF16), sba("dnrhs1", [128, 96], BF16)]
    zt16 = sba("zt16", [128, 16], F32)
    kvT = sba("kvT", [128, 256], BF16)
    msT = sba("msT", [128, 128], BF16)
    mixb = [sba("mixb0", [128, 128], BF16), sba("mixb1", [128, 128], BF16)]
    mixw = sba("mixw", [128, 512], F32)
    mixf128 = sba("mixf128", [128, 128], F32)
    dnc = sba("dnc", [128, 5], F32)
    dnr = sba("dnr", [128, 5], F32)
    facm = sba("facm", [128, 5], F32)
    wgb = [sba("wgbA", [128, 5], F32), sba("wgbB", [128, 5], F32)]
    ztmp = sba("ztmp", [128, 2], F32)

    wst_free = [("vE", C["vE"]), ("vE", C["vE"])]
    wst_ld = [("qW0", 0), ("qW1", 0)]

    def load_wst(slot, src_ap):
        qn = "qW%d" % slot
        def _(e, slot=slot, src_ap=src_ap, fr=wst_free[slot], qn=qn):
            if fr[1]:
                e.wait_ge(sem[fr[0]], fr[1])
            e.dma_start(wst[slot][:], src_ap).then_inc(sem[qn], 16)
        blk.sync(_)
        C[qn] += 16
        wst_ld[slot] = (qn, C[qn])

    am_chunks = [("W1", i) for i in range(FT)] + \
                [("W2", i) for i in range(64)]
    am_next = [0]

    def emit_am_chunk(slot):
        if am_next[0] >= len(am_chunks):
            return
        which, ci = am_chunks[am_next[0]]
        am_next[0] += 1
        if which == "W1":
            src = W1h[:, ci, :, :]
            acc = col("amax1")
            first = ci == 0
        else:
            dt, sub = divmod(ci, 4)
            src = W2h[:, dt, sub * 16:(sub + 1) * 16, :]
            acc = col("amax2")
            first = ci == 0
        load_wst(slot, src)
        ld = wst_ld[slot]

        def _(e, ld=ld, acc=acc, slot=slot, first=first):
            e.wait_ge(sem[ld[0]], ld[1])
            e.tensor_reduce(ztmp[:, 1:2], wst[slot][:], AX, Alu.max,
                            apply_absolute_value=True)
            e.drain()
            if first:
                e.tensor_copy(acc, ztmp[:, 1:2])
            else:
                e.tensor_tensor(acc, acc, ztmp[:, 1:2], Alu.max)
            e.drain().then_inc(sem["vE"], 1)
        blk.vector(_)
        C["vE"] += 1
        wst_free[slot] = ("vE", C["vE"])

    wb_cast = [0, 0]
    wb_free = [0, 0]
    memb_cast = [0, 0]
    memb_free = [0, 0]


    def _z(e):
        e.memzero(S16z[:])
        e.drain().then_inc(sem["vE"], 1)
    blk.vector(_z)
    C["vE"] += 1
    s16z0 = C["vE"]

    head_marks = {}
    p4x = [0]
    cur_dn_ev = [0]
    act_free = [0, 0]
    vec_free = [0, 0]

    def make_tasks(g):
        par = g % 2
        prev = head_marks.get(g - 1, {})
        marks = {"p1ev": 0, "eluadd": prev.get("qev_last", 0),
                 "exp": prev.get("exp", 0)}
        head_marks[g] = marks
        tasks = []

        def make_elu(wi, hf, par):
            def t_elu(wi=wi, hf=hf, par=par):
                vb = C["vE"]
                if wi == 0:
                    def _(e, ev=marks["p1ev"], hf=hf):
                        e.wait_ge(sem["vE"], ev)
                        e.reduce_sum(col("zt1" if hf == 0 else "zt2"),
                                     qevW[:], AX).then_inc(sem["vE"], 1)
                    blk.vector(_)
                    C["vE"] += 1
                    qs_red = C["vE"]
                else:
                    qs_red = 0
                cb = C["cE"]

                def _(e, ev=marks["p1ev"], ex=marks.get("exp", 0), cb=cb,
                      qs_red=qs_red):
                    e.wait_ge(sem["vE"], ev)
                    if ex:
                        e.wait_ge(sem["vE"], ex)
                    if qs_red:
                        e.wait_ge(sem["vE"], qs_red)
                    e.activation(tmpW[:], qevW[:], Act.Exp).then_inc(
                        sem["cE"], 1)
                    e.wait_ge(sem["cE"], cb + 1)
                    e.activation(qevW[:], qevW[:], Act.Relu).then_inc(
                        sem["cE"], 1)
                blk.scalar(_)
                C["cE"] += 2
                echk = C["cE"]
                vb2 = C["vE"]

                def _(e, echk=echk, wi=wi, hf=hf, par=par, vb2=vb2,
                      af=act_free[par]):
                    e.wait_ge(sem["cE"], echk)
                    if af and hf == 0:
                        e.wait_ge(sem["tM"], af)
                    dst = [qp, kp][wi]
                    e.scalar_tensor_tensor(
                        dst[par][:, hf * 1024:(hf + 1) * 1024], tmpW[:], 1.0,
                        qevW[:], Alu.min, Alu.add).then_inc(sem["vE"], 1)
                    if wi == 0 and hf == 1:
                        e.tensor_tensor(col("qsum"), col("zt1"), col("zt2"),
                                        Alu.add).then_inc(sem["vE"], 1)
                blk.vector(_)
                C["vE"] += 2 if (wi == 0 and hf == 1) else 1
                marks["exp"] = C["vE"] if not (wi == 0 and hf == 1) \
                    else C["vE"] - 1
                marks["eluadd"] = marks["exp"]
                if wi == 0 and hf == 1:
                    marks["qsum"] = C["vE"]
                if hf == 1:
                    marks["qp" if wi == 0 else "kp"] = marks["eluadd"]
                    if wi == 1:
                        marks["qev_last"] = marks["eluadd"]
            return t_elu

        for wi in range(3):
            wbslot = [0, 1, 0][wi]
            lslot = wi % 2

            def t_load(wi=wi, lslot=lslot, g=g):
                wten = [wq, wk, wv][wi]
                load_wst(lslot, wten[:, g, :, :])
            tasks.append(t_load)

            def t_cast(lslot=lslot, wbslot=wbslot):
                def _(e, ldw=wst_ld[lslot], fr=wb_free[wbslot]):
                    e.wait_ge(sem[ldw[0]], ldw[1])
                    if fr:
                        e.wait_ge(sem["tM"], fr)
                    e.tensor_copy(wb[wbslot][:], wst[lslot][:]).then_inc(
                        sem["vE"], 1)
                blk.vector(_)
                C["vE"] += 1
                wb_cast[wbslot] = C["vE"]
                wst_free[lslot] = ("vE", C["vE"])
            tasks.append(t_cast)

            for tb in range(4):
                def t_piece(wi=wi, wbslot=wbslot, tb=tb, par=par):
                    def _(e, wc=wb_cast[wbslot], pv=marks["p1ev"]):
                        e.wait_ge(sem["vE"], wc)
                        if pv:
                            e.wait_ge(sem["vE"], pv)
                        for dc in range(DC):
                            mm = e.matmul(
                                P[1][:], wb[wbslot][:, dc * 128:(dc + 1) * 128],
                                xfm[:, dc * 2048 + tb * 512:
                                    dc * 2048 + (tb + 1) * 512],
                                start=(dc == 0), stop=(dc == DC - 1),
                                skip_group_check=True)
                            if dc == DC - 1:
                                mm.then_inc(sem["tM"], 1)
                    blk.tensor(_)
                    C["tM"] += 1
                    pm = C["tM"]

                    def _(e, pm=pm, wi=wi, tb=tb, par=par,
                          qg=marks.get("eluadd", 0),
                          af=act_free[par]):
                        if tb % 2 == 0 and wi < 2 and qg:
                            e.wait_ge(sem["vE"], qg)
                        if tb == 0 and wi == 2 and af:
                            e.wait_ge(sem["tM"], af)
                        e.wait_ge(sem["tM"], pm)
                        dst = (vf[par][:, tb * 512:(tb + 1) * 512] if wi == 2
                               else qevW[:, (tb % 2) * 512:
                                          (tb % 2 + 1) * 512])
                        e.tensor_copy(dst, P[1][:]).then_inc(sem["vE"], 1)
                    blk.vector(_)
                    C["vE"] += 1
                    marks["p1ev"] = C["vE"]
                    if wi == 2 and tb == 3:
                        marks["vfc"] = C["vE"]
                tasks.append(t_piece)
                if wi < 2 and tb % 2 == 1:
                    tasks.append(make_elu(wi, tb // 2, par))


        def t_mem(g=g, par=par):
            def _(e, fr=wst_free[1]):
                if fr[1]:
                    e.wait_ge(sem[fr[0]], fr[1])
                e.dma_start(wst[1][:, 0:512], memh[g, :, :]).then_inc(
                    sem["qW1"], 16)
            blk.sync(_)
            C["qW1"] += 16
            wst_ld[1] = ("qW1", C["qW1"])

            def _(e, ldm=wst_ld[1], fr=memb_free[par]):
                e.wait_ge(sem[ldm[0]], ldm[1])
                if fr:
                    e.wait_ge(sem["tM"], fr)
                e.tensor_copy(memb[par][:], wst[1][:, 0:512]).then_inc(
                    sem["vE"], 1)
            blk.vector(_)
            C["vE"] += 1
            memb_cast[par] = C["vE"]
            marks["memb"] = C["vE"]
            wst_free[1] = ("vE", C["vE"])
        tasks.append(t_mem)

        def t_relmm(g=g):
            def _(e, qs=marks["qsum"], de=cur_dn_ev[0]):
                e.wait_ge(sem["vE"], qs)
                if de:
                    e.wait_ge(sem["vE"], de)
                e.matmul(P[4][0:1, 70:70 + M], col("qsum"),
                         g_mn[:, g * M:(g + 1) * M], start=True, stop=True,
                         skip_group_check=True).then_inc(sem["tM"], 1)
            blk.tensor(_)
            C["tM"] += 1
            marks["relmm"] = C["tM"]
        tasks.append(t_relmm)

        def t_relv1():
            vb = C["vE"]

            def _(e, rm=marks["relmm"], vb=vb):
                e.wait_ge(sem["tM"], rm)
                e.scalar_tensor_tensor(g_row[0:1, 0:M], P[4][0:1, 70:70 + M],
                                       SCALE / S, g_em30[0:1, :], Alu.mult,
                                       Alu.subtract).then_inc(sem["vE"], 1)
                e.wait_ge(sem["vE"], vb + 1)
                e.reduce_max(g_row[0:1, 8:9], g_row[0:1, 0:M],
                             AX).then_inc(sem["vE"], 1)
                e.wait_ge(sem["vE"], vb + 2)
                e.tensor_single_scalar(g_row[0:1, 9:10], g_row[0:1, 8:9],
                                       -1.0, Alu.mult).then_inc(sem["vE"], 1)
            blk.vector(_)
            C["vE"] += 3
            p4x[0] = max(p4x[0], C["vE"] - 2)
            marks["relv"] = C["vE"]
        tasks.append(t_relv1)

        def t_rele():
            def _(e, rv=marks["relv"]):
                e.wait_ge(sem["vE"], rv)
                e.activation(g_row[0:1, 4:4 + M], g_row[0:1, 0:M], Act.Exp,
                             bias=g_row[0:1, 9:10], scale=1.0).then_inc(
                    sem["cE"], 1)
            blk.scalar(_)
            C["cE"] += 1
            marks["rele"] = C["cE"]
        tasks.append(t_rele)

        def t_relv2(g=g):
            vb = C["vE"]

            def _(e, re_=marks["rele"], vb=vb, g=g):
                e.wait_ge(sem["cE"], re_)
                e.reduce_sum(g_row[0:1, 10:11], g_row[0:1, 4:4 + M],
                             AX).then_inc(sem["vE"], 1)
                e.wait_ge(sem["vE"], vb + 1)
                e.reciprocal(g_row[0:1, 11:12],
                             g_row[0:1, 10:11]).then_inc(sem["vE"], 1)
                e.wait_ge(sem["vE"], vb + 2)
                e.tensor_single_scalar(g_row[0:1, 12:12 + M],
                                       g_row[0:1, 4:4 + M],
                                       g_row[0:1, 11:12],
                                       Alu.mult).then_inc(sem["vE"], 1)
                e.wait_ge(sem["vE"], vb + 3)
                e.tensor_tensor(g_row2[0:1, 0:M], g_row[0:1, 12:12 + M],
                                g_emc[0:1, :], Alu.mult).then_inc(sem["vE"], 1)
                e.wait_ge(sem["vE"], vb + 4)
                e.tensor_single_scalar(g_row2[0:1, 4:4 + M], g_row2[0:1, 0:M],
                                       g_grow[0:1, g:g + 1],
                                       Alu.mult).then_inc(sem["vE"], 1)
                e.tensor_copy(g_row2[0:1, 8:9],
                              g_gbar[0:1, g:g + 1]).then_inc(sem["vE"], 1)
            blk.vector(_)
            C["vE"] += 6
            marks["wrow"] = C["vE"]
        tasks.append(t_relv2)

        def t_bcast():
            def _(e, wr=marks["wrow"], de=cur_dn_ev[0]):
                e.wait_ge(sem["vE"], wr)
                if de:
                    e.wait_ge(sem["vE"], de)
                e.matmul(P[4][:, 64:69], g_onesrow[:], g_row2[0:1, 4:9],
                         start=True, stop=True,
                         skip_group_check=True).then_inc(sem["tM"], 1)
            blk.tensor(_)
            C["tM"] += 1
            marks["bcmm"] = C["tM"]
        tasks.append(t_bcast)

        def t_wgb(par=par):
            def _(e, bm=marks["bcmm"], vfree=vec_free[par]):
                e.wait_ge(sem["tM"], bm)
                if vfree:
                    e.wait_ge(sem["vE"], vfree)
                e.tensor_copy(wgb[par][:], P[4][:, 64:69]).then_inc(
                    sem["vE"], 1)
            blk.vector(_)
            C["vE"] += 1
            p4x[0] = max(p4x[0], C["vE"])
            marks["wgb"] = C["vE"]
        tasks.append(t_wgb)

        def t_zscan(g=g, par=par):
            vb = C["vE"]

            def _(e, kpm=marks["kp"], vb=vb, par=par, g=g,
                  af=act_free[par]):
                e.wait_ge(sem["vE"], kpm)
                if af:
                    e.wait_ge(sem["tM"], af)
                e.tensor_reduce(zt16[:],
                                kp[par][:].rearrange("p (c j) -> p c j",
                                                     c=NCH),
                                AX, Alu.add).then_inc(sem["vE"], 1)
                e.tensor_copy(
                    bass.AP(dnrhs_all[par], 0, [[96, 128], [6, 16], [1, 4]]),
                    bass.AP(g_mnb, g * M, [[H * M, 128], [0, 16],
                                           [1, 4]])).then_inc(sem["vE"], 1)
                e.memzero(dnrhs_all[par][:, 4:6])
                e.drain().then_inc(sem["vE"], 1)
                e.wait_ge(sem["vE"], vb + 3)
                e.tensor_tensor_scan(
                    bass.AP(dnrhs_all[par], 10, [[96, 128], [6, 15]]),
                    zt16[:, 0:15], zt16[:, 0:15], 0.0, Alu.add,
                    Alu.bypass).then_inc(sem["vE"], 1)
            blk.vector(_)
            C["vE"] += 4
            marks["dn"] = C["vE"]
        tasks.append(t_zscan)

        return tasks

    kvT_ev = mx_ev = 0
    kv_ev = 0
    dn_ev = am_ev = sc_ev = 0
    mixb_rdy = [0, 0]

    for t in make_tasks(0):
        t()

    for h in range(H):
        par = h % 2
        m = head_marks[h]
        head_ready = max(m["qp"], m["kp"], m["memb"], m["wgb"],
                         m["dn"], m["vfc"], s16z0)
        head_ready_c = 0
        tasks = make_tasks(h + 1) if h + 1 < H else []
        ti = [0]

        def run_tasks(c):
            remaining = len(tasks) - ti[0]
            gaps = NCH - c
            need = -(-remaining // gaps) if gaps > 0 else remaining
            for _ in range(need):
                if ti[0] < len(tasks):
                    tasks[ti[0]]()
                    ti[0] += 1

        supd = ("vE", head_ready)

        for c in range(NCH):
            if c > 0:
                run_tasks(c)
            cs = slice(c * 128, (c + 1) * 128)
            vpre = C["vE"]
            msT_target = vpre + 1
            this_kvT = C["gE"] + 1

            def _(e, c=c, cs=cs, h=h, par=par, supd=supd, kvT_ev=kvT_ev,
                  sc_ev=sc_ev, dn_ev=dn_ev, am_ev=am_ev, kv_ev=kv_ev,
                  mx_ev=mx_ev, head_ready=head_ready, this_kvT=this_kvT,
                  head_ready_c=head_ready_c,
                  msT_target=msT_target, mbr=tuple(mixb_rdy), px=p4x[0]):
                e.wait_ge(sem["vE"], head_ready)
                if kvT_ev:
                    e.wait_ge(sem["gE"], kvT_ev)
                e.transpose(TA[:, 0:128], kp[par][:, cs], g_id16[:])
                e.transpose(TA[:, 128:256], vf[par][:, cs],
                            g_id16[:]).then_inc(sem["tM"], 1)
                if sc_ev:
                    e.wait_ge(sem["vE"], sc_ev)
                e.matmul(P[0][:, 0:128], kp[par][:, cs], qp[par][:, cs],
                         start=True, stop=True).then_inc(sem["tM"], 1)
                e.wait_ge(sem[supd[0]], supd[1])
                if am_ev:
                    e.wait_ge(sem["vE"], am_ev)
                e.matmul(P[3][:, 0:128], qp[par][:, cs],
                         (S16z if c == 0 else S16[c % 2])[:],
                         start=True, stop=False, skip_group_check=True)
                if dn_ev:
                    e.wait_ge(sem["vE"], dn_ev)
                if px:
                    e.wait_ge(sem["vE"], px)
                e.matmul(P[4][:, 0:5], qp[par][:, cs],
                         dnrhs_all[par][:, 6 * c:6 * c + 5], start=True,
                         stop=False, skip_group_check=True)
                if am_ev:
                    e.wait_ge(sem["vE"], am_ev)
                e.matmul(P[5][:], qp[par][:, cs], memb[par][:], start=True,
                         stop=True).then_inc(sem["tM"], 1)
                e.wait_ge(sem["vE"], msT_target)
                e.wait_ge(sem["gE"], this_kvT)
                e.matmul(P[3][:, 0:128], msT[:], kvT[:, 128:256],
                         start=False, stop=True, skip_group_check=True)
                e.matmul(P[4][:, 4:5], msT[:], g_ones16[:], start=False,
                         stop=True, skip_group_check=True).then_inc(
                    sem["tM"], 1)
                if kv_ev:
                    e.wait_ge(sem["gE"], kv_ev)
                e.matmul(P[2][:, 0:128], kvT[:, 0:128], kvT[:, 128:256],
                         start=(c == 0), stop=(c == NCH - 1),
                         skip_group_check=True).then_inc(sem["tM"], 1)
                if c > 0:
                    e.wait_ge(sem["vE"], mbr[(c - 1) % 2])
                    if mx_ev:
                        e.wait_ge(sem["gE"], mx_ev)
                    e.transpose(TB[:, 0:128], mixb[(c - 1) % 2][:],
                                g_id16[:]).then_inc(sem["tM"], 1)
            blk.tensor(_)
            tb = C["tM"]
            n_t = 5 + (1 if c > 0 else 0)
            C["tM"] += n_t
            t_kvT, t_sc, t_qgrp, t_intra, t_kv = (tb + 1, tb + 2, tb + 3,
                                                  tb + 4, tb + 5)
            t_mx = tb + 6 if c > 0 else 0

            spre = C["gE"]

            def _(e, c=c, h=h, t_kvT=t_kvT, t_kv=t_kv, t_mx=t_mx):
                e.wait_ge(sem["tM"], t_kvT)
                e.activation(kvT[:], TA[:, 0:256],
                             Act.Identity).then_inc(sem["gE"], 1)
                e.wait_ge(sem["tM"], t_kv)
                e.activation(S16[(c + 1) % 2][:], P[2][:, 0:128],
                             Act.Identity).then_inc(sem["gE"], 1)
                if t_mx:
                    e.wait_ge(sem["tM"], t_mx)
                    e.activation(
                        amix[:, h * 2048 + (c - 1) * 128:h * 2048 + c * 128],
                        TB[:, 0:128], Act.Identity).then_inc(sem["gE"], 1)
            blk.scalar(_)
            C["gE"] += 3 if c > 0 else 2
            kvT_ev = spre + 1
            kv_ev = spre + 2
            supd = ("gE", spre + 2)
            mx_ev = spre + 3 if c > 0 else 0

            def _(e, c=c, par=par, t_sc=t_sc, t_intra=t_intra, vpre=vpre):
                e.wait_ge(sem["tM"], t_sc)
                e.tensor_tensor(msT[:], P[0][:, 0:128], g_mask[:],
                                Alu.mult).then_inc(sem["vE"], 1)
                e.wait_ge(sem["tM"], t_intra)
                e.tensor_single_scalar(dnc[:], P[4][:, 0:5], EPS,
                                       Alu.max).then_inc(sem["vE"], 1)
                e.wait_ge(sem["vE"], vpre + 2)
                e.reciprocal(dnr[:], dnc[:]).then_inc(sem["vE"], 1)
                e.wait_ge(sem["vE"], vpre + 3)
                e.tensor_tensor(facm[:, 0:5], dnr[:, 0:5], wgb[par][:, 0:5],
                                Alu.mult).then_inc(sem["vE"], 1)
            blk.vector(_)
            C["vE"] += 4
            sc_ev = vpre + 1
            dn_ev = vpre + 2
            cur_dn_ev[0] = dn_ev
            fac_done = vpre + 4

            def _(e, c=c, fd=fac_done, vpre=vpre):
                e.wait_ge(sem["vE"], fd)
                e.tensor_tensor(
                    mixw[:], P[5][:, 0:512],
                    facm[:, 0:4].unsqueeze(-1).broadcast_to([128, 4, 128]),
                    Alu.mult).then_inc(sem["vE"], 1)
                e.wait_ge(sem["vE"], vpre + 5)
                e.tensor_reduce(mixf128[:],
                                mixw[:].rearrange("p (m j) -> p j m", m=4),
                                AX, Alu.add).then_inc(sem["vE"], 1)
                e.wait_ge(sem["vE"], vpre + 6)
                e.scalar_tensor_tensor(mixb[c % 2][:], P[3][:, 0:128],
                                       facm[:, 4:5], mixf128[:], Alu.mult,
                                       Alu.add).then_inc(sem["vE"], 1)
            blk.vector(_)
            C["vE"] += 3
            am_ev = C["vE"]
            mixb_rdy[c % 2] = C["vE"]

        def _(e, h=h, mr=mixb_rdy[(NCH - 1) % 2], mx_ev=mx_ev):
            e.wait_ge(sem["vE"], mr)
            if mx_ev:
                e.wait_ge(sem["gE"], mx_ev)
            e.transpose(TB[:, 0:128], mixb[(NCH - 1) % 2][:],
                        g_id16[:]).then_inc(sem["tM"], 1)
        blk.tensor(_)
        C["tM"] += 1
        tfin = C["tM"]
        memb_free[par] = C["tM"]
        act_free[par] = C["tM"]

        def _(e, h=h, tfin=tfin):
            e.wait_ge(sem["tM"], tfin)
            e.activation(amix[:, h * 2048 + (NCH - 1) * 128:(h + 1) * 2048],
                         TB[:, 0:128], Act.Identity).then_inc(sem["gE"], 1)
        blk.scalar(_)
        C["gE"] += 1
        mx_ev = C["gE"]
        vec_free[par] = C["vE"]
        run_tasks(NCH)


    def _(e):
        e.matmul(P[4][:, 32:34], g_onesrow[:], g_row2[0:1, 44:46], start=True,
                 stop=True, skip_group_check=True).then_inc(sem["tM"], 1)
    blk.tensor(_)
    C["tM"] += 1
    sbc0 = C["tM"]
    vbs = C["vE"]

    def _(e, sbc0=sbc0, vbs=vbs):
        e.wait_ge(sem["tM"], sbc0)
        e.tensor_copy(col("s1"), P[4][:, 32:33]).then_inc(sem["vE"], 1)
        e.tensor_copy(col("s2"), P[4][:, 33:34]).then_inc(sem["vE"], 1)
        e.wait_ge(sem["vE"], vbs + 2)
        e.reciprocal(col("invs1"), col("s1")).then_inc(sem["vE"], 1)
        e.reciprocal(col("invs2"), col("s2")).then_inc(sem["vE"], 1)
    blk.vector(_)
    C["vE"] += 4
    cols_done = C["vE"]

    barrier()
    es_A.close()
    es_xfm.close()

    es_B = contextlib.ExitStack()

    def sbb(name, shape, dt):
        return es_B.enter_context(nc.sbuf_tensor(name, shape, dt))

    wo_sb = sbb("wo_sb", [128, DC * 2048], BF16)
    wstB = [sbb("wstB0", [128, 2048], F32), sbb("wstB1", [128, 2048], F32)]
    ztmpB = sbb("ztmpB", [128, 2], F32)
    x1pre = sbb("x1pre", [128, 2048], F32)
    x1s = [sbb("x1s0", [128, 1024], F32)]
    x1s = [x1s[0], x1s[0]]
    xc2 = sbb("xc2", [128, 512], F32)
    rint_b = sbb("rint_b", [128, 512], I32)
    dbf_b = sbb("dbf_b", [128, 512], BF16)
    abf_b = sbb("abf_b", [128, 512], BF16)
    rb16_b = sbb("rb16_b", [128, 512], BF16)
    xqt = sbb("xqt", [128, 512], BF16)
    es_xq = contextlib.ExitStack()
    xq = es_xq.enter_context(nc.sbuf_tensor("xq", [128, DC * 2048], FP8, side="right"))

    wo_cast = 0
    for ec in range(DC):
        def _(e, ec=ec, wc=wo_cast):
            e.wait_ge(sem["vE"], wc)
            e.dma_start(g_x0st[:], wo[:, ec, :]).then_inc(sem["qX"], 16)
        blk.sync(_)
        C["qX"] += 16
        ld = C["qX"]

        def _(e, ec=ec, ld=ld):
            e.wait_ge(sem["qX"], ld)
            e.tensor_copy(wo_sb[:, ec * 2048:(ec + 1) * 2048],
                          g_x0st[:]).then_inc(sem["vE"], 1)
        blk.vector(_)
        C["vE"] += 1
        wo_cast = C["vE"]

    wB_free = [("vE", 0), ("vE", 0)]
    am_prev_v = [0]

    def emit_am_chunk_B(slot):
        if am_next[0] >= len(am_chunks):
            return
        which, ci = am_chunks[am_next[0]]
        am_next[0] += 1
        if which == "W1":
            src = W1h[:, ci, :, :]
            acc = col("amax1")
            first = ci == 0
        else:
            dt, sub = divmod(ci, 4)
            src = W2h[:, dt, sub * 16:(sub + 1) * 16, :]
            acc = col("amax2")
            first = ci == 0
        qn = "qW%d" % slot

        def _(e, src=src, slot=slot, fr=wB_free[slot], qn=qn):
            if fr[1]:
                e.wait_ge(sem[fr[0]], fr[1])
            e.dma_start(wstB[slot][:], src).then_inc(sem[qn], 16)
        blk.sync(_)
        C[qn] += 16
        ld = (qn, C[qn])
        vb = C["vE"]

        def _(e, ld=ld, acc=acc, slot=slot, first=first, vb=vb,
              pz=am_prev_v[0]):
            e.wait_ge(sem[ld[0]], ld[1])
            if pz:
                e.wait_ge(sem["vE"], pz)
            e.tensor_reduce(ztmpB[:, 1:2], wstB[slot][:], AX, Alu.max,
                            apply_absolute_value=True).then_inc(sem["vE"], 1)
            e.wait_ge(sem["vE"], vb + 1)
            if first:
                e.tensor_copy(acc, ztmpB[:, 1:2]).then_inc(sem["vE"], 1)
            else:
                e.tensor_tensor(acc, acc, ztmpB[:, 1:2],
                                Alu.max).then_inc(sem["vE"], 1)
        blk.vector(_)
        C["vE"] += 2
        wB_free[slot] = ("vE", C["vE"] - 1)
        am_prev_v[0] = C["vE"]

    pp_evac = {0: 0, 1: 0}
    x0_read = wo_cast
    bank_free = 0
    x1_store = [0, 0]
    last_xchain = 0
    last_sq = 0
    for st in range(ST):
        grp = 0
        for _am in range(8):
            emit_am_chunk_B(_am % 2)

        def _(e, st=st, xr=x0_read, sq=last_sq):
            e.wait_ge(sem["vE"], xr)
            if sq:
                e.wait_ge(sem["cE"], sq)
            e.dma_start(g_x0st[:], x0[st * 128:(st + 1) * 128, :]).then_inc(
                sem["qX"], 16)
        blk.sync(_)
        C["qX"] += 16
        ldx = C["qX"]

        def _(e, st=st, grp=grp, wc=wo_cast, bf=bank_free):
            e.wait_ge(sem["vE"], wc)
            if bf:
                e.wait_ge(sem["vE"], bf)
            for ec in range(DC):
                for db in range(4):
                    mm = e.matmul(
                        P[grp + db][:],
                        amix[:, ec * 2048 + st * 128:ec * 2048 + (st + 1) * 128],
                        wo_sb[:, ec * 2048 + db * 512:ec * 2048 + (db + 1) * 512],
                        start=(ec == 0), stop=(ec == DC - 1))
                    if ec == DC - 1:
                        mm.then_inc(sem["tM"], 1)
        blk.tensor(_)
        C["tM"] += 4
        omm = C["tM"]

        def _(e, grp=grp, omm=omm, ldx=ldx, ls=last_sq):
            e.wait_ge(sem["qX"], ldx)
            if ls:
                e.wait_ge(sem["cE"], ls)
            for db in range(4):
                e.wait_ge(sem["tM"], omm - 3 + db)
                e.tensor_tensor(x1pre[:, db * 512:(db + 1) * 512],
                                P[grp + db][:],
                                g_x0st[:, db * 512:(db + 1) * 512],
                                Alu.add)
            e.drain().then_inc(sem["vE"], 1)
            e.reduce_sum(col("zt0"), x1pre[:], AX)
            e.drain().then_inc(sem["vE"], 1)
        blk.vector(_)
        C["vE"] += 2
        x0_read = C["vE"] - 1
        resid = C["vE"]
        bank_free = C["vE"] - 1

        def _(e, resid=resid, lx=last_xchain):
            e.wait_ge(sem["vE"], resid)
            if lx:
                e.wait_ge(sem["vE"], lx)
            e.activation(g_x0st[:, 0:512], x1pre[:, 0:512], Act.Square,
                         accum_out=col("zt1"))
            e.activation(g_x0st[:, 512:1024], x1pre[:, 512:1024], Act.Square,
                         accum_out=col("zt2"))
            e.activation(g_x0st[:, 1024:1536], x1pre[:, 1024:1536], Act.Square,
                         accum_out=col("msq"))
            e.activation(g_x0st[:, 1536:2048], x1pre[:, 1536:2048], Act.Square,
                         accum_out=col("ssq"))
            e.drain().then_inc(sem["cE"], 1)
        blk.scalar(_)
        C["cE"] += 1
        sq2_done = C["cE"]

        vb = C["vE"]

        def _(e, sq2_done=sq2_done, vb=vb):
            e.wait_ge(sem["cE"], sq2_done)
            e.tensor_tensor(col("ssq"), col("ssq"), col("msq"),
                            Alu.add).then_inc(sem["vE"], 1)
            e.wait_ge(sem["vE"], vb + 1)
            e.tensor_tensor(col("ssq"), col("ssq"), col("zt1"),
                            Alu.add).then_inc(sem["vE"], 1)
            e.tensor_single_scalar(col("mean"), col("zt0"), INV_D,
                                   Alu.mult).then_inc(sem["vE"], 1)
            e.wait_ge(sem["vE"], vb + 3)
            e.tensor_tensor(col("ssq"), col("ssq"), col("zt2"),
                            Alu.add).then_inc(sem["vE"], 1)
            e.tensor_tensor(col("msq"), col("mean"), col("mean"),
                            Alu.mult).then_inc(sem["vE"], 1)
            e.wait_ge(sem["vE"], vb + 5)
            e.scalar_tensor_tensor(col("var"), col("ssq"), INV_D, col("msq"),
                                   Alu.mult, Alu.subtract).then_inc(
                sem["vE"], 1)
            e.wait_ge(sem["vE"], vb + 6)
            e.tensor_single_scalar(col("var"), col("var"), LNEPS,
                                   Alu.add).then_inc(sem["vE"], 1)
        blk.vector(_)
        C["vE"] += 7
        stat1 = C["vE"]

        cb = C["cE"]

        def _(e, stat1=stat1, cb=cb):
            e.wait_ge(sem["vE"], stat1)
            e.activation(col("tmp"), col("var"), Act.Ln).then_inc(
                sem["cE"], 1)
            e.wait_ge(sem["cE"], cb + 1)
            e.activation(col("rstd"), col("tmp"), Act.Exp, bias=0.0,
                         scale=-0.5).then_inc(sem["cE"], 1)
        blk.scalar(_)
        C["cE"] += 2
        rstd_done = C["cE"]

        vb = C["vE"]

        def _(e, rstd_done=rstd_done, vb=vb):
            e.wait_ge(sem["cE"], rstd_done)
            e.tensor_single_scalar(col("nmr"), col("mean"), -1.0,
                                   Alu.mult).then_inc(sem["vE"], 1)
            e.wait_ge(sem["vE"], vb + 1)
            e.tensor_tensor(col("nmr"), col("nmr"), col("rstd"),
                            Alu.mult).then_inc(sem["vE"], 1)
            e.tensor_tensor(col("scale2"), col("rstd"), col("invs1"),
                            Alu.mult).then_inc(sem["vE"], 1)
        blk.vector(_)
        C["vE"] += 3
        lncols = C["vE"]

        for half in range(2):
            def _(e, half=half, lncols=lncols, xs0=x1_store[0],
                  xs1=x1_store[1]):
                e.wait_ge(sem["vE"], lncols)
                if xs0:
                    e.wait_ge(sem["qSx0"], xs0)
                if xs1:
                    e.wait_ge(sem["qSx1"], xs1)
                e.activation(x1s[half][:],
                             x1pre[:, half * 1024:(half + 1) * 1024],
                             Act.Identity, bias=col("nmr"),
                             scale=col("rstd")).then_inc(sem["cE"], 1)
            blk.scalar(_)
            C["cE"] += 1
            x1a = C["cE"]

            def _(e, half=half, st=st, x1a=x1a):
                e.wait_ge(sem["cE"], x1a)
                e.dma_start(x1scr[st * 128:(st + 1) * 128,
                                  half * 1024:(half + 1) * 1024],
                            x1s[half][:]).then_inc(sem["qSx%d" % half], 16)
            blk.scalar(_)
            C["qSx%d" % half] += 16
            x1_store[half] = C["qSx%d" % half]
        last_sq = C["cE"]

        for q in range(4):
            qs = slice(q * 512, (q + 1) * 512)

            def _(e, qs=qs, lncols=lncols, lx=last_xchain):
                e.wait_ge(sem["vE"], lncols)
                e.tensor_single_scalar(xc2[:], x1pre[:, qs], col("mean"),
                                       Alu.subtract).then_inc(sem["vE"], 1)
            blk.vector(_)
            C["vE"] += 1
            xcd = C["vE"]

            def _(e, xcd=xcd):
                e.wait_ge(sem["vE"], xcd)
                e.activation(rint_b[:], xc2[:], Act.Identity, bias=0.0,
                             scale=col("scale2")).then_inc(sem["cE"], 1)
            blk.scalar(_)
            C["cE"] += 1
            rqd = C["cE"]

            def _(e, rqd=rqd):
                e.wait_ge(sem["cE"], rqd)
                e.scalar_tensor_tensor(dbf_b[:], xc2[:], col("scale2"),
                                       rint_b[:], Alu.mult, Alu.subtract)
                e.drain()
                e.scalar_tensor_tensor(abf_b[:], dbf_b[:], -0.5, rint_b[:],
                                       Alu.is_lt, Alu.subtract)
                e.drain()
                e.scalar_tensor_tensor(rb16_b[:], dbf_b[:], 0.5, abf_b[:],
                                       Alu.is_gt, Alu.subtract)
                e.drain()
                e.tensor_scalar(xqt[:], rb16_b[:], 7.0, -8.0, Alu.min, Alu.max)
                e.drain().then_inc(sem["vE"], 1)
            blk.vector(_)
            C["vE"] += 1
            last_xchain = C["vE"]
            xqtd = C["vE"]

            vbase = C["vE"]

            def _(e, xqtd=xqtd, q=q, st=st, vbase=vbase):
                e.wait_ge(sem["vE"], xqtd)
                for k in range(4):
                    bank = k % 2
                    if k >= 2:
                        e.wait_ge(sem["vE"], vbase + k - 1)
                    elif pp_evac[bank]:
                        e.wait_ge(sem["vE"], pp_evac[bank])
                    e.transpose(TAB[bank][:, 0:128],
                                xqt[:, k * 128:(k + 1) * 128],
                                g_id16[:]).then_inc(sem["tM"], 1)
            blk.tensor(_)
            tb = C["tM"]
            C["tM"] += 4

            def _(e, tb=tb, q=q, st=st):
                for k in range(4):
                    bank = k % 2
                    dc = q * 4 + k
                    e.wait_ge(sem["tM"], tb + k + 1)
                    e.tensor_copy(
                        xq[:, dc * 2048 + st * 128:dc * 2048 + (st + 1) * 128],
                        TAB[bank][:, 0:128]).then_inc(sem["vE"], 1)
            blk.vector(_)
            for k in range(4):
                C["vE"] += 1
                pp_evac[k % 2] = C["vE"]

    amfinB = C["vE"]

    def _(e, amfinB=amfinB):
        e.wait_ge(sem["vE"], amfinB)
        e.transpose(P[4][0:1, 0:128], col("amax1"), g_idf[:])
        e.transpose(P[4][0:1, 128:256], col("amax2"),
                    g_idf[:]).then_inc(sem["tM"], 1)
    blk.tensor(_)
    C["tM"] += 1
    trm = C["tM"]

    def _(e, trm=trm):
        e.wait_ge(sem["tM"], trm)
        e.reduce_max(g_row2[0:1, 46:47], P[4][0:1, 0:128], AX)
        e.reduce_max(g_row2[0:1, 47:48], P[4][0:1, 128:256], AX)
        e.drain().then_inc(sem["vE"], 1)
    blk.vector(_)
    C["vE"] += 1
    amred = C["vE"]

    def _(e, amred=amred):
        e.wait_ge(sem["vE"], amred)
        e.matmul(P[4][:, 34:36], g_onesrow[:], g_row2[0:1, 46:48], start=True,
                 stop=True, skip_group_check=True).then_inc(sem["tM"], 1)
    blk.tensor(_)
    C["tM"] += 1
    sbc1 = C["tM"]

    def _(e, sbc1=sbc1):
        e.wait_ge(sem["tM"], sbc1)
        e.tensor_single_scalar(col("sw1"), P[4][:, 34:35], 1.0 / 7.0, Alu.mult)
        e.tensor_single_scalar(col("sw2"), P[4][:, 35:36], 1.0 / 7.0, Alu.mult)
        e.drain()
        e.reciprocal(col("inv1"), col("sw1"))
        e.reciprocal(col("inv2"), col("sw2"))
        e.tensor_tensor(col("c1"), col("s1"), col("sw1"), Alu.mult)
        e.tensor_tensor(col("c2"), col("s2"), col("sw2"), Alu.mult)
        e.drain()
        e.reciprocal(col("invc1"), col("c1"))
        e.tensor_tensor(col("k"), col("c1"), col("invs2"), Alu.mult)
        e.reciprocal(col("invc2"), col("c2"))
        e.drain().then_inc(sem["vE"], 1)
    blk.vector(_)
    C["vE"] += 1
    cols_done = C["vE"]

    def _(e, cols_done=cols_done):
        e.wait_ge(sem["vE"], cols_done)
        e.activation(rint_b[:, 0:FT], g_b1[:], Act.Identity, bias=0.0,
                     scale=col("invc1")).then_inc(sem["cE"], 1)
    blk.scalar(_)
    C["cE"] += 1
    m1 = C["cE"]

    def _(e, m1=m1):
        e.wait_ge(sem["cE"], m1)
        e.scalar_tensor_tensor(dbf_b[:, 0:FT], g_b1[:], col("invc1"),
                               rint_b[:, 0:FT], Alu.mult, Alu.subtract)
        e.drain()
        e.scalar_tensor_tensor(abf_b[:, 0:FT], dbf_b[:, 0:FT], -0.5,
                               rint_b[:, 0:FT], Alu.is_lt, Alu.subtract)
        e.drain()
        e.scalar_tensor_tensor(rb16_b[:, 0:FT], dbf_b[:, 0:FT], 0.5,
                               abf_b[:, 0:FT], Alu.is_gt, Alu.subtract)
        e.drain()
        e.tensor_single_scalar(g_b1k[:], rb16_b[:, 0:FT], col("k"), Alu.mult)
        e.drain().then_inc(sem["vE"], 1)
    blk.vector(_)
    C["vE"] += 1
    b1k_done = C["vE"]

    def _(e, b1k_done=b1k_done):
        e.wait_ge(sem["vE"], b1k_done)
        e.activation(rint_b[:, 0:16], g_b2[:], Act.Identity, bias=0.0,
                     scale=col("invc2")).then_inc(sem["cE"], 1)
    blk.scalar(_)
    C["cE"] += 1
    m2 = C["cE"]

    def _(e, m2=m2):
        e.wait_ge(sem["cE"], m2)
        e.scalar_tensor_tensor(dbf_b[:, 0:16], g_b2[:], col("invc2"),
                               rint_b[:, 0:16], Alu.mult, Alu.subtract)
        e.drain()
        e.scalar_tensor_tensor(abf_b[:, 0:16], dbf_b[:, 0:16], -0.5,
                               rint_b[:, 0:16], Alu.is_lt, Alu.subtract)
        e.drain()
        e.scalar_tensor_tensor(rb16_b[:, 0:16], dbf_b[:, 0:16], 0.5,
                               abf_b[:, 0:16], Alu.is_gt, Alu.subtract)
        e.drain()
        e.tensor_single_scalar(g_b2q[:], rb16_b[:, 0:16], col("c2"), Alu.mult)
        e.drain().then_inc(sem["vE"], 1)
    blk.vector(_)
    C["vE"] += 1
    bias_prep = C["vE"]


    b_x1_stores = (C["qSx0"], C["qSx1"])
    barrier()
    es_B.close()
    es_am.close()

    es_C = contextlib.ExitStack()

    def sbc(name, shape, dt):
        return es_C.enter_context(nc.sbuf_tensor(name, shape, dt))

    wstc = [sbc("wstc0", [128, 2048], F32), sbc("wstc1", [128, 2048], F32)]
    hq = sbc("hq", [128, FT * 1024], FP8)
    ffn_tm = sbc("ffn_tm", [128, 8 * 2048], BF16)
    w1q = [sbc("w1q0", [128, 2048], FP8), sbc("w1q1", [128, 2048], FP8)]
    w2q = [sbc("w2q0", [128, 8192], FP8), sbc("w2q1", [128, 8192], FP8)]
    ubuf = sbc("ubuf", [128, 1024], F32)
    rint_h = sbc("rint_h", [128, 1024], I16)
    rintW = sbc("rintW", [128, 2048], I8)
    dbf_h = sbc("dbf_h", [128, 2048], BF16)
    abf_h = sbc("abf_h", [128, 2048], BF16)
    pbf_h = sbc("pbf_h", [128, 1024], BF16)
    o2b = sbc("o2b", [128, 1024], BF16)
    x2p = [wstc[0], wstc[1]]
    outh = [sbc("outh0", [128, 1024], F32), sbc("outh1", [128, 1024], F32)]

    wst_free = [("vE", C["vE"]), ("vE", C["vE"])]
    wst_ld = [("qW0", C["qW0"]), ("qW1", C["qW1"])]

    def load_wstc(slot, src_ap):
        qn = "qW%d" % slot
        def _(e, slot=slot, src_ap=src_ap, fr=wst_free[slot], qn=qn):
            if fr[1]:
                e.wait_ge(sem[fr[0]], fr[1])
            e.dma_start(wstc[slot][:], src_ap).then_inc(sem[qn], 16)
        blk.sync(_)
        C[qn] += 16
        wst_ld[slot] = (qn, C[qn])

    rintW_state = [0]

    def nonlocal_set_rintW(v):
        rintW_state[0] = v
    hq_end = 0
    hq_rint_rd = 0
    wq_wr = {}

    def w_cast_block(slot, src_w, dma_wait):
        def _(e, slot=slot, src_w=src_w, dma_wait=dma_wait, rr=rintW_state[0]):
            e.wait_ge(sem[dma_wait[0]], dma_wait[1])
            if rr:
                e.wait_ge(sem["vE"], rr)
            e.activation(rintW[:], wstc[slot][:], Act.Identity, bias=0.0,
                         scale=src_w).then_inc(sem["cE"], 1)
        blk.scalar(_)
        C["cE"] += 1
        return C["cE"]

    def w_chain_block(slot, dstbuf, dst_slot, invcol, cast_mark, extra=(),
                      dst_off=0):
        vb = C["vE"]

        def _(e, slot=slot, dstbuf=dstbuf, dst_slot=dst_slot, invcol=invcol,
              cast_mark=cast_mark, vb=vb, he=hq_end, extra=tuple(extra),
              dst_off=dst_off):
            e.wait_ge(sem["cE"], cast_mark)
            if he:
                e.wait_ge(sem["vE"], he)
            for sn, sv in extra:
                e.wait_ge(sem[sn], sv)
            ds = slice(dst_off, dst_off + 2048)
            e.scalar_tensor_tensor(dbf_h[:], wstc[slot][:], invcol,
                                   rintW[:], Alu.mult,
                                   Alu.subtract).then_inc(sem["vE"], 1)
            e.wait_ge(sem["vE"], vb + 1)
            e.scalar_tensor_tensor(abf_h[:], dbf_h[:], -0.5, rintW[:],
                                   Alu.is_lt,
                                   Alu.subtract).then_inc(sem["vE"], 1)
            e.wait_ge(sem["vE"], vb + 2)
            e.scalar_tensor_tensor(dstbuf[dst_slot][:, ds], dbf_h[:], 0.5,
                                   abf_h[:], Alu.is_gt,
                                   Alu.subtract).then_inc(sem["vE"], 1)
        blk.vector(_)
        C["vE"] += 3
        return C["vE"]

    w1q_free = [0, 0]
    w1q_store = [0, 0]
    w2q_free = [0, 0]
    w2q_store = [0, 0]
    hq_psum_free = 0
    qS_half0_end = []
    h2_end_tm = [0]
    x0_read_c = C["vE"]
    out_store = [0, 0]

    LN2 = {"x0r": x0_read_c, "lastsq": [0, 0], "ffn": 0}
    wstc_dead = [0]

    def emit_ln2_st(st):
        par2 = st % 2
        sfx = "" if par2 == 0 else "B"
        xp = x2p[par2]

        def cl(n):
            return col(n + sfx)

        def _(e, st=st, xr=LN2["x0r"], bx=b_x1_stores):
            e.wait_ge(sem["qSx0"], bx[0])
            e.wait_ge(sem["qSx1"], bx[1])
            e.wait_ge(sem["vE"], xr)
            e.dma_start(g_x0st[:],
                        x1scr[st * 128:(st + 1) * 128, :]).then_inc(
                sem["qX"], 16)
        blk.sync(_)
        C["qX"] += 16
        ldx1 = C["qX"]
        stl8 = st % 8

        def _(e, stl8=stl8, ldx1=ldx1, ls=LN2["lastsq"][par2],
              wd=(wstc_dead[0] if st < 2 else 0), xp=xp):
            e.wait_ge(sem["qX"], ldx1)
            if ls:
                e.wait_ge(sem["cE"], ls)
            if wd:
                e.wait_ge(sem["vE"], wd)
            e.tensor_tensor(xp[:], g_x0st[:],
                            ffn_tm[:, stl8 * 2048:(stl8 + 1) * 2048],
                            Alu.add)
            e.drain().then_inc(sem["vE"], 1)
            e.reduce_sum(cl("zt0"), xp[:], AX)
            e.drain().then_inc(sem["vE"], 1)
        blk.vector(_)
        C["vE"] += 2
        LN2["x0r"] = C["vE"] - 1
        LN2["ffn"] = C["vE"] - 1
        x2d = C["vE"]

        def _(e, x2d=x2d, os0=out_store[0], os1=out_store[1], xp=xp):
            e.wait_ge(sem["vE"], x2d)
            if os0:
                e.wait_ge(sem["qSo0"], os0)
            if os1:
                e.wait_ge(sem["qSo1"], os1)
            e.activation(outh[0][:], xp[:, 0:1024], Act.Square,
                         accum_out=cl("zt1"))
            e.activation(outh[1][:], xp[:, 1024:2048], Act.Square,
                         accum_out=cl("zt2"))
            e.drain().then_inc(sem["cE"], 1)
        blk.scalar(_)
        C["cE"] += 1
        sq2 = C["cE"]
        outh_sq = C["cE"]

        vb2 = C["vE"]

        def _(e, sq2=sq2, vb2=vb2):
            e.wait_ge(sem["cE"], sq2)
            e.tensor_tensor(cl("ssq"), cl("zt1"), cl("zt2"),
                            Alu.add).then_inc(sem["vE"], 1)
            e.tensor_single_scalar(cl("mean"), cl("zt0"), INV_D,
                                   Alu.mult).then_inc(sem["vE"], 1)
            e.wait_ge(sem["vE"], vb2 + 2)
            e.tensor_tensor(cl("msq"), cl("mean"), cl("mean"),
                            Alu.mult).then_inc(sem["vE"], 1)
            e.wait_ge(sem["vE"], vb2 + 3)
            e.scalar_tensor_tensor(cl("var"), cl("ssq"), INV_D,
                                   cl("msq"), Alu.mult,
                                   Alu.subtract).then_inc(sem["vE"], 1)
            e.wait_ge(sem["vE"], vb2 + 4)
            e.tensor_single_scalar(cl("var"), cl("var"), LNEPS,
                                   Alu.add).then_inc(sem["vE"], 1)
        blk.vector(_)
        C["vE"] += 5
        stat2 = C["vE"]

        cb2 = C["cE"]

        def _(e, stat2=stat2, cb2=cb2):
            e.wait_ge(sem["vE"], stat2)
            e.activation(cl("tmp"), cl("var"), Act.Ln).then_inc(
                sem["cE"], 1)
            e.wait_ge(sem["cE"], cb2 + 1)
            e.activation(cl("rstd"), cl("tmp"), Act.Exp, bias=0.0,
                         scale=-0.5).then_inc(sem["cE"], 1)
        blk.scalar(_)
        C["cE"] += 2
        r2d = C["cE"]

        vb3 = C["vE"]

        def _(e, r2d=r2d, vb3=vb3):
            e.wait_ge(sem["cE"], r2d)
            e.tensor_single_scalar(cl("nmr"), cl("mean"), -1.0,
                                   Alu.mult).then_inc(sem["vE"], 1)
            e.wait_ge(sem["vE"], vb3 + 1)
            e.tensor_tensor(cl("nmr"), cl("nmr"), cl("rstd"),
                            Alu.mult).then_inc(sem["vE"], 1)
        blk.vector(_)
        C["vE"] += 2
        nm2 = C["vE"]

        for half in range(2):
            def _(e, nm2=nm2, half=half, os=out_store[half], oq=outh_sq):
                e.wait_ge(sem["vE"], nm2)
                e.wait_ge(sem["cE"], oq)
                e.activation(outh[half][:],
                             xp[:, half * 1024:(half + 1) * 1024],
                             Act.Identity, bias=cl("nmr"),
                             scale=cl("rstd")).then_inc(sem["cE"], 1)
            blk.scalar(_)
            C["cE"] += 1
            oa = C["cE"]

            def _(e, st=st, half=half, oa=oa):
                e.wait_ge(sem["cE"], oa)
                e.dma_start(y[st * 128:(st + 1) * 128,
                              half * 1024:(half + 1) * 1024],
                            outh[half][:]).then_inc(sem["qSo%d" % half], 16)
            blk.scalar(_)
            C["qSo%d" % half] += 16
            out_store[half] = C["qSo%d" % half]
        LN2["lastsq"][par2] = C["cE"]

    for shalf in range(2):
        tok0 = shalf * 1024
        if shalf == 0:
            load_wstc(0, W1h[:, 0, :, :])
            cm = w_cast_block(0, col("inv1"), wst_ld[0])
            wr = w_chain_block(0, w1q, 0, col("inv1"), cm,
                               extra=([("tM", w1q_free[0])]
                                      if w1q_free[0] else []))
            wq_wr[("w1", 0)] = wr
            nonlocal_set_rintW(wr - 1)
            wst_free[0] = ("vE", wr)

            def _(e, wr=wr):
                e.wait_ge(sem["vE"], wr)
                e.dma_start(w1qscr[0, :, :], w1q[0][:]).then_inc(
                    sem["qSa0"], 16)
            blk.scalar(_)
            C["qSa0"] += 16
            w1q_store[0] = C["qSa0"]

        for ft in range(FT):
            slot = ft % 2
            nslot = (ft + 1) % 2
            if shalf == 0:
                if ft + 1 < FT:
                    load_wstc(nslot, W1h[:, ft + 1, :, :])
                    cm = w_cast_block(nslot, col("inv1"), wst_ld[nslot])
                    wr = w_chain_block(
                        nslot, w1q, nslot, col("inv1"), cm,
                        extra=([("tM", w1q_free[nslot]),
                                ("qSa%d" % nslot, w1q_store[nslot])]
                               if w1q_free[nslot] else []))
                    wq_wr[("w1", nslot)] = wr
                    nonlocal_set_rintW(wr - 1)
                    wst_free[nslot] = ("vE", wr)

                    def _(e, wr=wr, nslot=nslot, ft=ft):
                        e.wait_ge(sem["vE"], wr)
                        e.dma_start(w1qscr[ft + 1, :, :],
                                    w1q[nslot][:]).then_inc(
                            sem["qSa%d" % nslot], 16)
                    blk.scalar(_)
                    C["qSa%d" % nslot] += 16
                    w1q_store[nslot] = C["qSa%d" % nslot]
                w1q_rdy = ("vE", wq_wr[("w1", slot)])
            else:
                def _(e, slot=slot, ft=ft, fr=w1q_free[slot],
                      q0=qS_half0_end):
                    if ft < 2:
                        for qn, qv in q0:
                            e.wait_ge(sem[qn], qv)
                    if fr:
                        e.wait_ge(sem["tM"], fr)
                    e.dma_start(w1q[slot][:], w1qscr[ft, :, :]).then_inc(
                        sem["qF%d" % slot], 16)
                blk.sync(_)
                C["qF%d" % slot] += 16
                w1q_rdy = ("qF%d" % slot, C["qF%d" % slot])

            grp = (ft % 2) * 2

            def _(e, ft=ft, slot=slot, grp=grp, tok0=tok0, rdy=w1q_rdy,
                  pf=hq_psum_free):
                e.wait_ge(sem[rdy[0]], rdy[1])
                if pf:
                    e.wait_ge(sem["cE"], pf)
                for dc in range(DC):
                    for sb2 in range(2):
                        mm = e.matmul(
                            P[grp + sb2][:],
                            w1q[slot][:, dc * 128:(dc + 1) * 128],
                            xq[:, dc * 2048 + tok0 + sb2 * 512:
                               dc * 2048 + tok0 + (sb2 + 1) * 512],
                            start=(dc == 0), stop=(dc == DC - 1))
                        if dc == DC - 1:
                            mm.then_inc(sem["tM"], 1)
            blk.tensor(_)
            C["tM"] += 2
            h1mm = C["tM"]
            w1q_free[slot] = C["tM"]

            def _(e, ft=ft, grp=grp, h1mm=h1mm, he=hq_end, hr=hq_rint_rd):
                if he:
                    e.wait_ge(sem["vE"], he)
                e.wait_ge(sem["tM"], h1mm - 1)
                e.activation(ubuf[:, 0:512], P[grp][:], Act.Relu,
                             bias=g_b1k[:, ft:ft + 1], scale=col("k"))
                e.wait_ge(sem["tM"], h1mm)
                e.activation(ubuf[:, 512:1024], P[grp + 1][:], Act.Relu,
                             bias=g_b1k[:, ft:ft + 1], scale=col("k"))
                e.drain()
                e.activation(rint_h[:], ubuf[:], Act.Identity)
                e.drain().then_inc(sem["cE"], 1)
            blk.scalar(_)
            C["cE"] += 1
            hqu = C["cE"]
            hq_psum_free = C["cE"]

            vb = C["vE"]

            def _(e, hqu=hqu, ft=ft, vb=vb, he=hq_end,
                  h2e=(h2_end_tm[0] if (shalf == 1 and ft == 0) else 0)):
                e.wait_ge(sem["cE"], hqu)
                if h2e:
                    e.wait_ge(sem["tM"], h2e)
                if he:
                    e.wait_ge(sem["vE"], he)
                e.tensor_tensor(dbf_h[:, 0:1024], ubuf[:], rint_h[:],
                                Alu.subtract).then_inc(sem["vE"], 1)
                e.wait_ge(sem["vE"], vb + 1)
                e.scalar_tensor_tensor(o2b[:], dbf_h[:, 0:1024], -0.5, rint_h[:],
                                       Alu.is_lt,
                                       Alu.subtract).then_inc(sem["vE"], 1)
                e.tensor_single_scalar(pbf_h[:], dbf_h[:, 0:1024], 0.5,
                                       Alu.is_gt).then_inc(sem["vE"], 1)
                e.wait_ge(sem["vE"], vb + 3)
                e.scalar_tensor_tensor(abf_h[:, 0:1024], dbf_h[:, 0:1024], -1.5, o2b[:],
                                       Alu.is_lt, Alu.add).then_inc(
                    sem["vE"], 1)
                e.scalar_tensor_tensor(ubuf[:], dbf_h[:, 0:1024], 1.5,
                                       pbf_h[:], Alu.is_gt, Alu.add).then_inc(
                    sem["vE"], 1)
                e.wait_ge(sem["vE"], vb + 5)
                e.scalar_tensor_tensor(o2b[:], dbf_h[:, 0:1024], -2.5, abf_h[:, 0:1024],
                                       Alu.is_lt, Alu.add).then_inc(
                    sem["vE"], 1)
                e.scalar_tensor_tensor(pbf_h[:], dbf_h[:, 0:1024], 2.5, ubuf[:],
                                       Alu.is_gt, Alu.add).then_inc(
                    sem["vE"], 1)
                e.wait_ge(sem["vE"], vb + 7)
                e.tensor_tensor(abf_h[:, 0:1024], pbf_h[:], o2b[:],
                                Alu.subtract).then_inc(sem["vE"], 1)
                e.wait_ge(sem["vE"], vb + 8)
                e.tensor_scalar(hq[:, ft * 1024:(ft + 1) * 1024], abf_h[:, 0:1024],
                                15.0, 0.0, Alu.min, Alu.max).then_inc(
                    sem["vE"], 1)
            blk.vector(_)
            C["vE"] += 9
            hq_end = C["vE"]
            hq_rint_rd = vb + 2
            if shalf == 1 and ft % 8 == 7:
                emit_ln2_st(ft // 8)
        h1_end_v = C["vE"]

        def quant_w2_tile(nslot, dt2):
            global_rd = 0
            for sub in range(4):
                lslot = nslot
                load_wstc(lslot, W2h[:, dt2, sub * 16:(sub + 1) * 16, :])
                cm = w_cast_block(lslot, col("inv2"), wst_ld[lslot])
                wr = w_chain_block(
                    lslot, w2q, nslot, col("inv2"), cm,
                    extra=([("tM", w2q_free[nslot]),
                            ("qSb%d" % nslot, w2q_store[nslot])]
                           if (sub == 0 and w2q_free[nslot]) else []),
                    dst_off=sub * 2048)
                wst_free[lslot] = ("vE", wr)
                global_rd = wr
                nonlocal_set_rintW(wr - 1)
            wq_wr[("w2", nslot)] = global_rd

            def _(e, wr=global_rd, nslot=nslot, dt2=dt2):
                e.wait_ge(sem["vE"], wr)
                e.dma_start(w2qscr[dt2, :, :], w2q[nslot][:]).then_inc(
                    sem["qSb%d" % nslot], 16)
            blk.scalar(_)
            C["qSb%d" % nslot] += 16
            w2q_store[nslot] = C["qSb%d" % nslot]

        if shalf == 0:
            quant_w2_tile(0, 0)
        for dt in range(16):
            slot = dt % 2
            if shalf == 0:
                if dt + 1 < 16:
                    quant_w2_tile((dt + 1) % 2, dt + 1)
                w2q_rdy = ("vE", wq_wr[("w2", slot)])
            else:
                def _(e, slot=slot, dt=dt, fr=w2q_free[slot],
                      q0=qS_half0_end):
                    if dt < 2:
                        for qn, qv in q0:
                            e.wait_ge(sem[qn], qv)
                    if fr:
                        e.wait_ge(sem["tM"], fr)
                    e.dma_start(w2q[slot][:], w2qscr[dt, :, :]).then_inc(
                        sem["qG%d" % slot], 16)
                blk.sync(_)
                C["qG%d" % slot] += 16
                w2q_rdy = ("qG%d" % slot, C["qG%d" % slot])

            grp = (dt % 2) * 2

            def _(e, dt=dt, slot=slot, grp=grp, rdy=w2q_rdy, hv=h1_end_v):
                e.wait_ge(sem[rdy[0]], rdy[1])
                e.wait_ge(sem["vE"], hv)
                for fc in range(FT):
                    for sb2 in range(2):
                        mm = e.matmul(
                            P[grp + sb2][:],
                            w2q[slot][:, fc * 128:(fc + 1) * 128],
                            hq[:, fc * 1024 + sb2 * 512:
                               fc * 1024 + (sb2 + 1) * 512],
                            start=(fc == 0), stop=(fc == FT - 1))
                        if fc == FT - 1:
                            mm.then_inc(sem["tM"], 1)
            blk.tensor(_)
            C["tM"] += 2
            h2mm = C["tM"]
            w2q_free[slot] = C["tM"]

            def _(e, dt=dt, grp=grp, h2mm=h2mm):
                e.wait_ge(sem["tM"], h2mm - 1)
                e.activation(o2b[:, 0:512], P[grp][:], Act.Identity,
                             bias=g_b2q[:, dt:dt + 1], scale=col("c2"))
                e.wait_ge(sem["tM"], h2mm)
                e.activation(o2b[:, 512:1024], P[grp + 1][:], Act.Identity,
                             bias=g_b2q[:, dt:dt + 1], scale=col("c2"))
                e.drain().then_inc(sem["cE"], 1)
            blk.scalar(_)
            C["cE"] += 1
            o2d = C["cE"]

            vbase2 = C["vE"]

            def _(e, o2d=o2d, vbase2=vbase2):
                e.wait_ge(sem["cE"], o2d)
                for k in range(8):
                    bank = k % 2
                    if k >= 2:
                        e.wait_ge(sem["vE"], vbase2 + k - 1)
                    elif pp_evac[bank]:
                        e.wait_ge(sem["vE"], pp_evac[bank])
                    e.transpose(TAB[bank][:, 0:128],
                                o2b[:, k * 128:(k + 1) * 128],
                                g_id16[:]).then_inc(sem["tM"], 1)
            blk.tensor(_)
            tb = C["tM"]
            C["tM"] += 8

            def _(e, tb=tb, dt=dt, lf=(LN2["ffn"] if dt == 0 else 0)):
                if lf:
                    e.wait_ge(sem["vE"], lf)
                for k in range(8):
                    bank = k % 2
                    e.wait_ge(sem["tM"], tb + k + 1)
                    e.tensor_copy(
                        ffn_tm[:, k * 2048 + dt * 128:k * 2048 + (dt + 1) * 128],
                        TAB[bank][:, 0:128]).then_inc(sem["vE"], 1)
            blk.vector(_)
            for k in range(8):
                C["vE"] += 1
                pp_evac[k % 2] = C["vE"]

        if shalf == 1:
            for stl in range(8):
                emit_ln2_st(8 + stl)

        if shalf == 0:
            qS_half0_end = [("qSa0", C["qSa0"]), ("qSa1", C["qSa1"]),
                            ("qSb0", C["qSb0"]), ("qSb1", C["qSb1"])]
            h2_end_tm[0] = C["tM"]
            wstc_dead[0] = C["vE"]

    def _(e, t0=C["qSo0"], t1=C["qSo1"]):
        e.wait_ge(sem["qSo0"], t0)
        e.wait_ge(sem["qSo1"], t1)
    blk.scalar(_)

    blk_es.close()
    es_C.close()
    es_xq.close()
    es_p.close()
    es_g.close()
    es_s.close()
    return nc


def _host_layout(inputs):
    import ml_dtypes
    f32 = np.float32

    def c(a):
        return np.ascontiguousarray(a, dtype=f32)

    d = dict(
        wq=c(inputs["L0_attn_w_q"].T.reshape(DC, 128, H, 128)
             .transpose(1, 2, 0, 3)),
        wk=c(inputs["L0_attn_w_k"].T.reshape(DC, 128, H, 128)
             .transpose(1, 2, 0, 3)),
        wv=c(inputs["L0_attn_w_v"].T.reshape(DC, 128, H, 128)
             .transpose(1, 2, 0, 3)),
        wo=c(inputs["L0_attn_w_o"].T.reshape(DC, 128, D).transpose(1, 0, 2)),
        W1h=c(inputs["L0_ffn_W1"].T.reshape(DC, 128, FT, 128)
              .transpose(1, 2, 0, 3)),
        W2h=c(inputs["L0_ffn_W2"].T.reshape(64, 128, 16, 128)
              .transpose(1, 2, 0, 3)),
        memh=c(inputs["L0_attn_memories"].transpose(1, 2, 0, 3)
               .reshape(H, 128, M * 128)),
        mnh=c(inputs["L0_attn_memory_norms"].transpose(2, 1, 0)
              .reshape(128, H * M)),
        gater=c(inputs["L0_attn_gate"].reshape(1, H)),
        b1h=c(np.asarray(inputs["L0_ffn_b1"]).reshape(FT, 128).T),
        b2h=c(np.asarray(inputs["L0_ffn_b2"]).reshape(16, 128).T),
        s1t=c(np.asarray(inputs["L0_ffn_s_act1"]).reshape(1, 1)),
        s2t=c(np.asarray(inputs["L0_ffn_s_act2"]).reshape(1, 1)),
        maskT=np.triu(np.ones((128, 128), dtype=f32)),
        id16=np.eye(128, dtype=f32).astype(ml_dtypes.bfloat16),
        onescol=np.ones((128, 1), dtype=f32),
        ones16=np.ones((128, 1), dtype=f32).astype(ml_dtypes.bfloat16),
        onesrow=np.ones((1, 128), dtype=f32),
    )
    return d


def kernel(**inputs):
    shared = _host_layout(inputs)
    x = np.ascontiguousarray(inputs["L0_attn_hidden_states"], dtype=np.float32)
    nc = build_nc()
    in_maps = []
    for b in range(8):
        m = dict(shared)
        m["x0"] = x[b]
        in_maps.append(m)
    res = run_bass_kernel_spmd(nc, in_maps, core_ids=list(range(8)))
    out = np.stack([res.results[b]["y"] for b in range(8)], axis=0)
    return out.astype(np.float32, copy=False)
